# version 02
# baseline (speedup 1.0000x reference)
import numpy as np
import ml_dtypes
from contextlib import ExitStack

import concourse.bass as bass
import concourse.mybir as mybir
from concourse.bass_utils import run_bass_kernel_spmd

F32 = mybir.dt.float32
BF16 = mybir.dt.bfloat16
AF = mybir.ActivationFunctionType
ALU = mybir.AluOpType
AX = mybir.AxisListType

P = 128
NTOK = 2048
NT = NTOK // P
D = 4096
HT = D // P
I = 11008
IT = I // P
NHALF = I // 2
NQ = I // 4
KKEEP = float(I // 2)
EPS = 1e-5
BIS_ITERS = 14
HI0 = 8.0
SNAP = 1.0 + 2.0 ** -9
NB = 4
NBTOK = NTOK // NB
NGR = (IT + 7) // 8
NCH = 4 * IT
TR_TILE = 76

_PHASES = 6
_DBG = False


class Sem:
    def __init__(self, h):
        self.h = h
        self.v = 0

    def inc(self, instr, n=1):
        instr.then_inc(self.h, n)
        self.v += n
        return (self.h, self.v)


def chain(eng, sem, instr):
    ev = sem.inc(instr, 1)
    eng.wait_ge(*ev)
    return ev


def wt(eng, ev):
    if ev is not None:
        eng.wait_ge(*ev)


def build_program(phases=6, dbg=False):
    nc = bass.Bass("TRN2", target_bir_lowering=False)

    x_in = nc.dram_tensor("x", [NTOK, D], F32, kind="ExternalInput").ap()
    wgT = nc.dram_tensor("wgT", [D, I], F32, kind="ExternalInput").ap()
    wuT = nc.dram_tensor("wuT", [D, I], F32, kind="ExternalInput").ap()
    wdT = nc.dram_tensor("wdT", [I, D], F32, kind="ExternalInput").ap()
    wbT = nc.dram_tensor("wbT", [D, D], F32, kind="ExternalInput").ap()
    wrT = nc.dram_tensor("wrT", [D, 8], F32, kind="ExternalInput").ap()
    atT = nc.dram_tensor("atT", [D, P], F32, kind="ExternalInput").ap()
    btT = nc.dram_tensor("btT", [P, D], F32, kind="ExternalInput").ap()
    bsbi = nc.dram_tensor("bsbi", [P, HT], F32, kind="ExternalInput").ap()
    idbf = nc.dram_tensor("idbf", [P, P], BF16, kind="ExternalInput").ap()
    idf = nc.dram_tensor("idf", [P, P], F32, kind="ExternalInput").ap()
    repmi = nc.dram_tensor("repmi", [8, P], F32, kind="ExternalInput").ap()
    out = nc.dram_tensor("out", [NTOK, D], F32, kind="ExternalOutput").ap()

    sk = "ExternalOutput" if dbg else "Internal"
    sT_sp = nc.dram_tensor("sT_sp", [IT, P, NTOK], BF16, kind=sk).ap()
    sabs_sp = nc.dram_tensor("sabs_sp", [NT, P, I], BF16, kind=sk).ap()
    uT_sp = nc.dram_tensor("uT_sp", [IT, P, NTOK], BF16, kind=sk).ap()
    x2lnT_sp = nc.dram_tensor("x2lnT_sp", [HT, P, NTOK], BF16, kind=sk).ap()
    wd_bf = nc.dram_tensor("wd_bf", [HT, IT, P, P], BF16, kind="Internal").ap()
    tflat_sp = nc.dram_tensor("tflat_sp", [NTOK], F32, kind="Internal").ap()

    with ExitStack() as ctx:
        ident_bf = ctx.enter_context(nc.sbuf_tensor("ident_bf", [P, P], BF16))
        ident_f = ctx.enter_context(nc.sbuf_tensor("ident_f", [P, P], F32))
        t_all = ctx.enter_context(nc.sbuf_tensor("t_all", [P, NT], F32))
        ones1 = ctx.enter_context(nc.sbuf_tensor("ones1", [1, P], F32))
        cst = ctx.enter_context(nc.sbuf_tensor("cst", [P, 2], F32))

        S = {}
        for name in ["hld", "spl", "spl2", "wld", "pc", "mm", "tr", "sc", "ve",
                     "sc2", "ve2", "acc", "uesc", "ueve", "bsc", "bve", "uspl",
                     "pcg"]:
            S[name] = Sem(ctx.enter_context(nc.semaphore("s_" + name)))
        s_hld, s_spl, s_spl2, s_wld, s_pc = S["hld"], S["spl"], S["spl2"], S["wld"], S["pc"]
        s_mm, s_tr, s_sc, s_ve = S["mm"], S["tr"], S["sc"], S["ve"]
        s_sc2, s_ve2, s_acc = S["sc2"], S["ve2"], S["acc"]
        s_uesc, s_ueve = S["uesc"], S["ueve"]
        s_bsc, s_bve = S["bsc"], S["bve"]
        s_uspl = S["uspl"]

        sc_chunks = [j for j in range(NCH) if (j // 4) % 2 == 0]
        ve_chunks = [j for j in range(NCH) if (j // 4) % 2 == 1]

        def ue_ev(j):
            if (j // 4) % 2 == 0:
                return (s_uesc.h, sc_chunks.index(j) + 1)
            return (s_ueve.h, ve_chunks.index(j) + 1)

        def emit_ln(blk, x_ap, xsq_ap, st, out_bf_ap, ev_in_ready, ev_extra_sc=None):
            evs = {}

            @blk.scalar
            def _(sc):
                wt(sc, ev_in_ready)
                if ev_extra_sc is not None:
                    wt(sc, ev_extra_sc)
                evs["sq"] = s_sc.inc(sc.activation(out=xsq_ap, in_=x_ap, func=AF.Square,
                                                   accum_out=st[:, 2:3]))

            @blk.vector
            def _(v):
                wt(v, ev_in_ready)
                chain(v, s_ve2, v.reduce_sum(st[:, 0:1], x_ap, axis=AX.X))
                chain(v, s_ve2, v.tensor_scalar(out=st[:, 3:4], in0=st[:, 0:1],
                                                scalar1=1.0 / D, scalar2=None, op0=ALU.mult))
                wt(v, evs["sq"])
                chain(v, s_ve2, v.tensor_scalar(out=st[:, 4:5], in0=st[:, 2:3],
                                                scalar1=1.0 / D, scalar2=None, op0=ALU.mult))
                chain(v, s_ve2, v.tensor_tensor(out=st[:, 1:2], in0=st[:, 3:4],
                                                in1=st[:, 3:4], op=ALU.mult))
                chain(v, s_ve2, v.tensor_tensor(out=st[:, 5:6], in0=st[:, 4:5],
                                                in1=st[:, 1:2], op=ALU.subtract))
                evs["var"] = chain(v, s_ve2, v.tensor_scalar(out=st[:, 5:6], in0=st[:, 5:6],
                                                             scalar1=EPS, scalar2=None, op0=ALU.add))

            @blk.scalar
            def _(sc):
                wt(sc, evs["var"])
                evs["sqrt"] = s_sc.inc(sc.activation(out=st[:, 6:7], in_=st[:, 5:6], func=AF.Sqrt))

            @blk.vector
            def _(v):
                wt(v, evs["sqrt"])
                chain(v, s_ve2, v.reciprocal(st[:, 6:7], st[:, 6:7]))
                evs["rb"] = chain(v, s_ve2, v.scalar_tensor_tensor(
                    out=st[:, 7:8], in0=st[:, 3:4], scalar=-1.0, in1=st[:, 6:7],
                    op0=ALU.mult, op1=ALU.mult))

            @blk.scalar
            def _(sc):
                wt(sc, evs["rb"])
                evs["norm"] = s_sc.inc(sc.activation(out=out_bf_ap, in_=x_ap, func=AF.Identity,
                                                     scale=st[:, 6:7], bias=st[:, 7:8]))

            return evs["norm"], evs["rb"]

        tile_mm_last = {}

        with ExitStack() as ctx03:
            xlnT = ctx03.enter_context(nc.sbuf_tensor("xlnT", [P, HT * NTOK], BF16))
            w_sb = [ctx03.enter_context(nc.sbuf_tensor(f"w_sb{j}", [P, HT * P], BF16))
                    for j in range(2)]

            def xlnT_sl(ht, c0, c1):
                return xlnT[:, ht * NTOK + c0: ht * NTOK + c1]

            with ExitStack() as c0:
                x_st = [c0.enter_context(nc.sbuf_tensor(f"x_st{j}", [P, D], F32)) for j in range(2)]
                st0 = c0.enter_context(nc.sbuf_tensor("st0", [P, 8], F32))
                xln_st = [c0.enter_context(nc.sbuf_tensor(f"xln_st{j}", [P, D], BF16)) for j in range(2)]
                ps_trX = [c0.enter_context(nc.psum_tensor(f"ps_trX{j}", [P, 1024], BF16)) for j in range(2)]

                with nc.Block() as blk:
                    @blk.sync
                    def _(sync):
                        sync.dma_start(out=ident_bf[:, :], in_=idbf[:, :]).then_inc(s_hld.h, 16)
                        sync.dma_start(out=ident_f[:, :], in_=idf[:, :]).then_inc(s_hld.h, 16)
                    s_hld.v += 32

                    @blk.gpsimd
                    def _(g):
                        g.memset(ones1[:, :], 1.0)
                        g.memset(cst[:, 0:1], 1.0)
                        g.memset(cst[:, 1:2], HI0)

                    ev_norm = [None] * NT
                    ev_ldx = [None] * NT
                    ev_tr = {}
                    ev_evac = {}
                    for nt in range(NT):
                        @blk.sync
                        def _(sync, nt=nt):
                            if nt >= 2:
                                wt(sync, ev_norm[nt - 2])
                            ev_ldx[nt] = s_hld.inc(sync.dma_start(
                                out=x_st[nt % 2][:, :], in_=x_in[nt * P:(nt + 1) * P, :]), 16)

                        ev_norm[nt], _rb = emit_ln(
                            blk, x_st[nt % 2][:, :], xln_st[nt % 2][:, :], st0,
                            xln_st[nt % 2][:, :], ev_ldx[nt],
                            ev_extra_sc=(ev_tr[4 * (nt - 2) + 3] if nt >= 2 else None))

                        for g2 in range(4):
                            @blk.tensor
                            def _(t, nt=nt, g2=g2):
                                k = 4 * nt + g2
                                if g2 == 0:
                                    wt(t, ev_norm[nt])
                                if k >= 2:
                                    wt(t, ev_evac[k - 2])
                                for h in range(8):
                                    ht = 8 * g2 + h
                                    instr = t.transpose(ps_trX[g2 % 2][:, h * P:(h + 1) * P],
                                                        xln_st[nt % 2][:, ht * P:(ht + 1) * P],
                                                        ident_bf[:, :])
                                ev_tr[k] = s_tr.inc(instr)

                            @blk.vector
                            def _(v, nt=nt, g2=g2):
                                k = 4 * nt + g2
                                wt(v, ev_tr[k])
                                dst = xlnT[:, :].rearrange("p (ht n) -> p ht n", n=NTOK)[
                                    :, 8 * g2:8 * (g2 + 1), nt * P:(nt + 1) * P]
                                src = ps_trX[g2 % 2][:, :].rearrange("p (h n) -> p h n", n=P)
                                ev_evac[k] = s_ve.inc(v.tensor_copy(dst, src))

            ev_p0_done = (s_ve.h, s_ve.v)

            if phases < 1:
                _finish_partial(nc, S, out)
                return nc

            pc_jobs = []
            for ht in range(HT):
                for it0 in range(0, IT, 8):
                    pc_jobs.append((ht, it0, min(8, IT - it0)))

            with ExitStack() as c1:
                sT_st = [c1.enter_context(nc.sbuf_tensor(f"sT_st{j}", [P, NTOK], BF16))
                         for j in range(2)]
                ab_st = [c1.enter_context(nc.sbuf_tensor(f"ab_st{j}", [P, NTOK], BF16))
                         for j in range(2)]
                ps_g = [c1.enter_context(nc.psum_tensor(f"ps_g{j}", [P, 512], F32))
                        for j in range(4)]
                ps_trS = [c1.enter_context(nc.psum_tensor(f"ps_trS{j}", [P, 512], BF16))
                          for j in range(4)]

                with nc.Block() as blk:
                    ev_wld1 = {}
                    ev_mmc = {}
                    ev_silu = {}
                    ev_trc = {}
                    ev_absc = {}
                    ev_spl_sT = {}
                    ev_spl_ab = {}
                    pci = [0]

                    def emit_tr1(t, j):
                        ti, c = divmod(j, 4)
                        wt(t, ev_silu[j])
                        if j >= 4:
                            wt(t, ev_absc[j - 4])
                        for h in range(4):
                            tok = c * 4 + h
                            instr = t.transpose(ps_trS[j % 4][:, h * P:(h + 1) * P],
                                                sT_st[ti % 2][:, tok * P:(tok + 1) * P],
                                                ident_bf[:, :])
                        ev_trc[j] = s_tr.inc(instr)

                    def emit_abs1(sc, j):
                        ti, c = divmod(j, 4)
                        wt(sc, ev_trc[j])
                        if ti >= 2 and c == 0:
                            wt(sc, ev_spl_ab[ti - 2])
                        ev_absc[j] = s_sc.inc(sc.activation(
                            out=ab_st[ti % 2][:, c * 512:(c + 1) * 512],
                            in_=ps_trS[j % 4][:, :], func=AF.Abs))

                    for i in range(IT):
                        @blk.gpsimd
                        def _(g, i=i):
                            if i >= 2:
                                wt(g, tile_mm_last[i - 2])
                            for q in range(4):
                                src = wgT.rearrange("(ht p) (it i) -> p ht it i", p=P, i=P)[
                                    :, 8 * q:8 * (q + 1), i, :]
                                dst = w_sb[i % 2][:, :].rearrange("p (ht i) -> p ht i", i=P)[
                                    :, 8 * q:8 * (q + 1), :]
                                ev_wld1[i] = s_wld.inc(g.dma_start(out=dst, in_=src), 16)
                            n_pc = 5 if i < 8 else 4
                            for _k in range(n_pc):
                                if pci[0] < len(pc_jobs):
                                    htj, it0, n_it = pc_jobs[pci[0]]
                                    src = wdT.rearrange("(it p) (ht h) -> p it ht h", p=P, h=P)[
                                        :, it0:it0 + n_it, htj, :]
                                    dst = wd_bf[htj, it0:it0 + n_it, :, :].rearrange("it p h -> p it h")
                                    s_pc.inc(g.dma_start(out=dst, in_=src), 16)
                                    pci[0] += 1

                        @blk.tensor
                        def _(t, i=i):
                            for c in range(4):
                                j = 4 * i + c
                                if j == 0:
                                    wt(t, ev_p0_done)
                                if c == 0:
                                    wt(t, ev_wld1[i])
                                if j >= 4:
                                    wt(t, ev_silu[j - 4])
                                for ht in range(HT):
                                    instr = t.matmul(ps_g[j % 4][:, :],
                                                     w_sb[i % 2][:, ht * P:(ht + 1) * P],
                                                     xlnT_sl(ht, c * 512, (c + 1) * 512),
                                                     start=(ht == 0), stop=(ht == HT - 1))
                                ev_mmc[j] = s_mm.inc(instr)
                                if i >= 1:
                                    emit_tr1(t, 4 * (i - 1) + c)
                            tile_mm_last[i] = ev_mmc[4 * i + 3]

                        @blk.scalar
                        def _(sc, i=i):
                            for c in range(4):
                                j = 4 * i + c
                                wt(sc, ev_mmc[j])
                                if i >= 2 and c == 0:
                                    wt(sc, ev_spl_sT[i - 2])
                                if j >= 8:
                                    wt(sc, ev_trc[j - 8])
                                ev_silu[j] = s_sc.inc(sc.activation(
                                    out=sT_st[i % 2][:, c * 512:(c + 1) * 512],
                                    in_=ps_g[j % 4][:, :], func=AF.Silu))
                                if i >= 1:
                                    emit_abs1(sc, 4 * (i - 1) + c)

                        def emit_ab_spill(sync, i):
                            wt(sync, ev_absc[4 * i + 3])
                            for half in range(2):
                                src = ab_st[i % 2][:, :].rearrange("p (nt n) -> p nt n", n=P)[
                                    :, 8 * half:8 * (half + 1), :]
                                dst = sabs_sp.rearrange("nt p (it i) -> p nt it i", i=P)[
                                    :, 8 * half:8 * (half + 1), i, :]
                                ev_spl_ab[i] = s_spl2.inc(sync.dma_start(out=dst, in_=src), 16)

                        @blk.sync
                        def _(sync, i=i, emit_ab_spill=emit_ab_spill):
                            wt(sync, ev_silu[4 * i + 3])
                            ev_spl_sT[i] = s_spl.inc(sync.dma_start(
                                out=sT_sp[i, :, :], in_=sT_st[i % 2][:, :]), 16)
                            if i >= 1:
                                emit_ab_spill(sync, i - 1)

                    @blk.tensor
                    def _(t):
                        for c in range(4):
                            emit_tr1(t, 4 * (IT - 1) + c)

                    @blk.scalar
                    def _(sc):
                        for c in range(4):
                            emit_abs1(sc, 4 * (IT - 1) + c)

                    @blk.sync
                    def _(sync):
                        emit_ab_spill(sync, IT - 1)

            p1_sc_final = (s_sc.h, s_sc.v)
            p1_spl_final = (s_spl.h, s_spl.v)
            p1_spl2_final = (s_spl2.h, s_spl2.v)

            if phases < 2:
                _finish_partial(nc, S, out)
                return nc

            with ExitStack() as c2:
                vbuf = c2.enter_context(nc.sbuf_tensor("vbuf", [P, I], BF16))
                sbuf2 = c2.enter_context(nc.sbuf_tensor("sbuf2", [P, I], BF16))
                dummy = c2.enter_context(nc.sbuf_tensor("bis_dummy", [P, NQ], BF16))
                u_st = [c2.enter_context(nc.sbuf_tensor(f"u_st{j}", [P, NTOK], BF16))
                        for j in range(2)]
                vst = c2.enter_context(nc.sbuf_tensor("vst", [P, 10], F32))
                sst = c2.enter_context(nc.sbuf_tensor("sst", [P, 16], F32))
                vbf = c2.enter_context(nc.sbuf_tensor("vbf", [P, 1], BF16))
                vgt8 = c2.enter_context(nc.sbuf_tensor("vgt8", [P, 1], mybir.dt.uint8))
                vle8 = c2.enter_context(nc.sbuf_tensor("vle8", [P, 1], mybir.dt.uint8))
                sbf1 = c2.enter_context(nc.sbuf_tensor("sbf1", [P, 1], BF16))
                taT = c2.enter_context(nc.sbuf_tensor("taT", [NT, P], F32))
                ps_u = [c2.enter_context(nc.psum_tensor(f"ps_u{j}", [P, 512], F32))
                        for j in range(6)]
                ps_tt = c2.enter_context(nc.psum_tensor("ps_tt", [NT, P], F32))

                with nc.Block() as blk:
                    ev_wld2 = {}
                    ev_mmu = {}
                    ev_uspl = {}
                    ev_taT_tr = [None]
                    ev_vl = [None] * 8
                    ev_sl = [None] * 8
                    ev_vt = [None] * 8
                    ev_st_ = [None] * 8
                    ev_taT_ev = [None]

                    def emit_uspill(g, ti):
                        g.wait_ge(*ue_ev(4 * ti + 3))
                        ev_uspl[ti] = s_uspl.inc(g.dma_start(
                            out=uT_sp[ti, :, :], in_=u_st[ti % 2][:, :]), 16)

                    for ti in range(IT):
                        gi = IT + ti

                        @blk.gpsimd
                        def _(g, ti=ti, gi=gi):
                            wt(g, tile_mm_last[gi - 2])
                            for q in range(4):
                                src = wuT.rearrange("(ht p) (it i) -> p ht it i", p=P, i=P)[
                                    :, 8 * q:8 * (q + 1), ti, :]
                                dst = w_sb[gi % 2][:, :].rearrange("p (ht i) -> p ht i", i=P)[
                                    :, 8 * q:8 * (q + 1), :]
                                ev_wld2[ti] = s_wld.inc(g.dma_start(out=dst, in_=src), 16)
                            if ti >= 2:
                                emit_uspill(g, ti - 2)

                        @blk.tensor
                        def _(t, ti=ti, gi=gi):
                            if ti == TR_TILE:
                                t.wait_ge(s_bve.h, 8)
                                t.wait_ge(s_bsc.h, 8)
                                ev_taT_tr[0] = s_tr.inc(t.transpose(
                                    ps_tt[:, :], t_all[:, :], ident_f[:, :]))
                            for c in range(4):
                                j = 4 * ti + c
                                if j == 0:
                                    wt(t, p1_sc_final)
                                if c == 0:
                                    wt(t, ev_wld2[ti])
                                if j >= 6:
                                    t.wait_ge(*ue_ev(j - 6))
                                for ht in range(HT):
                                    instr = t.matmul(ps_u[j % 6][:, :],
                                                     w_sb[gi % 2][:, ht * P:(ht + 1) * P],
                                                     xlnT_sl(ht, c * 512, (c + 1) * 512),
                                                     start=(ht == 0), stop=(ht == HT - 1))
                                ev_mmu[j] = s_mm.inc(instr)
                            tile_mm_last[gi] = ev_mmu[4 * ti + 3]

                    @blk.gpsimd
                    def _(g):
                        emit_uspill(g, IT - 2)
                        emit_uspill(g, IT - 1)

                    sc_ptr = [0]
                    ve_ptr = [0]

                    def due_chunk(slot):
                        return 2 * (slot + 1)

                    def v_emit_evacs(v, upto_j):
                        while ve_ptr[0] < len(ve_chunks) and ve_chunks[ve_ptr[0]] <= upto_j:
                            j = ve_chunks[ve_ptr[0]]
                            ti, c = divmod(j, 4)
                            wt(v, ev_mmu[j])
                            if c == 0 and ti >= 2:
                                wt(v, ev_uspl[ti - 2])
                            s_ueve.inc(v.tensor_copy(
                                u_st[ti % 2][:, c * 512:(c + 1) * 512], ps_u[j % 6][:, :]))
                            ve_ptr[0] += 1

                    def s_emit_evacs(sc, upto_j):
                        while sc_ptr[0] < len(sc_chunks) and sc_chunks[sc_ptr[0]] <= upto_j:
                            j = sc_chunks[sc_ptr[0]]
                            ti, c = divmod(j, 4)
                            wt(sc, ev_mmu[j])
                            if c == 0 and ti >= 2:
                                wt(sc, ev_uspl[ti - 2])
                            s_uesc.inc(sc.activation(
                                out=u_st[ti % 2][:, c * 512:(c + 1) * 512],
                                in_=ps_u[j % 6][:, :], func=AF.Identity))
                            sc_ptr[0] += 1

                    for k in range(8):
                        @blk.sync
                        def _(sync, k=k):
                            if k == 0:
                                sync.wait_ge(*p1_spl2_final)
                                sync.wait_ge(*p1_spl_final)
                            if k >= 1:
                                wt(sync, ev_vt[k - 1])
                            ev_vl[k] = s_hld.inc(sync.dma_start(out=vbuf[:, :],
                                                                in_=sabs_sp[k, :, :]), 16)
                            if k >= 1:
                                wt(sync, ev_st_[k - 1])
                            ev_sl[k] = s_hld.inc(sync.dma_start(out=sbuf2[:, :],
                                                                in_=sabs_sp[8 + k, :, :]), 16)

                        @blk.vector
                        def _(v, k=k):
                            wt(v, ev_vl[k])
                            chain(v, s_ve2, v.memset(vst[:, 0:1], 0.0))
                            chain(v, s_ve2, v.memset(vst[:, 1:2], HI0))
                            for it_ in range(BIS_ITERS):
                                chain(v, s_ve2, v.tensor_scalar(out=vst[:, 2:3], in0=vst[:, 0:1],
                                                                scalar1=vst[:, 1:2], scalar2=0.5,
                                                                op0=ALU.add, op1=ALU.mult))
                                chain(v, s_ve2, v.tensor_copy(vbf[:, :], vst[:, 2:3]))
                                chain(v, s_ve2, v.tensor_copy(vst[:, 3:4], vbf[:, :]))
                                for q in range(4):
                                    acc_col = vst[:, 4:5] if q == 0 else vst[:, 5:6]
                                    chain(v, s_ve2, v.tensor_scalar(
                                        out=dummy[:, :], in0=vbuf[:, q * NQ:(q + 1) * NQ],
                                        scalar1=vst[:, 3:4], scalar2=0.0,
                                        op0=ALU.is_gt, op1=ALU.add, accum_out=acc_col))
                                    if q >= 1:
                                        chain(v, s_ve2, v.tensor_tensor(
                                            out=vst[:, 4:5], in0=vst[:, 4:5],
                                            in1=vst[:, 5:6], op=ALU.add))
                                chain(v, s_ve2, v.tensor_scalar(out=vgt8[:, :], in0=vst[:, 4:5],
                                                                scalar1=KKEEP, scalar2=None,
                                                                op0=ALU.is_gt))
                                chain(v, s_ve2, v.tensor_scalar(out=vle8[:, :], in0=vst[:, 4:5],
                                                                scalar1=KKEEP, scalar2=None,
                                                                op0=ALU.is_le))
                                chain(v, s_ve2, v.select(vst[:, 0:1], vgt8[:, :],
                                                         vst[:, 3:4], vst[:, 0:1]))
                                chain(v, s_ve2, v.select(vst[:, 1:2], vle8[:, :],
                                                         vst[:, 3:4], vst[:, 1:2]))
                                v_emit_evacs(v, due_chunk(k * BIS_ITERS + it_))
                            ev_vt[k] = chain(v, s_ve2, v.tensor_copy(t_all[:, k:k + 1], vst[:, 1:2]))
                            s_bve.inc(v.tensor_copy(vst[:, 9:10], vst[:, 1:2]))

                        @blk.scalar
                        def _(sc, k=k):
                            wt(sc, ev_sl[k])
                            chain(sc, s_sc2, sc.memzero(sst[:, 0:1]))
                            chain(sc, s_sc2, sc.activation(out=sst[:, 1:2], in_=sst[:, 0:1],
                                                           func=AF.Identity, scale=0.0,
                                                           bias=cst[:, 1:2]))
                            for it_ in range(BIS_ITERS):
                                chain(sc, s_sc2, sc.activation(out=sst[:, 2:3], in_=sst[:, 1:2],
                                                               func=AF.Identity, scale=0.5))
                                chain(sc, s_sc2, sc.activation(out=sst[:, 3:4], in_=sst[:, 0:1],
                                                               func=AF.Identity, scale=0.5,
                                                               bias=sst[:, 2:3]))
                                chain(sc, s_sc2, sc.activation(out=sbf1[:, :], in_=sst[:, 3:4],
                                                               func=AF.Identity))
                                chain(sc, s_sc2, sc.activation(out=sst[:, 4:5], in_=sbf1[:, :],
                                                               func=AF.Identity))
                                chain(sc, s_sc2, sc.activation(out=sst[:, 5:6], in_=sst[:, 4:5],
                                                               func=AF.Identity, scale=-SNAP))
                                for q in range(4):
                                    chain(sc, s_sc2, sc.activation(
                                        out=dummy[:, :], in_=sbuf2[:, q * NQ:(q + 1) * NQ],
                                        func=AF.Sign, bias=sst[:, 5:6],
                                        accum_out=sst[:, 12 + q:13 + q]))
                                chain(sc, s_sc2, sc.activation(out=sst[:, 6:7], in_=sst[:, 12:13],
                                                               func=AF.Identity, bias=sst[:, 13:14]))
                                chain(sc, s_sc2, sc.activation(out=sst[:, 2:3], in_=sst[:, 14:15],
                                                               func=AF.Identity, bias=sst[:, 15:16]))
                                chain(sc, s_sc2, sc.activation(out=sst[:, 3:4], in_=sst[:, 2:3],
                                                               func=AF.Identity, bias=sst[:, 6:7]))
                                chain(sc, s_sc2, sc.activation(out=sst[:, 7:8], in_=sst[:, 3:4],
                                                               func=AF.Sign))
                                chain(sc, s_sc2, sc.activation(out=sst[:, 8:9], in_=sst[:, 7:8],
                                                               func=AF.Relu))
                                chain(sc, s_sc2, sc.activation(out=sst[:, 9:10], in_=sst[:, 8:9],
                                                               func=AF.Identity, scale=-1.0,
                                                               bias=cst[:, 0:1]))
                                chain(sc, s_sc2, sc.activation(out=sst[:, 10:11], in_=sst[:, 0:1],
                                                               func=AF.Identity, scale=-1.0,
                                                               bias=sst[:, 4:5]))
                                chain(sc, s_sc2, sc.activation(out=sst[:, 11:12], in_=sst[:, 1:2],
                                                               func=AF.Identity, scale=-1.0,
                                                               bias=sst[:, 4:5]))
                                chain(sc, s_sc2, sc.activation(out=sst[:, 0:1], in_=sst[:, 8:9],
                                                               func=AF.Identity, scale=sst[:, 10:11],
                                                               bias=sst[:, 0:1]))
                                chain(sc, s_sc2, sc.activation(out=sst[:, 1:2], in_=sst[:, 9:10],
                                                               func=AF.Identity, scale=sst[:, 11:12],
                                                               bias=sst[:, 1:2]))
                                s_emit_evacs(sc, due_chunk(k * BIS_ITERS + it_))
                            ev_st_[k] = chain(sc, s_sc2, sc.activation(
                                out=t_all[:, 8 + k:9 + k], in_=sst[:, 0:1],
                                func=AF.Identity, scale=SNAP))
                            s_bsc.inc(sc.activation(out=sbf1[:, :], in_=sst[:, 1:2],
                                                    func=AF.Identity))

                    @blk.vector
                    def _(v):
                        v_emit_evacs(v, 4 * TR_TILE - 1)
                        wt(v, ev_taT_tr[0])
                        ev_taT_ev[0] = s_ve.inc(v.tensor_copy(taT[:, :], ps_tt[:, :]))
                        v_emit_evacs(v, NCH - 1)

                    @blk.scalar
                    def _(sc):
                        s_emit_evacs(sc, NCH - 1)

                    @blk.sync
                    def _(sync):
                        wt(sync, ev_taT_ev[0])
                        sync.dma_start(out=tflat_sp.rearrange("(a b) -> a b", b=P),
                                       in_=taT[:, :]).then_inc(s_spl.h, 16)
                        s_spl.v += 16

            p23_mm_final = (s_mm.h, s_mm.v)
            p23_sc2_final = (s_sc2.h, s_sc2.v)
            p23_ve2_final = (s_ve2.h, s_ve2.v)
            p23_spl_final = (s_spl.h, s_spl.v)
            p23_uspl_final = (s_uspl.h, s_uspl.v)
            ue_sc_total = (s_uesc.h, len(sc_chunks))
            ue_ve_total = (s_ueve.h, len(ve_chunks))

            if phases < 3:
                _finish_partial(nc, S, out)
                return nc

        with ExitStack() as c4:
            gcol = c4.enter_context(nc.sbuf_tensor("gcol", [P, IT * 512], BF16))
            s_stage = [c4.enter_context(nc.sbuf_tensor(f"s_stage{j}", [P, 8 * 512], BF16))
                       for j in range(2)]
            abs_st4 = c4.enter_context(nc.sbuf_tensor("abs_st4", [P, 8 * 512], BF16))
            t_rep = c4.enter_context(nc.sbuf_tensor("t_rep", [P, NTOK], F32))
            tfl = c4.enter_context(nc.sbuf_tensor("tfl", [1, NTOK], F32))
            wdr = c4.enter_context(nc.sbuf_tensor("wdr", [P, 4 * 8 * P], BF16))
            yT_sb = [c4.enter_context(nc.sbuf_tensor(f"yT_sb{j}", [P, 512], BF16)) for j in range(2)]
            x2y_sb = c4.enter_context(nc.sbuf_tensor("x2y_sb", [P, 4 * D], BF16))
            x_st4 = c4.enter_context(nc.sbuf_tensor("x_st4", [P, D], F32))
            st4 = c4.enter_context(nc.sbuf_tensor("st4", [P, 8], F32))
            x2ln_st = c4.enter_context(nc.sbuf_tensor("x2ln_st", [P, D], BF16))
            xlt_st = [c4.enter_context(nc.sbuf_tensor(f"xlt_st{j}", [P, 8 * P], BF16)) for j in range(2)]
            ps_yT = [c4.enter_context(nc.psum_tensor(f"ps_yT{j}", [P, 512], F32)) for j in range(2)]
            ps_trY = [c4.enter_context(nc.psum_tensor(f"ps_trY{j}", [P, 512], BF16)) for j in range(2)]
            ps_trL = [c4.enter_context(nc.psum_tensor(f"ps_trL{j}", [P, 1024], BF16)) for j in range(2)]

            it_groups = [(g * 8, min(8, IT - g * 8)) for g in range(NGR)]

            with nc.Block() as blk:
                ev_pc_done = (s_pc.h, s_pc.v)
                ev_tfl = [None]
                ev_trep = {}
                ev_uld = {}
                ev_sld = {}
                ev_absg = {}
                ev_apply = {}
                ev_ring = {}
                ev_mmg = {}
                ev_yev = {}
                ev_trY = {}
                ev_chev = {}
                ev_ldx4 = {}
                ev_add = {}
                ev_norm4 = {}
                ev_trL = {}
                ev_xlte = {}
                ev_spl_xlt = {}
                ev_out = {}

                @blk.gpsimd
                def _(g):
                    g.wait_ge(*ev_pc_done)
                    g.nop().then_inc(S["pcg"].h, 1)
                S["pcg"].v += 1
                ev_pc_all = (S["pcg"].h, S["pcg"].v)

                @blk.sync
                def _(sync):
                    sync.wait_ge(*p23_spl_final)
                    ev_tfl[0] = s_hld.inc(sync.dma_start(
                        out=tfl[:, :], in_=tflat_sp.rearrange("(a b) -> a b", a=1)), 16)

                for c in range(4):
                    @blk.tensor
                    def _(t, c=c):
                        if c == 0:
                            wt(t, ev_tfl[0])
                            t.wait_ge(*ue_sc_total)
                            t.wait_ge(*ue_ve_total)
                            wt(t, p23_ve2_final)
                        if c >= 2:
                            wt(t, ev_trep[c - 2])
                        instr = t.matmul(ps_yT[c % 2][:, :], ones1[:, :],
                                         tfl[:, c * 512:(c + 1) * 512], start=True, stop=True)
                        ev_trep[(c, "mm")] = s_tr.inc(instr)

                    @blk.vector
                    def _(v, c=c):
                        wt(v, ev_trep[(c, "mm")])
                        ev_trep[c] = s_ve.inc(v.tensor_copy(
                            t_rep[:, c * 512:(c + 1) * 512], ps_yT[c % 2][:, :]))

                for b in range(NB):
                    for g2, (it0, n_it) in enumerate(it_groups):
                        ga = b * NGR + g2

                        @blk.sync
                        def _(sync, b=b, g2=g2, it0=it0, n_it=n_it, ga=ga):
                            if ga == 0:
                                sync.wait_ge(*ue_sc_total)
                                sync.wait_ge(*ue_ve_total)
                                sync.wait_ge(*p23_sc2_final)
                                sync.wait_ge(*p23_ve2_final)
                                sync.wait_ge(*p23_mm_final)
                                sync.wait_ge(*p23_uspl_final)
                            if b > 0:
                                wt(sync, ev_mmg[(b - 1, HT - 1, g2)])
                            src = uT_sp[it0:it0 + n_it, :, b * NBTOK:(b + 1) * NBTOK].rearrange(
                                "it p n -> p it n")
                            dst = gcol[:, it0 * 512:(it0 + n_it) * 512].rearrange(
                                "p (it n) -> p it n", n=512)
                            ev_uld[ga] = s_hld.inc(sync.dma_start(out=dst, in_=src), 16)
                            if ga >= 2:
                                wt(sync, ev_apply[ga - 2])
                            src2 = sT_sp[it0:it0 + n_it, :, b * NBTOK:(b + 1) * NBTOK].rearrange(
                                "it p n -> p it n")
                            dst2 = s_stage[ga % 2][:, 0:n_it * 512].rearrange(
                                "p (it n) -> p it n", n=512)
                            ev_sld[ga] = s_hld.inc(sync.dma_start(out=dst2, in_=src2), 16)

                        @blk.scalar
                        def _(sc, ga=ga, n_it=n_it):
                            wt(sc, ev_sld[ga])
                            if ga >= 1:
                                wt(sc, ev_apply[ga - 1])
                            ev_absg[ga] = s_sc.inc(sc.activation(
                                out=abs_st4[:, 0:n_it * 512], in_=s_stage[ga % 2][:, 0:n_it * 512],
                                func=AF.Abs))

                        @blk.vector
                        def _(v, b=b, g2=g2, it0=it0, n_it=n_it, ga=ga):
                            wt(v, ev_uld[ga])
                            wt(v, ev_sld[ga])
                            gsl = gcol[:, it0 * 512:(it0 + n_it) * 512]
                            chain(v, s_ve2, v.tensor_tensor(
                                out=gsl, in0=gsl, in1=s_stage[ga % 2][:, 0:n_it * 512],
                                op=ALU.mult))
                            wt(v, ev_absg[ga])
                            for jj in range(n_it):
                                chain(v, s_ve2, v.tensor_tensor(
                                    out=s_stage[ga % 2][:, jj * 512:(jj + 1) * 512],
                                    in0=abs_st4[:, jj * 512:(jj + 1) * 512],
                                    in1=t_rep[:, b * NBTOK:(b + 1) * NBTOK], op=ALU.is_gt))
                            ev_apply[ga] = chain(v, s_ve2, v.tensor_tensor(
                                out=gsl, in0=gsl, in1=s_stage[ga % 2][:, 0:n_it * 512],
                                op=ALU.mult))

                    for ht in range(HT):
                        hi = b * HT + ht

                        for g2, (it0, n_it) in enumerate(it_groups):
                            @blk.gpsimd
                            def _(g, b=b, ht=ht, hi=hi, g2=g2, it0=it0, n_it=n_it):
                                gi = hi * NGR + g2
                                if gi == 0:
                                    g.wait_ge(*ev_pc_all)
                                    g.wait_ge(*p23_mm_final)
                                if gi >= 4:
                                    pb, rem = divmod(gi - 4, HT * NGR)
                                    pht, pgg = divmod(rem, NGR)
                                    wt(g, ev_mmg[(pb, pht, pgg)])
                                src = wd_bf[ht, it0:it0 + n_it, :, :].rearrange("it p h -> p it h")
                                dst = wdr[:, :].rearrange("p (r i h) -> p r i h", r=4, h=P)[
                                    :, gi % 4, 0:n_it, :]
                                ev_ring[gi] = s_wld.inc(g.dma_start(out=dst, in_=src), 16)

                            @blk.tensor
                            def _(t, b=b, ht=ht, hi=hi, g2=g2, it0=it0, n_it=n_it):
                                gi = hi * NGR + g2
                                if g2 == 0 and hi >= 2:
                                    wt(t, ev_yev[hi - 2])
                                wt(t, ev_ring[gi])
                                if ht == 0:
                                    wt(t, ev_apply[b * NGR + g2])
                                for jj in range(n_it):
                                    it = it0 + jj
                                    instr = t.matmul(
                                        ps_yT[ht % 2][:, :],
                                        wdr[:, :].rearrange("p (r i h) -> p r i h", r=4, h=P)[
                                            :, gi % 4, jj, :],
                                        gcol[:, it * 512:(it + 1) * 512],
                                        start=(it == 0), stop=(it == IT - 1))
                                ev_mmg[(b, ht, g2)] = s_mm.inc(instr)

                        @blk.scalar
                        def _(sc, b=b, ht=ht, hi=hi):
                            wt(sc, ev_mmg[(b, ht, NGR - 1)])
                            if hi >= 2:
                                wt(sc, ev_trY[hi - 2])
                            ev_yev[hi] = s_sc.inc(sc.activation(out=yT_sb[ht % 2][:, :],
                                                                in_=ps_yT[ht % 2][:, :],
                                                                func=AF.Identity))

                        @blk.tensor
                        def _(t, b=b, ht=ht, hi=hi):
                            wt(t, ev_yev[hi])
                            if hi >= 2:
                                wt(t, ev_chev[hi - 2])
                            for c in range(4):
                                instr = t.transpose(ps_trY[ht % 2][:, c * P:(c + 1) * P],
                                                    yT_sb[ht % 2][:, c * P:(c + 1) * P],
                                                    ident_bf[:, :])
                            ev_trY[hi] = s_tr.inc(instr)

                        @blk.vector
                        def _(v, b=b, ht=ht, hi=hi):
                            wt(v, ev_trY[hi])
                            if ht == 0 and b > 0:
                                for nt in range(4):
                                    wt(v, ev_out[(b - 1, nt)])
                                    wt(v, ev_norm4[(b - 1, nt)])
                            for c in range(4):
                                instr = v.tensor_copy(
                                    x2y_sb[:, c * D + ht * P: c * D + (ht + 1) * P],
                                    ps_trY[ht % 2][:, c * P:(c + 1) * P])
                            ev_chev[hi] = s_ve.inc(instr)

                    for nt in range(4):
                        @blk.sync
                        def _(sync, b=b, nt=nt):
                            if nt >= 1 or b > 0:
                                pb, pnt = (b, nt - 1) if nt >= 1 else (b - 1, 3)
                                wt(sync, ev_out[(pb, pnt)])
                                wt(sync, ev_norm4[(pb, pnt)])
                            ev_ldx4[(b, nt)] = s_hld.inc(sync.dma_start(
                                out=x_st4[:, :],
                                in_=x_in[(b * 4 + nt) * P:(b * 4 + nt + 1) * P, :]), 16)

                        @blk.vector
                        def _(v, b=b, nt=nt):
                            wt(v, ev_ldx4[(b, nt)])
                            wt(v, ev_chev[b * HT + HT - 1])
                            ev_add[(b, nt)] = chain(v, s_ve2, v.tensor_tensor(
                                out=x_st4[:, :],
                                in0=x2y_sb[:, nt * D:(nt + 1) * D],
                                in1=x_st4[:, :], op=ALU.add))

                        ev_norm4[(b, nt)], _ = emit_ln(
                            blk, x_st4[:, :], x2y_sb[:, nt * D:(nt + 1) * D], st4,
                            x2ln_st[:, :], ev_add[(b, nt)])

                        for g2 in range(4):
                            @blk.tensor
                            def _(t, b=b, nt=nt, g2=g2):
                                k = (b * 4 + nt) * 4 + g2
                                if g2 == 0:
                                    wt(t, ev_norm4[(b, nt)])
                                if k >= 2:
                                    wt(t, ev_xlte[divmod(k - 2, 4)])
                                for h in range(8):
                                    ht2 = 8 * g2 + h
                                    instr = t.transpose(ps_trL[g2 % 2][:, h * P:(h + 1) * P],
                                                        x2ln_st[:, ht2 * P:(ht2 + 1) * P],
                                                        ident_bf[:, :])
                                ev_trL[(b * 4 + nt, g2)] = s_tr.inc(instr)

                            @blk.scalar
                            def _(sc, b=b, nt=nt, g2=g2):
                                k = (b * 4 + nt) * 4 + g2
                                wt(sc, ev_trL[(b * 4 + nt, g2)])
                                if k >= 2:
                                    wt(sc, ev_spl_xlt[divmod(k - 2, 4)])
                                ev_xlte[(b * 4 + nt, g2)] = s_sc.inc(sc.activation(
                                    out=xlt_st[k % 2][:, :], in_=ps_trL[g2 % 2][:, :],
                                    func=AF.Identity))

                            @blk.sync
                            def _(sync, b=b, nt=nt, g2=g2):
                                k = (b * 4 + nt) * 4 + g2
                                wt(sync, ev_xlte[(b * 4 + nt, g2)])
                                src = xlt_st[k % 2][:, :].rearrange("p (h n) -> p h n", n=P)
                                dst = x2lnT_sp[8 * g2:8 * (g2 + 1), :,
                                               (b * 4 + nt) * P:(b * 4 + nt + 1) * P].rearrange(
                                    "ht p n -> p ht n")
                                ev_spl_xlt[(b * 4 + nt, g2)] = s_spl2.inc(
                                    sync.dma_start(out=dst, in_=src), 16)

                        @blk.sync
                        def _(sync, b=b, nt=nt):
                            wt(sync, ev_add[(b, nt)])
                            ev_out[(b, nt)] = s_spl.inc(sync.dma_start(
                                out=out[(b * 4 + nt) * P:(b * 4 + nt + 1) * P, :],
                                in_=x_st4[:, :]), 16)

        p4_finals = [(s_mm.h, s_mm.v), (s_tr.h, s_tr.v), (s_sc.h, s_sc.v),
                     (s_ve.h, s_ve.v), (s_ve2.h, s_ve2.v),
                     (s_spl.h, s_spl.v), (s_spl2.h, s_spl2.v)]

        if phases < 5:
            _finish_partial(nc, S, out)
            return nc

        with ExitStack() as c5:
            xlnT2 = c5.enter_context(nc.sbuf_tensor("xlnT2", [P, HT * NTOK], BF16))
            bt_sb = c5.enter_context(nc.sbuf_tensor("bt_sb", [P, D], BF16))
            bs_sb = c5.enter_context(nc.sbuf_tensor("bs_sb", [P, HT], F32))
            hw_sb = c5.enter_context(nc.sbuf_tensor("hw_sb", [P, NTOK], BF16))

            ev5 = {}
            with ExitStack() as c5a:
                wr_sb = c5a.enter_context(nc.sbuf_tensor("wr_sb", [P, HT * 8], BF16))
                at_sb = c5a.enter_context(nc.sbuf_tensor("at_sb", [P, HT * P], BF16))
                rep_sb = c5a.enter_context(nc.sbuf_tensor("rep_sb", [8, P], F32))
                lgT_sb = c5a.enter_context(nc.sbuf_tensor("lgT_sb", [8, NTOK], F32))
                lg_sb = c5a.enter_context(nc.sbuf_tensor("lg_sb", [P, NT * 8], F32))
                m8_sb = c5a.enter_context(nc.sbuf_tensor("m8_sb", [P, NT * 8], F32))
                dcol = c5a.enter_context(nc.sbuf_tensor("dcol", [P, NT], F32))
                w1c = c5a.enter_context(nc.sbuf_tensor("w1c", [P, NT], F32))
                w1s = c5a.enter_context(nc.sbuf_tensor("w1s", [P, NT], F32))
                w2s = c5a.enter_context(nc.sbuf_tensor("w2s", [P, NT], F32))
                wdc = c5a.enter_context(nc.sbuf_tensor("wdc", [P, NT], F32))
                eq1 = c5a.enter_context(nc.sbuf_tensor("eq1", [P, 8], F32))
                eq2 = c5a.enter_context(nc.sbuf_tensor("eq2", [P, 8], F32))
                t2s = c5a.enter_context(nc.sbuf_tensor("t2s", [P, 8], F32))
                comb = c5a.enter_context(nc.sbuf_tensor("comb", [P, NT * 8], F32))
                ctT_sb = c5a.enter_context(nc.sbuf_tensor("ctT_sb", [8, NTOK], F32))
                hT_sb = c5a.enter_context(nc.sbuf_tensor("hT_sb", [P, NTOK], F32))
                ps8 = c5a.enter_context(nc.psum_tensor("ps8", [8, NTOK], F32))
                ps_big = c5a.enter_context(nc.psum_tensor("ps_big", [P, NTOK], F32))

                with nc.Block() as blk:
                    ev_x2l = [None] * HT

                    @blk.sync
                    def _(sync):
                        sync.wait_ge(s_spl2.h, s_spl2.v)
                        for ht in range(HT):
                            ev_x2l[ht] = s_hld.inc(sync.dma_start(
                                out=xlnT2[:, ht * NTOK:(ht + 1) * NTOK],
                                in_=x2lnT_sp[ht, :, :]), 16)
                        ev5["bs"] = s_hld.inc(sync.dma_start(out=bs_sb[:, :], in_=bsbi[:, :]), 16)
                        ev5["rep"] = s_hld.inc(sync.dma_start(out=rep_sb[:, :], in_=repmi[:, :]), 16)

                    @blk.gpsimd
                    def _(g):
                        for ev in p4_finals:
                            g.wait_ge(*ev)
                        for q in range(4):
                            src = wrT.rearrange("(ht p) e -> p ht e", p=P)[:, 8 * q:8 * (q + 1), :]
                            dst = wr_sb[:, :].rearrange("p (ht e) -> p ht e", e=8)[
                                :, 8 * q:8 * (q + 1), :]
                            ev5["wr"] = s_wld.inc(g.dma_start(out=dst, in_=src), 16)
                        for q in range(4):
                            src = atT.rearrange("(ht p) e -> p ht e", p=P)[:, 8 * q:8 * (q + 1), :]
                            dst = at_sb[:, :].rearrange("p (ht e) -> p ht e", e=P)[
                                :, 8 * q:8 * (q + 1), :]
                            ev5["at"] = s_wld.inc(g.dma_start(out=dst, in_=src), 16)
                        ev5["bt"] = s_wld.inc(g.dma_start(out=bt_sb[:, :], in_=btT[:, :]), 16)

                    @blk.tensor
                    def _(t):
                        wt(t, ev5["wr"])
                        for ht in range(HT):
                            wt(t, ev_x2l[ht])
                            for c in range(4):
                                instr = t.matmul(ps8[:, c * 512:(c + 1) * 512],
                                                 wr_sb[:, ht * 8:(ht + 1) * 8],
                                                 xlnT2[:, ht * NTOK + c * 512: ht * NTOK + (c + 1) * 512],
                                                 start=(ht == 0), stop=(ht == HT - 1))
                        ev5["lgT_mm"] = s_mm.inc(instr)

                    @blk.vector
                    def _(v):
                        wt(v, ev5["lgT_mm"])
                        ev5["lgT_ev"] = s_ve.inc(v.tensor_copy(lgT_sb[:, :], ps8[:, :]))

                    @blk.tensor
                    def _(t):
                        wt(t, ev5["lgT_ev"])
                        for nt in range(NT):
                            instr = t.transpose(ps_big[:, nt * 8:(nt + 1) * 8],
                                                lgT_sb[:, nt * P:(nt + 1) * P],
                                                ident_f[0:8, 0:8])
                        ev5["lg_tr"] = s_tr.inc(instr)

                    @blk.vector
                    def _(v):
                        wt(v, ev5["lg_tr"])
                        ev5["lg_ev"] = s_ve.inc(v.tensor_copy(lg_sb[:, :], ps_big[:, 0:NT * 8]))

                    @blk.tensor
                    def _(t):
                        wt(t, ev5["lg_ev"])
                        wt(t, ev5["at"])
                        for ht in range(HT):
                            for c in range(4):
                                instr = t.matmul(ps_big[:, c * 512:(c + 1) * 512],
                                                 at_sb[:, ht * P:(ht + 1) * P],
                                                 xlnT2[:, ht * NTOK + c * 512: ht * NTOK + (c + 1) * 512],
                                                 start=(ht == 0), stop=(ht == HT - 1))
                        ev5["hT_mm"] = s_mm.inc(instr)

                    @blk.vector
                    def _(v):
                        wt(v, ev5["hT_mm"])
                        ev5["hT_ev"] = s_ve.inc(v.tensor_copy(hT_sb[:, :], ps_big[:, :]))
                        wt(v, ev5["lg_ev"])
                        for nt in range(NT):
                            chain(v, s_ve2, v.max(out=m8_sb[:, nt * 8:(nt + 1) * 8],
                                                  in_=lg_sb[:, nt * 8:(nt + 1) * 8]))
                        l1v = m8_sb[:, :].rearrange("p (nt e) -> p nt e", e=8)[:, :, 0:1]
                        l2v = m8_sb[:, :].rearrange("p (nt e) -> p nt e", e=8)[:, :, 1:2]
                        ev5["dcol"] = chain(v, s_ve2, v.tensor_tensor(
                            out=dcol[:, :].rearrange("p (nt o) -> p nt o", o=1),
                            in0=l1v, in1=l2v, op=ALU.subtract))

                    @blk.scalar
                    def _(sc):
                        wt(sc, ev5["dcol"])
                        ev5["w1"] = s_sc.inc(sc.activation(out=w1c[:, :], in_=dcol[:, :],
                                                           func=AF.Sigmoid))

                    @blk.vector
                    def _(v):
                        wt(v, ev5["w1"])
                        chain(v, s_ve2, v.tensor_scalar(out=w1s[:, :], in0=w1c[:, :],
                                                        scalar1=2.0, scalar2=None, op0=ALU.mult))
                        chain(v, s_ve2, v.tensor_scalar(out=w2s[:, :], in0=w1c[:, :],
                                                        scalar1=-2.0, scalar2=2.0,
                                                        op0=ALU.mult, op1=ALU.add))
                        chain(v, s_ve2, v.tensor_tensor(out=wdc[:, :], in0=w1s[:, :],
                                                        in1=w2s[:, :], op=ALU.subtract))
                        for nt in range(NT):
                            lg_sl = lg_sb[:, nt * 8:(nt + 1) * 8]
                            chain(v, s_ve2, v.tensor_scalar(out=eq1[:, :], in0=lg_sl,
                                                            scalar1=m8_sb[:, nt * 8:nt * 8 + 1],
                                                            scalar2=None, op0=ALU.is_equal))
                            chain(v, s_ve2, v.tensor_scalar(out=eq2[:, :], in0=lg_sl,
                                                            scalar1=m8_sb[:, nt * 8 + 1:nt * 8 + 2],
                                                            scalar2=None, op0=ALU.is_equal))
                            chain(v, s_ve2, v.tensor_scalar(out=t2s[:, :], in0=eq2[:, :],
                                                            scalar1=w2s[:, nt:nt + 1],
                                                            scalar2=None, op0=ALU.mult))
                            chain(v, s_ve2, v.scalar_tensor_tensor(
                                out=comb[:, nt * 8:(nt + 1) * 8], in0=eq1[:, :],
                                scalar=w1s[:, nt:nt + 1], in1=t2s[:, :],
                                op0=ALU.mult, op1=ALU.add))
                        ev5["comb"] = chain(v, s_ve2, v.tensor_copy(comb[:, 0:1], comb[:, 0:1]))

                    @blk.tensor
                    def _(t):
                        wt(t, ev5["comb"])
                        for nt in range(NT):
                            instr = t.transpose(ps8[:, nt * P:(nt + 1) * P],
                                                comb[:, nt * 8:(nt + 1) * 8], ident_f[:, :])
                        ev5["ct_tr"] = s_tr.inc(instr)

                    @blk.vector
                    def _(v):
                        wt(v, ev5["ct_tr"])
                        ev5["ct_ev"] = s_ve.inc(v.tensor_copy(ctT_sb[:, :], ps8[:, :]))

                    @blk.tensor
                    def _(t):
                        wt(t, ev5["ct_ev"])
                        wt(t, ev5["hT_ev"])
                        wt(t, ev5["rep"])
                        for c in range(4):
                            instr = t.matmul(ps_big[:, c * 512:(c + 1) * 512], rep_sb[:, :],
                                             ctT_sb[:, c * 512:(c + 1) * 512],
                                             start=True, stop=True)
                        ev5["crep"] = s_tr.inc(instr)

                    @blk.vector
                    def _(v):
                        wt(v, ev5["crep"])
                        ev5["hw"] = chain(v, s_ve2, v.tensor_tensor(
                            out=hw_sb[:, :], in0=hT_sb[:, :], in1=ps_big[:, :], op=ALU.mult))

            with ExitStack() as c5b:
                wbr = c5b.enter_context(nc.sbuf_tensor("wbr", [P, 8 * 8 * P], BF16))
                y2b = [c5b.enter_context(nc.sbuf_tensor(f"y2b{j}", [P, 1024], F32)) for j in range(2)]
                o_st = [c5b.enter_context(nc.sbuf_tensor(f"o_st{j}", [P, 1024], F32)) for j in range(2)]
                ps_y2 = [c5b.enter_context(nc.psum_tensor(f"ps_y2{j}", [P, 1024], F32)) for j in range(2)]
                ps_trO = [c5b.enter_context(nc.psum_tensor(f"ps_trO{j}", [P, 1024], F32)) for j in range(2)]

                with nc.Block() as blk:
                    ev_ring5 = {}
                    ev_mm5 = {}
                    ev_y2e = {}
                    ev_tro = {}
                    ev_oev = {}
                    ev_oacc = {}
                    ev_out_all = (s_spl.h, s_spl.v)

                    for ho in range(HT):
                        @blk.gpsimd
                        def _(g, ho=ho):
                            if ho == 0:
                                g.wait_ge(*ev_out_all)
                            for q in range(4):
                                gi = ho * 4 + q
                                if gi >= 8:
                                    pho, pq = divmod(gi - 8, 4)
                                    wt(g, ev_mm5[(pho, 1)])
                                src = wbT.rearrange("(hi p) (ho h) -> p hi ho h", p=P, h=P)[
                                    :, 8 * q:8 * (q + 1), ho, :]
                                dst = wbr[:, :].rearrange("p (r i h) -> p r i h", r=8, h=P)[
                                    :, gi % 8, :, :]
                                ev_ring5[gi] = s_wld.inc(g.dma_start(out=dst, in_=src), 16)

                        for nh in range(2):
                            @blk.tensor
                            def _(t, ho=ho, nh=nh):
                                k = ho * 2 + nh
                                if k >= 2:
                                    wt(t, ev_y2e[k - 2])
                                if nh == 0:
                                    wt(t, ev_ring5[ho * 4 + 3])
                                if k == 0:
                                    wt(t, ev5["hw"])
                                for hi in range(HT):
                                    for c in range(2):
                                        instr = t.matmul(
                                            ps_y2[nh][:, c * 512:(c + 1) * 512],
                                            wbr[:, :].rearrange("p (r i h) -> p r i h", r=8, h=P)[
                                                :, (ho * 4 + hi // 8) % 8, hi % 8, :],
                                            xlnT2[:, hi * NTOK + nh * 1024 + c * 512:
                                                  hi * NTOK + nh * 1024 + (c + 1) * 512],
                                            start=(hi == 0), stop=False)
                                for c in range(2):
                                    instr = t.matmul(
                                        ps_y2[nh][:, c * 512:(c + 1) * 512],
                                        bt_sb[:, ho * P:(ho + 1) * P],
                                        hw_sb[:, nh * 1024 + c * 512: nh * 1024 + (c + 1) * 512],
                                        start=False, stop=True)
                                ev_mm5[(ho, nh)] = s_mm.inc(instr)

                            @blk.scalar
                            def _(sc, ho=ho, nh=nh):
                                k = ho * 2 + nh
                                wt(sc, ev_mm5[(ho, nh)])
                                if k >= 2:
                                    wt(sc, ev_tro[k - 2])
                                ev_y2e[k] = s_sc.inc(sc.activation(
                                    out=y2b[nh][:, :], in_=ps_y2[nh][:, :], func=AF.Identity,
                                    bias=bs_sb[:, ho:ho + 1]))

                            @blk.tensor
                            def _(t, ho=ho, nh=nh):
                                k = ho * 2 + nh
                                wt(t, ev_y2e[k])
                                if k >= 2:
                                    wt(t, ev_oev[k - 2])
                                for c in range(8):
                                    instr = t.transpose(ps_trO[nh][:, c * P:(c + 1) * P],
                                                        y2b[nh][:, c * P:(c + 1) * P],
                                                        ident_f[:, :])
                                ev_tro[k] = s_tr.inc(instr)

                            @blk.vector
                            def _(v, ho=ho, nh=nh):
                                k = ho * 2 + nh
                                wt(v, ev_tro[k])
                                if k >= 2:
                                    wt(v, ev_oacc[k - 2])
                                ev_oev[k] = s_ve.inc(v.tensor_copy(o_st[nh][:, :], ps_trO[nh][:, :]))

                            @blk.gpsimd
                            def _(g, ho=ho, nh=nh):
                                k = ho * 2 + nh
                                wt(g, ev_oev[k])
                                src = o_st[nh][:, :].rearrange("p (nt n) -> p nt n", n=P)
                                dst = out.rearrange("(nt p) (ho h) -> p nt ho h", p=P, h=P)[
                                    :, nh * 8:(nh + 1) * 8, ho, :]
                                ev_oacc[k] = s_acc.inc(g.dma_start(out=dst, in_=src,
                                                                   accum_op=ALU.add), 16)

        with nc.Block() as blk:
            @blk.gpsimd
            def _(g):
                g.wait_ge(s_acc.h, s_acc.v)

            @blk.sync
            def _(sync):
                sync.wait_ge(s_spl.h, s_spl.v)
                sync.wait_ge(s_spl2.h, s_spl2.v)

    return nc


def _finish_partial(nc, S, out):
    with nc.Block() as blk:
        @blk.gpsimd
        def _(g):
            g.wait_ge(S["wld"].h, S["wld"].v)
            g.wait_ge(S["pc"].h, S["pc"].v)
            g.wait_ge(S["acc"].h, S["acc"].v)

        @blk.sync
        def _(sync):
            sync.wait_ge(S["hld"].h, S["hld"].v)
            sync.wait_ge(S["spl"].h, S["spl"].v)
            sync.wait_ge(S["spl2"].h, S["spl2"].v)


def _prep(inputs):
    f32 = np.float32
    x = np.ascontiguousarray(np.asarray(inputs["L0_attn_x"], dtype=f32).reshape(NTOK * 8, D))
    wgT_ = np.ascontiguousarray(np.asarray(inputs["L0_attn_w_gate"], dtype=f32).T)
    wuT_ = np.ascontiguousarray(np.asarray(inputs["L0_attn_w_up"], dtype=f32).T)
    wdT_ = np.ascontiguousarray(np.asarray(inputs["L0_attn_w_down"], dtype=f32).T)
    wbT_ = np.ascontiguousarray(np.asarray(inputs["L0_ffn_W_base"], dtype=f32).T)
    wrT_ = np.ascontiguousarray(np.asarray(inputs["L0_ffn_W_router"], dtype=f32).T)
    A = np.asarray(inputs["L0_ffn_A"], dtype=f32)
    B = np.asarray(inputs["L0_ffn_B"], dtype=f32)
    atT_ = np.ascontiguousarray(np.transpose(A, (2, 0, 1)).reshape(D, P))
    btT_ = np.ascontiguousarray(np.transpose(B, (0, 2, 1)).reshape(P, D))
    b_base = np.asarray(inputs["L0_ffn_b_base"], dtype=f32)
    bsb_ = np.ascontiguousarray(b_base.reshape(HT, P).T)
    idbf_ = np.eye(P, dtype=ml_dtypes.bfloat16)
    idf_ = np.eye(P, dtype=f32)
    repm_ = np.zeros((8, P), dtype=f32)
    for e in range(8):
        repm_[e, e * 16:(e + 1) * 16] = 1.0
    shared = {"wgT": wgT_, "wuT": wuT_, "wdT": wdT_, "wbT": wbT_, "wrT": wrT_,
              "atT": atT_, "btT": btT_, "bsbi": bsb_, "idbf": idbf_, "idf": idf_,
              "repmi": repm_}
    in_maps = []
    for c in range(8):
        m = dict(shared)
        m["x"] = x[c * NTOK:(c + 1) * NTOK]
        in_maps.append(m)
    return in_maps


_LAST_RESULTS = None


def kernel(**inputs):
    global _LAST_RESULTS
    in_maps = _prep(inputs)
    nc = build_program(phases=_PHASES, dbg=_DBG)
    res = run_bass_kernel_spmd(nc, in_maps, core_ids=list(range(8)))
    _LAST_RESULTS = res
    outs = [r["out"] for r in res.results]
    return np.concatenate(outs, axis=0).reshape(4, 4096, D)


# version 05
# speedup vs baseline: 1.0470x; 1.0470x over previous
import numpy as np
import ml_dtypes
from contextlib import ExitStack

import concourse.bass as bass
import concourse.mybir as mybir
from concourse.bass_utils import run_bass_kernel_spmd

F32 = mybir.dt.float32
BF16 = mybir.dt.bfloat16
AF = mybir.ActivationFunctionType
ALU = mybir.AluOpType
AX = mybir.AxisListType

P = 128
NTOK = 2048
NT = NTOK // P
D = 4096
HT = D // P
I = 11008
IT = I // P
NHALF = I // 2
NQ = I // 4
KKEEP = float(I // 2)
EPS = 1e-5
BIS_ITERS = 14
HI0 = 8.0
SNAP = 1.0 + 2.0 ** -9
NB = 4
NBTOK = NTOK // NB
NGR = (IT + 7) // 8
NCH = 4 * IT
TR_TILE = 76

_PHASES = 6
_DBG = False


class Sem:
    def __init__(self, h):
        self.h = h
        self.v = 0

    def inc(self, instr, n=1):
        instr.then_inc(self.h, n)
        self.v += n
        return (self.h, self.v)


def chain(eng, sem, instr):
    ev = sem.inc(instr, 1)
    eng.wait_ge(*ev)
    return ev


def wt(eng, ev):
    if ev is not None:
        eng.wait_ge(*ev)


def build_program(phases=6, dbg=False):
    nc = bass.Bass("TRN2", target_bir_lowering=False)

    x_in = nc.dram_tensor("x", [NTOK, D], F32, kind="ExternalInput").ap()
    wgT = nc.dram_tensor("wgT", [D, I], F32, kind="ExternalInput").ap()
    wuT = nc.dram_tensor("wuT", [D, I], F32, kind="ExternalInput").ap()
    wdT = nc.dram_tensor("wdT", [I, D], F32, kind="ExternalInput").ap()
    wbT = nc.dram_tensor("wbT", [D, D], F32, kind="ExternalInput").ap()
    wrT = nc.dram_tensor("wrT", [D, 8], F32, kind="ExternalInput").ap()
    atT = nc.dram_tensor("atT", [D, P], F32, kind="ExternalInput").ap()
    btT = nc.dram_tensor("btT", [P, D], F32, kind="ExternalInput").ap()
    bsbi = nc.dram_tensor("bsbi", [P, HT], F32, kind="ExternalInput").ap()
    idbf = nc.dram_tensor("idbf", [P, P], BF16, kind="ExternalInput").ap()
    idf = nc.dram_tensor("idf", [P, P], F32, kind="ExternalInput").ap()
    repmi = nc.dram_tensor("repmi", [8, P], F32, kind="ExternalInput").ap()
    out = nc.dram_tensor("out", [NTOK, D], F32, kind="ExternalOutput").ap()

    sk = "ExternalOutput" if dbg else "Internal"
    sT_sp = nc.dram_tensor("sT_sp", [IT, P, NTOK], BF16, kind=sk).ap()
    sabs_sp = nc.dram_tensor("sabs_sp", [NT, P, I], BF16, kind=sk).ap()
    uT_sp = nc.dram_tensor("uT_sp", [IT, P, NTOK], BF16, kind=sk).ap()
    x2lnT_sp = nc.dram_tensor("x2lnT_sp", [HT, P, NTOK], BF16, kind=sk).ap()
    wd_bf = nc.dram_tensor("wd_bf", [HT, IT, P, P], BF16, kind="Internal").ap()
    tflat_sp = nc.dram_tensor("tflat_sp", [NTOK], F32, kind="Internal").ap()

    with ExitStack() as ctx:
        ident_bf = ctx.enter_context(nc.sbuf_tensor("ident_bf", [P, P], BF16))
        ident_f = ctx.enter_context(nc.sbuf_tensor("ident_f", [P, P], F32))
        t_all = ctx.enter_context(nc.sbuf_tensor("t_all", [P, NT], F32))
        ones1 = ctx.enter_context(nc.sbuf_tensor("ones1", [1, P], F32))
        cst = ctx.enter_context(nc.sbuf_tensor("cst", [P, 2], F32))

        S = {}
        for name in ["hld", "spl", "spl2", "wld", "pc", "mm", "tr", "sc", "ve",
                     "sc2", "ve2", "acc", "uesc", "ueve", "bsc", "bve", "uspl",
                     "pcg", "xcp", "trl"]:
            S[name] = Sem(ctx.enter_context(nc.semaphore("s_" + name)))
        s_hld, s_spl, s_spl2, s_wld, s_pc = S["hld"], S["spl"], S["spl2"], S["wld"], S["pc"]
        s_mm, s_tr, s_sc, s_ve = S["mm"], S["tr"], S["sc"], S["ve"]
        s_sc2, s_ve2, s_acc = S["sc2"], S["ve2"], S["acc"]
        s_uesc, s_ueve = S["uesc"], S["ueve"]
        s_bsc, s_bve = S["bsc"], S["bve"]
        s_uspl = S["uspl"]

        sc_tiles = [q for q in range(IT) if q % 2 == 0]
        ve_tiles = [q for q in range(IT) if q % 2 == 1]

        def ue_ev(ti):
            if ti % 2 == 0:
                return (s_uesc.h, sc_tiles.index(ti) + 1)
            return (s_ueve.h, ve_tiles.index(ti) + 1)

        def emit_ln(blk, x_ap, xsq_ap, st, out_bf_ap, ev_in_ready, ev_extra_sc=None):
            evs = {}

            @blk.scalar
            def _(sc):
                wt(sc, ev_in_ready)
                if ev_extra_sc is not None:
                    wt(sc, ev_extra_sc)
                evs["sq"] = s_sc.inc(sc.activation(out=xsq_ap, in_=x_ap, func=AF.Square,
                                                   accum_out=st[:, 2:3]))

            @blk.vector
            def _(v):
                wt(v, ev_in_ready)
                chain(v, s_ve2, v.reduce_sum(st[:, 0:1], x_ap, axis=AX.X))
                chain(v, s_ve2, v.tensor_scalar(out=st[:, 3:4], in0=st[:, 0:1],
                                                scalar1=1.0 / D, scalar2=None, op0=ALU.mult))
                wt(v, evs["sq"])
                chain(v, s_ve2, v.tensor_scalar(out=st[:, 4:5], in0=st[:, 2:3],
                                                scalar1=1.0 / D, scalar2=None, op0=ALU.mult))
                chain(v, s_ve2, v.tensor_tensor(out=st[:, 1:2], in0=st[:, 3:4],
                                                in1=st[:, 3:4], op=ALU.mult))
                chain(v, s_ve2, v.tensor_tensor(out=st[:, 5:6], in0=st[:, 4:5],
                                                in1=st[:, 1:2], op=ALU.subtract))
                evs["var"] = chain(v, s_ve2, v.tensor_scalar(out=st[:, 5:6], in0=st[:, 5:6],
                                                             scalar1=EPS, scalar2=None, op0=ALU.add))

            @blk.scalar
            def _(sc):
                wt(sc, evs["var"])
                evs["sqrt"] = s_sc.inc(sc.activation(out=st[:, 6:7], in_=st[:, 5:6], func=AF.Sqrt))

            @blk.vector
            def _(v):
                wt(v, evs["sqrt"])
                chain(v, s_ve2, v.reciprocal(st[:, 6:7], st[:, 6:7]))
                evs["rb"] = chain(v, s_ve2, v.scalar_tensor_tensor(
                    out=st[:, 7:8], in0=st[:, 3:4], scalar=-1.0, in1=st[:, 6:7],
                    op0=ALU.mult, op1=ALU.mult))

            @blk.scalar
            def _(sc):
                wt(sc, evs["rb"])
                evs["norm"] = s_sc.inc(sc.activation(out=out_bf_ap, in_=x_ap, func=AF.Identity,
                                                     scale=st[:, 6:7], bias=st[:, 7:8]))

            return evs["norm"], evs["rb"]

        tile_mm_last = {}

        with ExitStack() as ctx03:
            xlnT = ctx03.enter_context(nc.sbuf_tensor("xlnT", [P, HT * NTOK], BF16))
            w_sb = [ctx03.enter_context(nc.sbuf_tensor(f"w_sb{j}", [P, HT * P], BF16))
                    for j in range(2)]

            def xlnT_sl(ht, c0, c1):
                return xlnT[:, ht * NTOK + c0: ht * NTOK + c1]

            with ExitStack() as c0:
                x_st = [c0.enter_context(nc.sbuf_tensor(f"x_st{j}", [P, D], F32)) for j in range(2)]
                st0 = c0.enter_context(nc.sbuf_tensor("st0", [P, 8], F32))
                xln_st = [c0.enter_context(nc.sbuf_tensor(f"xln_st{j}", [P, D], BF16)) for j in range(2)]
                ps_trX = [c0.enter_context(nc.psum_tensor(f"ps_trX{j}", [P, 1024], BF16)) for j in range(2)]

                with nc.Block() as blk:
                    @blk.sync
                    def _(sync):
                        sync.dma_start(out=ident_bf[:, :], in_=idbf[:, :]).then_inc(s_hld.h, 16)
                        sync.dma_start(out=ident_f[:, :], in_=idf[:, :]).then_inc(s_hld.h, 16)
                    s_hld.v += 32

                    @blk.gpsimd
                    def _(g):
                        g.memset(ones1[:, :], 1.0)
                        g.memset(cst[:, 0:1], 1.0)
                        g.memset(cst[:, 1:2], HI0)
                        for nt in range(NT):
                            S["xcp"].inc(g.dma_start(
                                out=out[nt * P:(nt + 1) * P, :],
                                in_=x_in[nt * P:(nt + 1) * P, :]), 16)

                    ev_norm = [None] * NT
                    ev_ldx = [None] * NT
                    ev_tr = {}
                    ev_evac = {}
                    for nt in range(NT):
                        @blk.sync
                        def _(sync, nt=nt):
                            if nt >= 2:
                                wt(sync, ev_norm[nt - 2])
                            ev_ldx[nt] = s_hld.inc(sync.dma_start(
                                out=x_st[nt % 2][:, :], in_=x_in[nt * P:(nt + 1) * P, :]), 16)

                        ev_norm[nt], _rb = emit_ln(
                            blk, x_st[nt % 2][:, :], xln_st[nt % 2][:, :], st0,
                            xln_st[nt % 2][:, :], ev_ldx[nt],
                            ev_extra_sc=(ev_tr[4 * (nt - 2) + 3] if nt >= 2 else None))

                        for g2 in range(4):
                            @blk.tensor
                            def _(t, nt=nt, g2=g2):
                                k = 4 * nt + g2
                                if g2 == 0:
                                    wt(t, ev_norm[nt])
                                if k >= 2:
                                    wt(t, ev_evac[k - 2])
                                for h in range(8):
                                    ht = 8 * g2 + h
                                    instr = t.transpose(ps_trX[g2 % 2][:, h * P:(h + 1) * P],
                                                        xln_st[nt % 2][:, ht * P:(ht + 1) * P],
                                                        ident_bf[:, :])
                                ev_tr[k] = s_tr.inc(instr)

                            @blk.vector
                            def _(v, nt=nt, g2=g2):
                                k = 4 * nt + g2
                                wt(v, ev_tr[k])
                                dst = xlnT[:, :].rearrange("p (ht n) -> p ht n", n=NTOK)[
                                    :, 8 * g2:8 * (g2 + 1), nt * P:(nt + 1) * P]
                                src = ps_trX[g2 % 2][:, :].rearrange("p (h n) -> p h n", n=P)
                                ev_evac[k] = s_ve.inc(v.tensor_copy(dst, src))

            ev_p0_done = (s_ve.h, s_ve.v)

            if phases < 1:
                _finish_partial(nc, S, out)
                return nc

            pc_jobs = []
            for ht in range(HT):
                for it0 in range(0, IT, 8):
                    pc_jobs.append((ht, it0, min(8, IT - it0)))

            with ExitStack() as c1:
                sT_st = [c1.enter_context(nc.sbuf_tensor(f"sT_st{j}", [P, NTOK], BF16))
                         for j in range(2)]
                ab_st = [c1.enter_context(nc.sbuf_tensor(f"ab_st{j}", [P, NTOK], BF16))
                         for j in range(2)]
                ps_g = [c1.enter_context(nc.psum_tensor(f"ps_g{j}", [P, 512], F32))
                        for j in range(4)]
                ps_trS = [c1.enter_context(nc.psum_tensor(f"ps_trS{j}", [P, 512], BF16))
                          for j in range(4)]

                with nc.Block() as blk:
                    ev_wld1 = {}
                    ev_mmc = {}
                    ev_silu = {}
                    ev_trc = {}
                    ev_absc = {}
                    ev_spl_sT = {}
                    ev_spl_ab = {}
                    pci = [0]

                    def emit_tr1(t, j):
                        ti, c = divmod(j, 4)
                        wt(t, ev_silu[j])
                        if j >= 4:
                            wt(t, ev_absc[j - 4])
                        for h in range(4):
                            tok = c * 4 + h
                            instr = t.transpose(ps_trS[j % 4][:, h * P:(h + 1) * P],
                                                sT_st[ti % 2][:, tok * P:(tok + 1) * P],
                                                ident_bf[:, :])
                        ev_trc[j] = s_tr.inc(instr)

                    def emit_abs1(sc, j):
                        ti, c = divmod(j, 4)
                        wt(sc, ev_trc[j])
                        if ti >= 2 and c == 0:
                            wt(sc, ev_spl_ab[ti - 2])
                        ev_absc[j] = s_sc.inc(sc.activation(
                            out=ab_st[ti % 2][:, c * 512:(c + 1) * 512],
                            in_=ps_trS[j % 4][:, :], func=AF.Abs))

                    for i in range(IT):
                        @blk.gpsimd
                        def _(g, i=i):
                            if i >= 2:
                                wt(g, tile_mm_last[i - 2])
                            for q in range(4):
                                src = wgT.rearrange("(ht p) (it i) -> p ht it i", p=P, i=P)[
                                    :, 8 * q:8 * (q + 1), i, :]
                                dst = w_sb[i % 2][:, :].rearrange("p (ht i) -> p ht i", i=P)[
                                    :, 8 * q:8 * (q + 1), :]
                                ev_wld1[i] = s_wld.inc(g.dma_start(out=dst, in_=src), 16)
                            n_pc = 5 if i < 8 else 4
                            for _k in range(n_pc):
                                if pci[0] < len(pc_jobs):
                                    htj, it0, n_it = pc_jobs[pci[0]]
                                    src = wdT.rearrange("(it p) (ht h) -> p it ht h", p=P, h=P)[
                                        :, it0:it0 + n_it, htj, :]
                                    dst = wd_bf[htj, it0:it0 + n_it, :, :].rearrange("it p h -> p it h")
                                    s_pc.inc(g.dma_start(out=dst, in_=src), 16)
                                    pci[0] += 1

                        @blk.tensor
                        def _(t, i=i):
                            for c in range(4):
                                j = 4 * i + c
                                if j == 0:
                                    wt(t, ev_p0_done)
                                if c == 0:
                                    wt(t, ev_wld1[i])
                                if j >= 4:
                                    wt(t, ev_silu[j - 4])
                                for ht in range(HT):
                                    instr = t.matmul(ps_g[j % 4][:, :],
                                                     w_sb[i % 2][:, ht * P:(ht + 1) * P],
                                                     xlnT_sl(ht, c * 512, (c + 1) * 512),
                                                     start=(ht == 0), stop=(ht == HT - 1))
                                ev_mmc[j] = s_mm.inc(instr)
                                if i >= 1:
                                    emit_tr1(t, 4 * (i - 1) + c)
                            tile_mm_last[i] = ev_mmc[4 * i + 3]

                        @blk.scalar
                        def _(sc, i=i):
                            for c in range(4):
                                j = 4 * i + c
                                wt(sc, ev_mmc[j])
                                if i >= 2 and c == 0:
                                    wt(sc, ev_spl_sT[i - 2])
                                if j >= 8:
                                    wt(sc, ev_trc[j - 8])
                                ev_silu[j] = s_sc.inc(sc.activation(
                                    out=sT_st[i % 2][:, c * 512:(c + 1) * 512],
                                    in_=ps_g[j % 4][:, :], func=AF.Silu))
                                if i >= 1:
                                    emit_abs1(sc, 4 * (i - 1) + c)

                        def emit_ab_spill(sync, i):
                            wt(sync, ev_absc[4 * i + 3])
                            for half in range(2):
                                src = ab_st[i % 2][:, :].rearrange("p (nt n) -> p nt n", n=P)[
                                    :, 8 * half:8 * (half + 1), :]
                                dst = sabs_sp.rearrange("nt p (it i) -> p nt it i", i=P)[
                                    :, 8 * half:8 * (half + 1), i, :]
                                ev_spl_ab[i] = s_spl2.inc(sync.dma_start(out=dst, in_=src), 16)

                        @blk.sync
                        def _(sync, i=i, emit_ab_spill=emit_ab_spill):
                            wt(sync, ev_silu[4 * i + 3])
                            ev_spl_sT[i] = s_spl.inc(sync.dma_start(
                                out=sT_sp[i, :, :], in_=sT_st[i % 2][:, :]), 16)
                            if i >= 1:
                                emit_ab_spill(sync, i - 1)

                    @blk.tensor
                    def _(t):
                        for c in range(4):
                            emit_tr1(t, 4 * (IT - 1) + c)

                    @blk.scalar
                    def _(sc):
                        for c in range(4):
                            emit_abs1(sc, 4 * (IT - 1) + c)

                    @blk.sync
                    def _(sync):
                        emit_ab_spill(sync, IT - 1)

            p1_sc_final = (s_sc.h, s_sc.v)
            p1_spl_final = (s_spl.h, s_spl.v)
            p1_spl2_final = (s_spl2.h, s_spl2.v)

            if phases < 2:
                _finish_partial(nc, S, out)
                return nc

            with ExitStack() as c2:
                vbuf = c2.enter_context(nc.sbuf_tensor("vbuf", [P, I], BF16))
                sbuf2 = c2.enter_context(nc.sbuf_tensor("sbuf2", [P, I], BF16))
                dummy = c2.enter_context(nc.sbuf_tensor("bis_dummy", [P, NQ], BF16))
                u_st = [c2.enter_context(nc.sbuf_tensor(f"u_st{j}", [P, NTOK], BF16))
                        for j in range(2)]
                vst = c2.enter_context(nc.sbuf_tensor("vst", [P, 10], F32))
                sst = c2.enter_context(nc.sbuf_tensor("sst", [P, 16], F32))
                vbf = c2.enter_context(nc.sbuf_tensor("vbf", [P, 1], BF16))
                vgt8 = c2.enter_context(nc.sbuf_tensor("vgt8", [P, 1], mybir.dt.uint8))
                vle8 = c2.enter_context(nc.sbuf_tensor("vle8", [P, 1], mybir.dt.uint8))
                sbf1 = c2.enter_context(nc.sbuf_tensor("sbf1", [P, 1], BF16))
                ps_u = [c2.enter_context(nc.psum_tensor(f"ps_u{j}", [P, 2048], F32))
                        for j in range(2)]

                with nc.Block() as blk:
                    ev_wld2 = {}
                    ev_mmu = {}
                    ev_uspl = {}
                    ev_vl = [None] * 8
                    ev_sl = [None] * 8
                    ev_vt = [None] * 8
                    ev_st_ = [None] * 8

                    def emit_uspill(g, ti):
                        g.wait_ge(*ue_ev(ti))
                        ev_uspl[ti] = s_uspl.inc(g.dma_start(
                            out=uT_sp[ti, :, :], in_=u_st[ti % 2][:, :]), 16)

                    for ti in range(IT):
                        gi = IT + ti

                        @blk.gpsimd
                        def _(g, ti=ti, gi=gi):
                            wt(g, tile_mm_last[gi - 2])
                            for q in range(4):
                                src = wuT.rearrange("(ht p) (it i) -> p ht it i", p=P, i=P)[
                                    :, 8 * q:8 * (q + 1), ti, :]
                                dst = w_sb[gi % 2][:, :].rearrange("p (ht i) -> p ht i", i=P)[
                                    :, 8 * q:8 * (q + 1), :]
                                ev_wld2[ti] = s_wld.inc(g.dma_start(out=dst, in_=src), 16)
                            if ti >= 2:
                                emit_uspill(g, ti - 2)

                        @blk.tensor
                        def _(t, ti=ti, gi=gi):
                            for c in range(4):
                                j = 4 * ti + c
                                if j == 0:
                                    wt(t, p1_sc_final)
                                if c == 0:
                                    wt(t, ev_wld2[ti])
                                    if ti >= 2:
                                        t.wait_ge(*ue_ev(ti - 2))
                                for ht in range(HT):
                                    instr = t.matmul(ps_u[ti % 2][:, c * 512:(c + 1) * 512],
                                                     w_sb[gi % 2][:, ht * P:(ht + 1) * P],
                                                     xlnT_sl(ht, c * 512, (c + 1) * 512),
                                                     start=(ht == 0), stop=(ht == HT - 1))
                                ev_mmu[j] = s_mm.inc(instr)
                            tile_mm_last[gi] = ev_mmu[4 * ti + 3]

                    @blk.gpsimd
                    def _(g):
                        emit_uspill(g, IT - 2)
                        emit_uspill(g, IT - 1)

                    sc_ptr = [0]
                    ve_ptr = [0]

                    def due_tile(slot):
                        return slot // 2

                    def v_emit_evacs(v, upto_ti):
                        while ve_ptr[0] < len(ve_tiles) and ve_tiles[ve_ptr[0]] <= upto_ti:
                            ti = ve_tiles[ve_ptr[0]]
                            wt(v, ev_mmu[4 * ti + 3])
                            if ti >= 2:
                                wt(v, ev_uspl[ti - 2])
                            s_ueve.inc(v.tensor_copy(u_st[ti % 2][:, :], ps_u[ti % 2][:, :]))
                            ve_ptr[0] += 1

                    def s_emit_evacs(sc, upto_ti):
                        while sc_ptr[0] < len(sc_tiles) and sc_tiles[sc_ptr[0]] <= upto_ti:
                            ti = sc_tiles[sc_ptr[0]]
                            wt(sc, ev_mmu[4 * ti + 3])
                            if ti >= 2:
                                wt(sc, ev_uspl[ti - 2])
                            s_uesc.inc(sc.activation(
                                out=u_st[ti % 2][:, :], in_=ps_u[ti % 2][:, :],
                                func=AF.Identity))
                            sc_ptr[0] += 1

                    for k in range(8):
                        @blk.sync
                        def _(sync, k=k):
                            if k == 0:
                                sync.wait_ge(*p1_spl2_final)
                                sync.wait_ge(*p1_spl_final)
                            if k >= 1:
                                wt(sync, ev_vt[k - 1])
                            ev_vl[k] = s_hld.inc(sync.dma_start(out=vbuf[:, :],
                                                                in_=sabs_sp[k, :, :]), 16)
                            if k >= 1:
                                wt(sync, ev_st_[k - 1])
                            ev_sl[k] = s_hld.inc(sync.dma_start(out=sbuf2[:, :],
                                                                in_=sabs_sp[8 + k, :, :]), 16)

                        @blk.vector
                        def _(v, k=k):
                            wt(v, ev_vl[k])
                            chain(v, s_ve2, v.memset(vst[:, 0:1], 0.0))
                            chain(v, s_ve2, v.memset(vst[:, 1:2], HI0))
                            for it_ in range(BIS_ITERS):
                                chain(v, s_ve2, v.tensor_scalar(out=vst[:, 2:3], in0=vst[:, 0:1],
                                                                scalar1=vst[:, 1:2], scalar2=0.5,
                                                                op0=ALU.add, op1=ALU.mult))
                                chain(v, s_ve2, v.tensor_copy(vbf[:, :], vst[:, 2:3]))
                                chain(v, s_ve2, v.tensor_copy(vst[:, 3:4], vbf[:, :]))
                                for q in range(4):
                                    acc_col = vst[:, 4:5] if q == 0 else vst[:, 5:6]
                                    chain(v, s_ve2, v.tensor_scalar(
                                        out=dummy[:, :], in0=vbuf[:, q * NQ:(q + 1) * NQ],
                                        scalar1=vst[:, 3:4], scalar2=0.0,
                                        op0=ALU.is_gt, op1=ALU.add, accum_out=acc_col))
                                    if q >= 1:
                                        chain(v, s_ve2, v.tensor_tensor(
                                            out=vst[:, 4:5], in0=vst[:, 4:5],
                                            in1=vst[:, 5:6], op=ALU.add))
                                chain(v, s_ve2, v.tensor_scalar(out=vgt8[:, :], in0=vst[:, 4:5],
                                                                scalar1=KKEEP, scalar2=None,
                                                                op0=ALU.is_gt))
                                chain(v, s_ve2, v.tensor_scalar(out=vle8[:, :], in0=vst[:, 4:5],
                                                                scalar1=KKEEP, scalar2=None,
                                                                op0=ALU.is_le))
                                chain(v, s_ve2, v.select(vst[:, 0:1], vgt8[:, :],
                                                         vst[:, 3:4], vst[:, 0:1]))
                                chain(v, s_ve2, v.select(vst[:, 1:2], vle8[:, :],
                                                         vst[:, 3:4], vst[:, 1:2]))
                                v_emit_evacs(v, due_tile(k * BIS_ITERS + it_))
                            ev_vt[k] = chain(v, s_ve2, v.tensor_copy(t_all[:, k:k + 1], vst[:, 1:2]))
                            s_bve.inc(v.tensor_copy(vst[:, 9:10], vst[:, 1:2]))

                        @blk.scalar
                        def _(sc, k=k):
                            wt(sc, ev_sl[k])
                            chain(sc, s_sc2, sc.memzero(sst[:, 0:1]))
                            chain(sc, s_sc2, sc.activation(out=sst[:, 1:2], in_=sst[:, 0:1],
                                                           func=AF.Identity, scale=0.0,
                                                           bias=cst[:, 1:2]))
                            for it_ in range(BIS_ITERS):
                                chain(sc, s_sc2, sc.activation(out=sst[:, 2:3], in_=sst[:, 1:2],
                                                               func=AF.Identity, scale=0.5))
                                chain(sc, s_sc2, sc.activation(out=sst[:, 3:4], in_=sst[:, 0:1],
                                                               func=AF.Identity, scale=0.5,
                                                               bias=sst[:, 2:3]))
                                chain(sc, s_sc2, sc.activation(out=sbf1[:, :], in_=sst[:, 3:4],
                                                               func=AF.Identity))
                                chain(sc, s_sc2, sc.activation(out=sst[:, 4:5], in_=sbf1[:, :],
                                                               func=AF.Identity))
                                chain(sc, s_sc2, sc.activation(out=sst[:, 5:6], in_=sst[:, 4:5],
                                                               func=AF.Identity, scale=-SNAP))
                                for q in range(4):
                                    chain(sc, s_sc2, sc.activation(
                                        out=dummy[:, :], in_=sbuf2[:, q * NQ:(q + 1) * NQ],
                                        func=AF.Sign, bias=sst[:, 5:6],
                                        accum_out=sst[:, 12 + q:13 + q]))
                                chain(sc, s_sc2, sc.activation(out=sst[:, 6:7], in_=sst[:, 12:13],
                                                               func=AF.Identity, bias=sst[:, 13:14]))
                                chain(sc, s_sc2, sc.activation(out=sst[:, 2:3], in_=sst[:, 14:15],
                                                               func=AF.Identity, bias=sst[:, 15:16]))
                                chain(sc, s_sc2, sc.activation(out=sst[:, 3:4], in_=sst[:, 2:3],
                                                               func=AF.Identity, bias=sst[:, 6:7]))
                                chain(sc, s_sc2, sc.activation(out=sst[:, 7:8], in_=sst[:, 3:4],
                                                               func=AF.Sign))
                                chain(sc, s_sc2, sc.activation(out=sst[:, 8:9], in_=sst[:, 7:8],
                                                               func=AF.Relu))
                                chain(sc, s_sc2, sc.activation(out=sst[:, 9:10], in_=sst[:, 8:9],
                                                               func=AF.Identity, scale=-1.0,
                                                               bias=cst[:, 0:1]))
                                chain(sc, s_sc2, sc.activation(out=sst[:, 10:11], in_=sst[:, 0:1],
                                                               func=AF.Identity, scale=-1.0,
                                                               bias=sst[:, 4:5]))
                                chain(sc, s_sc2, sc.activation(out=sst[:, 11:12], in_=sst[:, 1:2],
                                                               func=AF.Identity, scale=-1.0,
                                                               bias=sst[:, 4:5]))
                                chain(sc, s_sc2, sc.activation(out=sst[:, 0:1], in_=sst[:, 8:9],
                                                               func=AF.Identity, scale=sst[:, 10:11],
                                                               bias=sst[:, 0:1]))
                                chain(sc, s_sc2, sc.activation(out=sst[:, 1:2], in_=sst[:, 9:10],
                                                               func=AF.Identity, scale=sst[:, 11:12],
                                                               bias=sst[:, 1:2]))
                                s_emit_evacs(sc, due_tile(k * BIS_ITERS + it_))
                            ev_st_[k] = chain(sc, s_sc2, sc.activation(
                                out=t_all[:, 8 + k:9 + k], in_=sst[:, 0:1],
                                func=AF.Identity, scale=SNAP))
                            s_bsc.inc(sc.activation(out=sbf1[:, :], in_=sst[:, 1:2],
                                                    func=AF.Identity))

                    @blk.vector
                    def _(v):
                        v_emit_evacs(v, IT - 1)

                    @blk.scalar
                    def _(sc):
                        s_emit_evacs(sc, IT - 1)

                    @blk.sync
                    def _(sync):
                        sync.wait_ge(s_bve.h, 8)
                        sync.wait_ge(s_bsc.h, 8)
                        sync.dma_start(out=tflat_sp.rearrange("(k p) -> p k", p=P),
                                       in_=t_all[:, :],
                                       allow_slow_non_contiguous=True).then_inc(s_spl.h, 16)
                        s_spl.v += 16

            p23_mm_final = (s_mm.h, s_mm.v)
            p23_sc2_final = (s_sc2.h, s_sc2.v)
            p23_ve2_final = (s_ve2.h, s_ve2.v)
            p23_spl_final = (s_spl.h, s_spl.v)
            p23_uspl_final = (s_uspl.h, s_uspl.v)
            ue_sc_total = (s_uesc.h, len(sc_tiles))
            ue_ve_total = (s_ueve.h, len(ve_tiles))

            if phases < 3:
                _finish_partial(nc, S, out)
                return nc

        with ExitStack() as c4:
            gcol = c4.enter_context(nc.sbuf_tensor("gcol", [P, IT * 512], BF16))
            s_stage = c4.enter_context(nc.sbuf_tensor("s_stage", [P, 8 * 512], BF16))
            abs_st4 = c4.enter_context(nc.sbuf_tensor("abs_st4", [P, 8 * 512], BF16))
            t_rep = c4.enter_context(nc.sbuf_tensor("t_rep", [P, NTOK], F32))
            tfl = t_rep[0:1, :]
            wdr = c4.enter_context(nc.sbuf_tensor("wdr", [P, 4 * 8 * P], BF16))
            yT_sb = [c4.enter_context(nc.sbuf_tensor(f"yT_sb{j}", [P, 512], BF16)) for j in range(2)]
            x2y_sb = c4.enter_context(nc.sbuf_tensor("x2y_sb", [P, 4 * D], BF16))
            x_st4 = [c4.enter_context(nc.sbuf_tensor(f"x_st4{j}", [P, D], F32))
                     for j in range(2)]
            st4 = c4.enter_context(nc.sbuf_tensor("st4", [P, 8], F32))
            x2ln_st = [c4.enter_context(nc.sbuf_tensor(f"x2ln_st{j}", [P, D], BF16))
                       for j in range(2)]
            xlt_st = [c4.enter_context(nc.sbuf_tensor(f"xlt_st{j}", [P, 8 * P], BF16)) for j in range(2)]
            ps_yT = [c4.enter_context(nc.psum_tensor(f"ps_yT{j}", [P, 512], F32)) for j in range(2)]
            ps_trY = [c4.enter_context(nc.psum_tensor(f"ps_trY{j}", [P, 512], BF16)) for j in range(2)]
            ps_trL = [c4.enter_context(nc.psum_tensor(f"ps_trL{j}", [P, 1024], BF16)) for j in range(2)]

            it_groups = [(g * 8, min(8, IT - g * 8)) for g in range(NGR)]

            with nc.Block() as blk:
                ev_pc_done = (s_pc.h, s_pc.v)
                ev_tfl = [None]
                ev_trep = {}
                ev_uld = {}
                ev_sld = {}
                ev_absg = {}
                ev_apply = {}
                ev_ring = {}
                ev_mmg = {}
                ev_yev = {}
                ev_trY = {}
                ev_chev = {}
                ev_rmw = {}
                ev_rb = {}
                ev_norm4 = {}
                ev_trL = {}
                ev_xlte = {}
                ev_spl_xlt = {}
                s_trl = S["trl"]

                @blk.gpsimd
                def _(g):
                    g.wait_ge(*ev_pc_done)
                    g.nop().then_inc(S["pcg"].h, 1)
                S["pcg"].v += 1
                ev_pc_all = (S["pcg"].h, S["pcg"].v)

                @blk.sync
                def _(sync):
                    sync.wait_ge(*p23_spl_final)
                    ev_tfl[0] = s_hld.inc(sync.dma_start(
                        out=tfl, in_=tflat_sp.rearrange("(a b) -> a b", a=1)), 16)

                for c in range(4):
                    @blk.tensor
                    def _(t, c=c):
                        if c == 0:
                            wt(t, ev_tfl[0])
                            t.wait_ge(*ue_sc_total)
                            t.wait_ge(*ue_ve_total)
                            wt(t, p23_ve2_final)
                        if c >= 2:
                            wt(t, ev_trep[c - 2])
                        instr = t.matmul(ps_yT[c % 2][:, :], ones1[:, :],
                                         tfl[:, c * 512:(c + 1) * 512], start=True, stop=True)
                        ev_trep[(c, "mm")] = s_tr.inc(instr)

                    @blk.vector
                    def _(v, c=c):
                        wt(v, ev_trep[(c, "mm")])
                        ev_trep[c] = s_ve.inc(v.tensor_copy(
                            t_rep[:, c * 512:(c + 1) * 512], ps_yT[c % 2][:, :]))

                def emit_rmw(b, nt):
                    @blk.gpsimd
                    def _(g, b=b, nt=nt):
                        if b == 0 and nt == 0:
                            g.wait_ge(S["xcp"].h, S["xcp"].v)
                        wt(g, ev_chev[b * HT + HT - 1])
                        for c0 in range(0, D, 1024):
                            ev_rmw[(b, nt)] = s_acc.inc(g.dma_start(
                                out=out[(b * 4 + nt) * P:(b * 4 + nt + 1) * P,
                                        c0:c0 + 1024],
                                in_=x2y_sb[:, nt * D + c0:nt * D + c0 + 1024],
                                accum_op=ALU.add), 16)

                def emit_rb(b, nt):
                    m = b * 4 + nt

                    @blk.sync
                    def _(sync, b=b, nt=nt, m=m):
                        if m >= 2:
                            wt(sync, ev_norm4[divmod(m - 2, 4)])
                        wt(sync, ev_rmw[(b, nt)])
                        ev_rb[(b, nt)] = s_hld.inc(sync.dma_start(
                            out=x_st4[m % 2][:, :], in_=out[m * P:(m + 1) * P, :]), 16)

                def emit_lnchain(b, nt):
                    m = b * 4 + nt
                    extra = (s_trl.h, 4 * (m - 2) + 4) if m >= 2 else None
                    ev_norm4[(b, nt)], _ = emit_ln(
                        blk, x_st4[m % 2][:, :], x2ln_st[m % 2][:, :], st4,
                        x2ln_st[m % 2][:, :], ev_rb[(b, nt)], ev_extra_sc=extra)

                def emit_trl(b, nt, g2):
                    m = b * 4 + nt
                    k = m * 4 + g2

                    @blk.tensor
                    def _(t, m=m, g2=g2, k=k, b=b, nt=nt):
                        if g2 == 0:
                            wt(t, ev_norm4[(b, nt)])
                        if k >= 2:
                            wt(t, ev_xlte[divmod(k - 2, 4)])
                        for h in range(8):
                            ht2 = 8 * g2 + h
                            instr = t.transpose(ps_trL[g2 % 2][:, h * P:(h + 1) * P],
                                                x2ln_st[m % 2][:, ht2 * P:(ht2 + 1) * P],
                                                ident_bf[:, :])
                        ev_trL[(m, g2)] = s_trl.inc(instr)

                    @blk.scalar
                    def _(sc, m=m, g2=g2, k=k):
                        wt(sc, ev_trL[(m, g2)])
                        if k >= 2:
                            wt(sc, ev_spl_xlt[divmod(k - 2, 4)])
                        ev_xlte[(m, g2)] = s_sc.inc(sc.activation(
                            out=xlt_st[k % 2][:, :], in_=ps_trL[g2 % 2][:, :],
                            func=AF.Identity))

                    @blk.sync
                    def _(sync, m=m, g2=g2, k=k):
                        wt(sync, ev_xlte[(m, g2)])
                        src = xlt_st[k % 2][:, :].rearrange("p (h n) -> p h n", n=P)
                        dst = x2lnT_sp[8 * g2:8 * (g2 + 1), :,
                                       m * P:(m + 1) * P].rearrange("ht p n -> p ht n")
                        ev_spl_xlt[(m, g2)] = s_spl2.inc(
                            sync.dma_start(out=dst, in_=src), 16)

                def deferred_sched(pb):
                    sched = {}

                    def at(ht, fn):
                        sched.setdefault(ht, []).append(fn)

                    at(1, lambda: emit_rb(pb, 0))
                    at(2, lambda: emit_rb(pb, 1))
                    at(2, lambda: emit_lnchain(pb, 0))
                    at(3, lambda: emit_lnchain(pb, 1))
                    at(6, lambda: emit_rb(pb, 2))
                    at(7, lambda: emit_rb(pb, 3))
                    for g2 in range(4):
                        at(8 + g2, lambda g2=g2: emit_trl(pb, 0, g2))
                    for g2 in range(4):
                        at(12 + g2, lambda g2=g2: emit_trl(pb, 1, g2))
                    at(12, lambda: emit_lnchain(pb, 2))
                    for g2 in range(4):
                        at(16 + g2, lambda g2=g2: emit_trl(pb, 2, g2))
                    at(16, lambda: emit_lnchain(pb, 3))
                    for g2 in range(4):
                        at(20 + g2, lambda g2=g2: emit_trl(pb, 3, g2))
                    return sched

                def emit_apply(b, g2):
                    it0, n_it = it_groups[g2]
                    ga = b * NGR + g2

                    @blk.sync
                    def _(sync, b=b, g2=g2, it0=it0, n_it=n_it, ga=ga):
                        if ga == 0:
                            sync.wait_ge(*ue_sc_total)
                            sync.wait_ge(*ue_ve_total)
                            sync.wait_ge(*p23_sc2_final)
                            sync.wait_ge(*p23_ve2_final)
                            sync.wait_ge(*p23_mm_final)
                            sync.wait_ge(*p23_uspl_final)
                        if b > 0:
                            wt(sync, ev_mmg[(b - 1, HT - 1, g2)])
                        src = uT_sp[it0:it0 + n_it, :, b * NBTOK:(b + 1) * NBTOK].rearrange(
                            "it p n -> p it n")
                        dst = gcol[:, it0 * 512:(it0 + n_it) * 512].rearrange(
                            "p (it n) -> p it n", n=512)
                        ev_uld[ga] = s_hld.inc(sync.dma_start(out=dst, in_=src), 16)
                        if ga >= 1:
                            wt(sync, ev_apply[ga - 1])
                        src2 = sT_sp[it0:it0 + n_it, :, b * NBTOK:(b + 1) * NBTOK].rearrange(
                            "it p n -> p it n")
                        dst2 = s_stage[:, 0:n_it * 512].rearrange(
                            "p (it n) -> p it n", n=512)
                        ev_sld[ga] = s_hld.inc(sync.dma_start(out=dst2, in_=src2), 16)

                    @blk.scalar
                    def _(sc, ga=ga, n_it=n_it):
                        wt(sc, ev_sld[ga])
                        if ga >= 1:
                            wt(sc, ev_apply[ga - 1])
                        ev_absg[ga] = s_sc.inc(sc.activation(
                            out=abs_st4[:, 0:n_it * 512], in_=s_stage[:, 0:n_it * 512],
                            func=AF.Abs))

                    @blk.vector
                    def _(v, b=b, g2=g2, it0=it0, n_it=n_it, ga=ga):
                        wt(v, ev_uld[ga])
                        wt(v, ev_sld[ga])
                        gsl = gcol[:, it0 * 512:(it0 + n_it) * 512]
                        chain(v, s_ve2, v.tensor_tensor(
                            out=gsl, in0=gsl, in1=s_stage[:, 0:n_it * 512],
                            op=ALU.mult))
                        wt(v, ev_absg[ga])
                        for jj in range(n_it):
                            chain(v, s_ve2, v.tensor_tensor(
                                out=s_stage[:, jj * 512:(jj + 1) * 512],
                                in0=abs_st4[:, jj * 512:(jj + 1) * 512],
                                in1=t_rep[:, b * NBTOK:(b + 1) * NBTOK], op=ALU.is_gt))
                        ev_apply[ga] = chain(v, s_ve2, v.tensor_tensor(
                            out=gsl, in0=gsl, in1=s_stage[:, 0:n_it * 512],
                            op=ALU.mult))

                N_EARLY = 4

                for b in range(NB):
                    sched = deferred_sched(b - 1) if b > 0 else {}
                    for g2 in range(N_EARLY if b > 0 else 0, NGR):
                        emit_apply(b, g2)

                    for ht in range(HT):
                        hi = b * HT + ht

                        for g2, (it0, n_it) in enumerate(it_groups):
                            @blk.gpsimd
                            def _(g, b=b, ht=ht, hi=hi, g2=g2, it0=it0, n_it=n_it):
                                gi = hi * NGR + g2
                                if gi == 0:
                                    g.wait_ge(*ev_pc_all)
                                    g.wait_ge(*p23_mm_final)
                                if gi >= 4:
                                    pb, rem = divmod(gi - 4, HT * NGR)
                                    pht, pgg = divmod(rem, NGR)
                                    wt(g, ev_mmg[(pb, pht, pgg)])
                                src = wd_bf[ht, it0:it0 + n_it, :, :].rearrange("it p h -> p it h")
                                dst = wdr[:, :].rearrange("p (r i h) -> p r i h", r=4, h=P)[
                                    :, gi % 4, 0:n_it, :]
                                ev_ring[gi] = s_wld.inc(g.dma_start(out=dst, in_=src), 16)

                            @blk.tensor
                            def _(t, b=b, ht=ht, hi=hi, g2=g2, it0=it0, n_it=n_it):
                                gi = hi * NGR + g2
                                if g2 == 0 and hi >= 2:
                                    wt(t, ev_yev[hi - 2])
                                wt(t, ev_ring[gi])
                                if ht == 0:
                                    wt(t, ev_apply[b * NGR + g2])
                                for jj in range(n_it):
                                    it = it0 + jj
                                    instr = t.matmul(
                                        ps_yT[ht % 2][:, :],
                                        wdr[:, :].rearrange("p (r i h) -> p r i h", r=4, h=P)[
                                            :, gi % 4, jj, :],
                                        gcol[:, it * 512:(it + 1) * 512],
                                        start=(it == 0), stop=(it == IT - 1))
                                ev_mmg[(b, ht, g2)] = s_mm.inc(instr)

                        if ht == 0 and b > 0:
                            for nt in range(4):
                                emit_rmw(b - 1, nt)
                        if ht == HT - 1 and b + 1 < NB:
                            for g2 in range(N_EARLY):
                                emit_apply(b + 1, g2)

                        @blk.scalar
                        def _(sc, b=b, ht=ht, hi=hi):
                            wt(sc, ev_mmg[(b, ht, NGR - 1)])
                            if hi >= 2:
                                wt(sc, ev_trY[hi - 2])
                            ev_yev[hi] = s_sc.inc(sc.activation(out=yT_sb[ht % 2][:, :],
                                                                in_=ps_yT[ht % 2][:, :],
                                                                func=AF.Identity))

                        @blk.tensor
                        def _(t, b=b, ht=ht, hi=hi):
                            wt(t, ev_yev[hi])
                            if hi >= 2:
                                wt(t, ev_chev[hi - 2])
                            for c in range(4):
                                instr = t.transpose(ps_trY[ht % 2][:, c * P:(c + 1) * P],
                                                    yT_sb[ht % 2][:, c * P:(c + 1) * P],
                                                    ident_bf[:, :])
                            ev_trY[hi] = s_tr.inc(instr)

                        @blk.vector
                        def _(v, b=b, ht=ht, hi=hi):
                            wt(v, ev_trY[hi])
                            if ht == 0 and b > 0:
                                wt(v, ev_rmw[(b - 1, 3)])
                            for c in range(4):
                                instr = v.tensor_copy(
                                    x2y_sb[:, c * D + ht * P: c * D + (ht + 1) * P],
                                    ps_trY[ht % 2][:, c * P:(c + 1) * P])
                            ev_chev[hi] = s_ve.inc(instr)

                        for fn in sched.get(ht, []):
                            fn()

                lb = NB - 1
                for nt in range(4):
                    emit_rmw(lb, nt)
                emit_rb(lb, 0)
                emit_rb(lb, 1)
                emit_lnchain(lb, 0)
                emit_lnchain(lb, 1)
                for g2 in range(4):
                    emit_trl(lb, 0, g2)
                emit_rb(lb, 2)
                emit_lnchain(lb, 2)
                for g2 in range(4):
                    emit_trl(lb, 1, g2)
                emit_rb(lb, 3)
                emit_lnchain(lb, 3)
                for g2 in range(4):
                    emit_trl(lb, 2, g2)
                for g2 in range(4):
                    emit_trl(lb, 3, g2)

        p4_finals = [(s_mm.h, s_mm.v), (s_tr.h, s_tr.v), (s_sc.h, s_sc.v),
                     (s_ve.h, s_ve.v), (s_ve2.h, s_ve2.v), (s_hld.h, s_hld.v),
                     (s_spl.h, s_spl.v), (s_spl2.h, s_spl2.v),
                     (s_trl.h, s_trl.v)]

        if phases < 5:
            _finish_partial(nc, S, out)
            return nc

        with ExitStack() as c5:
            xlnT2 = c5.enter_context(nc.sbuf_tensor("xlnT2", [P, HT * NTOK], BF16))
            bt_sb = c5.enter_context(nc.sbuf_tensor("bt_sb", [P, D], BF16))
            bs_sb = c5.enter_context(nc.sbuf_tensor("bs_sb", [P, HT], F32))
            hw_sb = c5.enter_context(nc.sbuf_tensor("hw_sb", [P, NTOK], BF16))

            ev5 = {}
            with ExitStack() as c5a:
                wr_sb = c5a.enter_context(nc.sbuf_tensor("wr_sb", [P, HT * 8], BF16))
                at_sb = c5a.enter_context(nc.sbuf_tensor("at_sb", [P, HT * P], BF16))
                rep_sb = c5a.enter_context(nc.sbuf_tensor("rep_sb", [8, P], F32))
                lgT_sb = c5a.enter_context(nc.sbuf_tensor("lgT_sb", [8, NTOK], F32))
                lg_sb = c5a.enter_context(nc.sbuf_tensor("lg_sb", [P, NT * 8], F32))
                m8_sb = c5a.enter_context(nc.sbuf_tensor("m8_sb", [P, NT * 8], F32))
                dcol = c5a.enter_context(nc.sbuf_tensor("dcol", [P, NT], F32))
                w1c = c5a.enter_context(nc.sbuf_tensor("w1c", [P, NT], F32))
                w1s = c5a.enter_context(nc.sbuf_tensor("w1s", [P, NT], F32))
                w2s = c5a.enter_context(nc.sbuf_tensor("w2s", [P, NT], F32))
                wdc = c5a.enter_context(nc.sbuf_tensor("wdc", [P, NT], F32))
                eq1 = c5a.enter_context(nc.sbuf_tensor("eq1", [P, 8], F32))
                eq2 = c5a.enter_context(nc.sbuf_tensor("eq2", [P, 8], F32))
                t2s = c5a.enter_context(nc.sbuf_tensor("t2s", [P, 8], F32))
                comb = c5a.enter_context(nc.sbuf_tensor("comb", [P, NT * 8], F32))
                ctT_sb = c5a.enter_context(nc.sbuf_tensor("ctT_sb", [8, NTOK], F32))
                hT_sb = c5a.enter_context(nc.sbuf_tensor("hT_sb", [P, NTOK], F32))
                ps8 = c5a.enter_context(nc.psum_tensor("ps8", [8, NTOK], F32))
                ps_big = c5a.enter_context(nc.psum_tensor("ps_big", [P, NTOK], F32))

                with nc.Block() as blk:
                    ev_x2l = [None] * HT

                    @blk.sync
                    def _(sync):
                        sync.wait_ge(s_spl2.h, s_spl2.v)
                        for ht in range(HT):
                            ev_x2l[ht] = s_hld.inc(sync.dma_start(
                                out=xlnT2[:, ht * NTOK:(ht + 1) * NTOK],
                                in_=x2lnT_sp[ht, :, :]), 16)
                        ev5["bs"] = s_hld.inc(sync.dma_start(out=bs_sb[:, :], in_=bsbi[:, :]), 16)
                        ev5["rep"] = s_hld.inc(sync.dma_start(out=rep_sb[:, :], in_=repmi[:, :]), 16)

                    @blk.gpsimd
                    def _(g):
                        for ev in p4_finals:
                            g.wait_ge(*ev)
                        for q in range(4):
                            src = wrT.rearrange("(ht p) e -> p ht e", p=P)[:, 8 * q:8 * (q + 1), :]
                            dst = wr_sb[:, :].rearrange("p (ht e) -> p ht e", e=8)[
                                :, 8 * q:8 * (q + 1), :]
                            ev5["wr"] = s_wld.inc(g.dma_start(out=dst, in_=src), 16)
                        for q in range(4):
                            src = atT.rearrange("(ht p) e -> p ht e", p=P)[:, 8 * q:8 * (q + 1), :]
                            dst = at_sb[:, :].rearrange("p (ht e) -> p ht e", e=P)[
                                :, 8 * q:8 * (q + 1), :]
                            ev5["at"] = s_wld.inc(g.dma_start(out=dst, in_=src), 16)
                        ev5["bt"] = s_wld.inc(g.dma_start(out=bt_sb[:, :], in_=btT[:, :]), 16)

                    @blk.tensor
                    def _(t):
                        wt(t, ev5["wr"])
                        for ht in range(HT):
                            wt(t, ev_x2l[ht])
                            for c in range(4):
                                instr = t.matmul(ps8[:, c * 512:(c + 1) * 512],
                                                 wr_sb[:, ht * 8:(ht + 1) * 8],
                                                 xlnT2[:, ht * NTOK + c * 512: ht * NTOK + (c + 1) * 512],
                                                 start=(ht == 0), stop=(ht == HT - 1))
                        ev5["lgT_mm"] = s_mm.inc(instr)

                    @blk.vector
                    def _(v):
                        wt(v, ev5["lgT_mm"])
                        ev5["lgT_ev"] = s_ve.inc(v.tensor_copy(lgT_sb[:, :], ps8[:, :]))

                    @blk.tensor
                    def _(t):
                        wt(t, ev5["lgT_ev"])
                        for nt in range(NT):
                            instr = t.transpose(ps_big[:, nt * 8:(nt + 1) * 8],
                                                lgT_sb[:, nt * P:(nt + 1) * P],
                                                ident_f[0:8, 0:8])
                        ev5["lg_tr"] = s_tr.inc(instr)

                    @blk.vector
                    def _(v):
                        wt(v, ev5["lg_tr"])
                        ev5["lg_ev"] = s_ve.inc(v.tensor_copy(lg_sb[:, :], ps_big[:, 0:NT * 8]))

                    @blk.tensor
                    def _(t):
                        wt(t, ev5["lg_ev"])
                        wt(t, ev5["at"])
                        for ht in range(HT):
                            for c in range(4):
                                instr = t.matmul(ps_big[:, c * 512:(c + 1) * 512],
                                                 at_sb[:, ht * P:(ht + 1) * P],
                                                 xlnT2[:, ht * NTOK + c * 512: ht * NTOK + (c + 1) * 512],
                                                 start=(ht == 0), stop=(ht == HT - 1))
                        ev5["hT_mm"] = s_mm.inc(instr)

                    @blk.vector
                    def _(v):
                        wt(v, ev5["hT_mm"])
                        ev5["hT_ev"] = s_ve.inc(v.tensor_copy(hT_sb[:, :], ps_big[:, :]))
                        wt(v, ev5["lg_ev"])
                        for nt in range(NT):
                            chain(v, s_ve2, v.max(out=m8_sb[:, nt * 8:(nt + 1) * 8],
                                                  in_=lg_sb[:, nt * 8:(nt + 1) * 8]))
                        l1v = m8_sb[:, :].rearrange("p (nt e) -> p nt e", e=8)[:, :, 0:1]
                        l2v = m8_sb[:, :].rearrange("p (nt e) -> p nt e", e=8)[:, :, 1:2]
                        ev5["dcol"] = chain(v, s_ve2, v.tensor_tensor(
                            out=dcol[:, :].rearrange("p (nt o) -> p nt o", o=1),
                            in0=l1v, in1=l2v, op=ALU.subtract))

                    @blk.scalar
                    def _(sc):
                        wt(sc, ev5["dcol"])
                        ev5["w1"] = s_sc.inc(sc.activation(out=w1c[:, :], in_=dcol[:, :],
                                                           func=AF.Sigmoid))

                    @blk.vector
                    def _(v):
                        wt(v, ev5["w1"])
                        chain(v, s_ve2, v.tensor_scalar(out=w1s[:, :], in0=w1c[:, :],
                                                        scalar1=2.0, scalar2=None, op0=ALU.mult))
                        chain(v, s_ve2, v.tensor_scalar(out=w2s[:, :], in0=w1c[:, :],
                                                        scalar1=-2.0, scalar2=2.0,
                                                        op0=ALU.mult, op1=ALU.add))
                        chain(v, s_ve2, v.tensor_tensor(out=wdc[:, :], in0=w1s[:, :],
                                                        in1=w2s[:, :], op=ALU.subtract))
                        for nt in range(NT):
                            lg_sl = lg_sb[:, nt * 8:(nt + 1) * 8]
                            chain(v, s_ve2, v.tensor_scalar(out=eq1[:, :], in0=lg_sl,
                                                            scalar1=m8_sb[:, nt * 8:nt * 8 + 1],
                                                            scalar2=None, op0=ALU.is_equal))
                            chain(v, s_ve2, v.tensor_scalar(out=eq2[:, :], in0=lg_sl,
                                                            scalar1=m8_sb[:, nt * 8 + 1:nt * 8 + 2],
                                                            scalar2=None, op0=ALU.is_equal))
                            chain(v, s_ve2, v.tensor_scalar(out=t2s[:, :], in0=eq2[:, :],
                                                            scalar1=w2s[:, nt:nt + 1],
                                                            scalar2=None, op0=ALU.mult))
                            chain(v, s_ve2, v.scalar_tensor_tensor(
                                out=comb[:, nt * 8:(nt + 1) * 8], in0=eq1[:, :],
                                scalar=w1s[:, nt:nt + 1], in1=t2s[:, :],
                                op0=ALU.mult, op1=ALU.add))
                        ev5["comb"] = chain(v, s_ve2, v.tensor_copy(comb[:, 0:1], comb[:, 0:1]))

                    @blk.tensor
                    def _(t):
                        wt(t, ev5["comb"])
                        for nt in range(NT):
                            instr = t.transpose(ps8[:, nt * P:(nt + 1) * P],
                                                comb[:, nt * 8:(nt + 1) * 8], ident_f[:, :])
                        ev5["ct_tr"] = s_tr.inc(instr)

                    @blk.vector
                    def _(v):
                        wt(v, ev5["ct_tr"])
                        ev5["ct_ev"] = s_ve.inc(v.tensor_copy(ctT_sb[:, :], ps8[:, :]))

                    @blk.tensor
                    def _(t):
                        wt(t, ev5["ct_ev"])
                        wt(t, ev5["hT_ev"])
                        wt(t, ev5["rep"])
                        for c in range(4):
                            instr = t.matmul(ps_big[:, c * 512:(c + 1) * 512], rep_sb[:, :],
                                             ctT_sb[:, c * 512:(c + 1) * 512],
                                             start=True, stop=True)
                        ev5["crep"] = s_tr.inc(instr)

                    @blk.vector
                    def _(v):
                        wt(v, ev5["crep"])
                        ev5["hw"] = chain(v, s_ve2, v.tensor_tensor(
                            out=hw_sb[:, :], in0=hT_sb[:, :], in1=ps_big[:, :], op=ALU.mult))

            with ExitStack() as c5b:
                wbr = c5b.enter_context(nc.sbuf_tensor("wbr", [P, 8 * 8 * P], BF16))
                y2b = [c5b.enter_context(nc.sbuf_tensor(f"y2b{j}", [P, 1024], F32)) for j in range(3)]
                o_st = [c5b.enter_context(nc.sbuf_tensor(f"o_st{j}", [P, 1024], F32)) for j in range(4)]
                ps_y2 = [c5b.enter_context(nc.psum_tensor(f"ps_y2{j}", [P, 1024], F32)) for j in range(3)]
                ps_trO = c5b.enter_context(nc.psum_tensor("ps_trO", [P, 1024], F32))

                with nc.Block() as blk:
                    ev_ring5 = {}
                    ev_mm5 = {}
                    ev_y2e = {}
                    ev_tro = {}
                    ev_oev = {}
                    ev_oacc = {}
                    ev_out_all = (s_acc.h, s_acc.v)

                    for ho in range(HT):
                        @blk.gpsimd
                        def _(g, ho=ho):
                            if ho == 0:
                                g.wait_ge(*ev_out_all)
                            for q in range(4):
                                gi = ho * 4 + q
                                if gi >= 8:
                                    pho, pq = divmod(gi - 8, 4)
                                    wt(g, ev_mm5[(pho, 1)])
                                src = wbT.rearrange("(hi p) (ho h) -> p hi ho h", p=P, h=P)[
                                    :, 8 * q:8 * (q + 1), ho, :]
                                dst = wbr[:, :].rearrange("p (r i h) -> p r i h", r=8, h=P)[
                                    :, gi % 8, :, :]
                                ev_ring5[gi] = s_wld.inc(g.dma_start(out=dst, in_=src), 16)

                        for nh in range(2):
                            @blk.tensor
                            def _(t, ho=ho, nh=nh):
                                k = ho * 2 + nh
                                if k >= 3:
                                    wt(t, ev_y2e[k - 3])
                                if nh == 0:
                                    wt(t, ev_ring5[ho * 4 + 3])
                                if k == 0:
                                    wt(t, ev5["hw"])
                                    wt(t, ev5["bt"])
                                for hi in range(HT):
                                    for c in range(2):
                                        instr = t.matmul(
                                            ps_y2[k % 3][:, c * 512:(c + 1) * 512],
                                            wbr[:, :].rearrange("p (r i h) -> p r i h", r=8, h=P)[
                                                :, (ho * 4 + hi // 8) % 8, hi % 8, :],
                                            xlnT2[:, hi * NTOK + nh * 1024 + c * 512:
                                                  hi * NTOK + nh * 1024 + (c + 1) * 512],
                                            start=(hi == 0), stop=False)
                                for c in range(2):
                                    instr = t.matmul(
                                        ps_y2[k % 3][:, c * 512:(c + 1) * 512],
                                        bt_sb[:, ho * P:(ho + 1) * P],
                                        hw_sb[:, nh * 1024 + c * 512: nh * 1024 + (c + 1) * 512],
                                        start=False, stop=True)
                                ev_mm5[(ho, nh)] = s_mm.inc(instr)

                            @blk.scalar
                            def _(sc, ho=ho, nh=nh):
                                k = ho * 2 + nh
                                wt(sc, ev_mm5[(ho, nh)])
                                if k >= 3:
                                    wt(sc, ev_tro[k - 3])
                                ev_y2e[k] = s_sc.inc(sc.activation(
                                    out=y2b[k % 3][:, :], in_=ps_y2[k % 3][:, :], func=AF.Identity,
                                    bias=bs_sb[:, ho:ho + 1]))

                            @blk.tensor
                            def _(t, ho=ho, nh=nh):
                                k = ho * 2 + nh
                                wt(t, ev_y2e[k])
                                if k >= 1:
                                    wt(t, ev_oev[k - 1])
                                for c in range(8):
                                    instr = t.transpose(ps_trO[:, c * P:(c + 1) * P],
                                                        y2b[k % 3][:, c * P:(c + 1) * P],
                                                        ident_f[:, :])
                                ev_tro[k] = s_tr.inc(instr)

                            @blk.vector
                            def _(v, ho=ho, nh=nh):
                                k = ho * 2 + nh
                                wt(v, ev_tro[k])
                                if k >= 4:
                                    wt(v, ev_oacc[k - 4])
                                ev_oev[k] = s_ve.inc(v.tensor_copy(o_st[k % 4][:, :], ps_trO[:, :]))

                            @blk.gpsimd
                            def _(g, ho=ho, nh=nh):
                                k = ho * 2 + nh
                                wt(g, ev_oev[k])
                                src = o_st[k % 4][:, :].rearrange("p (nt n) -> p nt n", n=P)
                                dst = out.rearrange("(nt p) (ho h) -> p nt ho h", p=P, h=P)[
                                    :, nh * 8:(nh + 1) * 8, ho, :]
                                ev_oacc[k] = s_acc.inc(g.dma_start(out=dst, in_=src,
                                                                   accum_op=ALU.add), 16)

        with nc.Block() as blk:
            @blk.gpsimd
            def _(g):
                g.wait_ge(s_acc.h, s_acc.v)

            @blk.sync
            def _(sync):
                sync.wait_ge(s_spl.h, s_spl.v)
                sync.wait_ge(s_spl2.h, s_spl2.v)

    return nc


def _finish_partial(nc, S, out):
    with nc.Block() as blk:
        @blk.gpsimd
        def _(g):
            g.wait_ge(S["wld"].h, S["wld"].v)
            g.wait_ge(S["pc"].h, S["pc"].v)
            g.wait_ge(S["acc"].h, S["acc"].v)

        @blk.sync
        def _(sync):
            sync.wait_ge(S["hld"].h, S["hld"].v)
            sync.wait_ge(S["spl"].h, S["spl"].v)
            sync.wait_ge(S["spl2"].h, S["spl2"].v)


def _prep(inputs):
    f32 = np.float32
    x = np.ascontiguousarray(np.asarray(inputs["L0_attn_x"], dtype=f32).reshape(NTOK * 8, D))
    wgT_ = np.ascontiguousarray(np.asarray(inputs["L0_attn_w_gate"], dtype=f32).T)
    wuT_ = np.ascontiguousarray(np.asarray(inputs["L0_attn_w_up"], dtype=f32).T)
    wdT_ = np.ascontiguousarray(np.asarray(inputs["L0_attn_w_down"], dtype=f32).T)
    wbT_ = np.ascontiguousarray(np.asarray(inputs["L0_ffn_W_base"], dtype=f32).T)
    wrT_ = np.ascontiguousarray(np.asarray(inputs["L0_ffn_W_router"], dtype=f32).T)
    A = np.asarray(inputs["L0_ffn_A"], dtype=f32)
    B = np.asarray(inputs["L0_ffn_B"], dtype=f32)
    atT_ = np.ascontiguousarray(np.transpose(A, (2, 0, 1)).reshape(D, P))
    btT_ = np.ascontiguousarray(np.transpose(B, (0, 2, 1)).reshape(P, D))
    b_base = np.asarray(inputs["L0_ffn_b_base"], dtype=f32)
    bsb_ = np.ascontiguousarray(b_base.reshape(HT, P).T)
    idbf_ = np.eye(P, dtype=ml_dtypes.bfloat16)
    idf_ = np.eye(P, dtype=f32)
    repm_ = np.zeros((8, P), dtype=f32)
    for e in range(8):
        repm_[e, e * 16:(e + 1) * 16] = 1.0
    shared = {"wgT": wgT_, "wuT": wuT_, "wdT": wdT_, "wbT": wbT_, "wrT": wrT_,
              "atT": atT_, "btT": btT_, "bsbi": bsb_, "idbf": idbf_, "idf": idf_,
              "repmi": repm_}
    in_maps = []
    for c in range(8):
        m = dict(shared)
        m["x"] = x[c * NTOK:(c + 1) * NTOK]
        in_maps.append(m)
    return in_maps


_LAST_RESULTS = None


def kernel(**inputs):
    global _LAST_RESULTS
    in_maps = _prep(inputs)
    nc = build_program(phases=_PHASES, dbg=_DBG)
    res = run_bass_kernel_spmd(nc, in_maps, core_ids=list(range(8)))
    _LAST_RESULTS = res
    outs = [r["out"] for r in res.results]
    return np.concatenate(outs, axis=0).reshape(4, 4096, D)


# version 06
# speedup vs baseline: 1.0904x; 1.0415x over previous
import numpy as np
import ml_dtypes
from contextlib import ExitStack

import concourse.bass as bass
import concourse.mybir as mybir
from concourse.bass_utils import run_bass_kernel_spmd

F32 = mybir.dt.float32
BF16 = mybir.dt.bfloat16
AF = mybir.ActivationFunctionType
ALU = mybir.AluOpType
AX = mybir.AxisListType

P = 128
NTOK = 2048
NT = NTOK // P
D = 4096
HT = D // P
I = 11008
IT = I // P
NHALF = I // 2
NQ = I // 4
KKEEP = float(I // 2)
EPS = 1e-5
BIS_ITERS = 14
HI0 = 8.0
SNAP = 1.0 + 2.0 ** -9
NB = 4
NBTOK = NTOK // NB
NGR = (IT + 7) // 8
NCH = 4 * IT
TR_TILE = 76

_PHASES = 6
_DBG = False


class Sem:
    def __init__(self, h):
        self.h = h
        self.v = 0

    def inc(self, instr, n=1):
        instr.then_inc(self.h, n)
        self.v += n
        return (self.h, self.v)


def chain(eng, sem, instr):
    ev = sem.inc(instr, 1)
    eng.wait_ge(*ev)
    return ev


def wt(eng, ev):
    if ev is not None:
        eng.wait_ge(*ev)


def build_program(phases=6, dbg=False):
    nc = bass.Bass("TRN2", target_bir_lowering=False)

    x_in = nc.dram_tensor("x", [NTOK, D], F32, kind="ExternalInput").ap()
    wgT = nc.dram_tensor("wgT", [D, I], F32, kind="ExternalInput").ap()
    wuT = nc.dram_tensor("wuT", [D, I], F32, kind="ExternalInput").ap()
    wdT = nc.dram_tensor("wdT", [I, D], F32, kind="ExternalInput").ap()
    wbT = nc.dram_tensor("wbT", [D, D], F32, kind="ExternalInput").ap()
    wrT = nc.dram_tensor("wrT", [D, 8], F32, kind="ExternalInput").ap()
    atT = nc.dram_tensor("atT", [D, P], F32, kind="ExternalInput").ap()
    btT = nc.dram_tensor("btT", [P, D], F32, kind="ExternalInput").ap()
    bsbi = nc.dram_tensor("bsbi", [P, HT], F32, kind="ExternalInput").ap()
    idbf = nc.dram_tensor("idbf", [P, P], BF16, kind="ExternalInput").ap()
    idf = nc.dram_tensor("idf", [P, P], F32, kind="ExternalInput").ap()
    repmi = nc.dram_tensor("repmi", [8, P], F32, kind="ExternalInput").ap()
    out = nc.dram_tensor("out", [NTOK, D], F32, kind="ExternalOutput").ap()

    sk = "ExternalOutput" if dbg else "Internal"
    sT_sp = nc.dram_tensor("sT_sp", [IT, P, NTOK], BF16, kind=sk).ap()
    sabs_sp = nc.dram_tensor("sabs_sp", [NT, P, I], BF16, kind=sk).ap()
    uT_sp = nc.dram_tensor("uT_sp", [IT, P, NTOK], BF16, kind=sk).ap()
    x2lnT_sp = nc.dram_tensor("x2lnT_sp", [HT, P, NTOK], BF16, kind=sk).ap()
    wd_bf = nc.dram_tensor("wd_bf", [HT, IT, P, P], BF16, kind="Internal").ap()
    tflat_sp = nc.dram_tensor("tflat_sp", [NTOK], F32, kind="Internal").ap()

    with ExitStack() as ctx:
        ident_bf = ctx.enter_context(nc.sbuf_tensor("ident_bf", [P, P], BF16))
        ident_f = ctx.enter_context(nc.sbuf_tensor("ident_f", [P, P], F32))
        t_all = ctx.enter_context(nc.sbuf_tensor("t_all", [P, NT], F32))
        ones1 = ctx.enter_context(nc.sbuf_tensor("ones1", [1, P], F32))
        cst = ctx.enter_context(nc.sbuf_tensor("cst", [P, 2], F32))

        S = {}
        for name in ["hld", "spl", "spl2", "wld", "pc", "mm", "tr", "sc", "ve",
                     "sc2", "ve2", "acc", "uesc", "ueve", "bsc", "bve", "uspl",
                     "pcg", "xcp", "trl"]:
            S[name] = Sem(ctx.enter_context(nc.semaphore("s_" + name)))
        s_hld, s_spl, s_spl2, s_wld, s_pc = S["hld"], S["spl"], S["spl2"], S["wld"], S["pc"]
        s_mm, s_tr, s_sc, s_ve = S["mm"], S["tr"], S["sc"], S["ve"]
        s_sc2, s_ve2, s_acc = S["sc2"], S["ve2"], S["acc"]
        s_uesc, s_ueve = S["uesc"], S["ueve"]
        s_bsc, s_bve = S["bsc"], S["bve"]
        s_uspl = S["uspl"]

        sc_tiles = [q for q in range(IT) if q % 2 == 0]
        ve_tiles = [q for q in range(IT) if q % 2 == 1]

        def ue_ev(ti):
            if ti % 2 == 0:
                return (s_uesc.h, sc_tiles.index(ti) + 1)
            return (s_ueve.h, ve_tiles.index(ti) + 1)

        def emit_ln(blk, x_ap, xsq_ap, st, out_bf_ap, ev_in_ready, ev_extra_sc=None):
            evs = {}

            @blk.scalar
            def _(sc):
                wt(sc, ev_in_ready)
                if ev_extra_sc is not None:
                    wt(sc, ev_extra_sc)
                evs["sq"] = s_sc.inc(sc.activation(out=xsq_ap, in_=x_ap, func=AF.Square,
                                                   accum_out=st[:, 2:3]))

            @blk.vector
            def _(v):
                wt(v, ev_in_ready)
                chain(v, s_ve2, v.reduce_sum(st[:, 0:1], x_ap, axis=AX.X))
                chain(v, s_ve2, v.tensor_scalar(out=st[:, 3:4], in0=st[:, 0:1],
                                                scalar1=1.0 / D, scalar2=None, op0=ALU.mult))
                wt(v, evs["sq"])
                chain(v, s_ve2, v.tensor_scalar(out=st[:, 4:5], in0=st[:, 2:3],
                                                scalar1=1.0 / D, scalar2=None, op0=ALU.mult))
                chain(v, s_ve2, v.tensor_tensor(out=st[:, 1:2], in0=st[:, 3:4],
                                                in1=st[:, 3:4], op=ALU.mult))
                chain(v, s_ve2, v.tensor_tensor(out=st[:, 5:6], in0=st[:, 4:5],
                                                in1=st[:, 1:2], op=ALU.subtract))
                evs["var"] = chain(v, s_ve2, v.tensor_scalar(out=st[:, 5:6], in0=st[:, 5:6],
                                                             scalar1=EPS, scalar2=None, op0=ALU.add))

            @blk.scalar
            def _(sc):
                wt(sc, evs["var"])
                evs["sqrt"] = s_sc.inc(sc.activation(out=st[:, 6:7], in_=st[:, 5:6], func=AF.Sqrt))

            @blk.vector
            def _(v):
                wt(v, evs["sqrt"])
                chain(v, s_ve2, v.reciprocal(st[:, 6:7], st[:, 6:7]))
                evs["rb"] = chain(v, s_ve2, v.scalar_tensor_tensor(
                    out=st[:, 7:8], in0=st[:, 3:4], scalar=-1.0, in1=st[:, 6:7],
                    op0=ALU.mult, op1=ALU.mult))

            @blk.scalar
            def _(sc):
                wt(sc, evs["rb"])
                evs["norm"] = s_sc.inc(sc.activation(out=out_bf_ap, in_=x_ap, func=AF.Identity,
                                                     scale=st[:, 6:7], bias=st[:, 7:8]))

            return evs["norm"], evs["rb"]

        tile_mm_last = {}

        with ExitStack() as ctx03:
            xlnT = ctx03.enter_context(nc.sbuf_tensor("xlnT", [P, HT * NTOK], BF16))
            w_sb = [ctx03.enter_context(nc.sbuf_tensor(f"w_sb{j}", [P, HT * P], BF16))
                    for j in range(2)]

            def xlnT_sl(ht, c0, c1):
                return xlnT[:, ht * NTOK + c0: ht * NTOK + c1]

            with ExitStack() as c0:
                x_st = [c0.enter_context(nc.sbuf_tensor(f"x_st{j}", [P, D], F32)) for j in range(2)]
                st0 = c0.enter_context(nc.sbuf_tensor("st0", [P, 8], F32))
                xln_st = [c0.enter_context(nc.sbuf_tensor(f"xln_st{j}", [P, D], BF16)) for j in range(2)]
                ps_trX = [c0.enter_context(nc.psum_tensor(f"ps_trX{j}", [P, 1024], BF16)) for j in range(2)]

                with nc.Block() as blk:
                    @blk.sync
                    def _(sync):
                        sync.dma_start(out=ident_bf[:, :], in_=idbf[:, :]).then_inc(s_hld.h, 16)
                        sync.dma_start(out=ident_f[:, :], in_=idf[:, :]).then_inc(s_hld.h, 16)
                    s_hld.v += 32

                    @blk.gpsimd
                    def _(g):
                        g.memset(ones1[:, :], 1.0)
                        g.memset(cst[:, 0:1], 1.0)
                        g.memset(cst[:, 1:2], HI0)
                        for nt in range(NT):
                            S["xcp"].inc(g.dma_start(
                                out=out[nt * P:(nt + 1) * P, :],
                                in_=x_in[nt * P:(nt + 1) * P, :]), 16)

                    ev_norm = [None] * NT
                    ev_ldx = [None] * NT
                    ev_tr = {}
                    ev_evac = {}
                    for nt in range(NT):
                        @blk.sync
                        def _(sync, nt=nt):
                            if nt >= 2:
                                wt(sync, ev_norm[nt - 2])
                            ev_ldx[nt] = s_hld.inc(sync.dma_start(
                                out=x_st[nt % 2][:, :], in_=x_in[nt * P:(nt + 1) * P, :]), 16)

                        ev_norm[nt], _rb = emit_ln(
                            blk, x_st[nt % 2][:, :], xln_st[nt % 2][:, :], st0,
                            xln_st[nt % 2][:, :], ev_ldx[nt],
                            ev_extra_sc=(ev_tr[4 * (nt - 2) + 3] if nt >= 2 else None))

                        for g2 in range(4):
                            @blk.tensor
                            def _(t, nt=nt, g2=g2):
                                k = 4 * nt + g2
                                if g2 == 0:
                                    wt(t, ev_norm[nt])
                                if k >= 2:
                                    wt(t, ev_evac[k - 2])
                                for h in range(8):
                                    ht = 8 * g2 + h
                                    instr = t.transpose(ps_trX[g2 % 2][:, h * P:(h + 1) * P],
                                                        xln_st[nt % 2][:, ht * P:(ht + 1) * P],
                                                        ident_bf[:, :])
                                ev_tr[k] = s_tr.inc(instr)

                            @blk.vector
                            def _(v, nt=nt, g2=g2):
                                k = 4 * nt + g2
                                wt(v, ev_tr[k])
                                dst = xlnT[:, :].rearrange("p (ht n) -> p ht n", n=NTOK)[
                                    :, 8 * g2:8 * (g2 + 1), nt * P:(nt + 1) * P]
                                src = ps_trX[g2 % 2][:, :].rearrange("p (h n) -> p h n", n=P)
                                ev_evac[k] = s_ve.inc(v.tensor_copy(dst, src))

            ev_p0_done = (s_ve.h, s_ve.v)

            if phases < 1:
                _finish_partial(nc, S, out)
                return nc

            pc_jobs = []
            for ht in range(HT):
                for it0 in range(0, IT, 8):
                    pc_jobs.append((ht, it0, min(8, IT - it0)))

            with ExitStack() as c1:
                sT_st = [c1.enter_context(nc.sbuf_tensor(f"sT_st{j}", [P, NTOK], BF16))
                         for j in range(2)]
                ab_st = [c1.enter_context(nc.sbuf_tensor(f"ab_st{j}", [P, NTOK], BF16))
                         for j in range(2)]
                ps_g = [c1.enter_context(nc.psum_tensor(f"ps_g{j}", [P, 512], F32))
                        for j in range(4)]
                ps_trS = [c1.enter_context(nc.psum_tensor(f"ps_trS{j}", [P, 512], BF16))
                          for j in range(4)]

                with nc.Block() as blk:
                    ev_wld1 = {}
                    ev_mmc = {}
                    ev_silu = {}
                    ev_trc = {}
                    ev_absc = {}
                    ev_spl_sT = {}
                    ev_spl_ab = {}
                    pci = [0]

                    def emit_tr1(t, j):
                        ti, c = divmod(j, 4)
                        wt(t, ev_silu[j])
                        if j >= 4:
                            wt(t, ev_absc[j - 4])
                        for h in range(4):
                            tok = c * 4 + h
                            instr = t.transpose(ps_trS[j % 4][:, h * P:(h + 1) * P],
                                                sT_st[ti % 2][:, tok * P:(tok + 1) * P],
                                                ident_bf[:, :])
                        ev_trc[j] = s_tr.inc(instr)

                    def emit_abs1(sc, j):
                        ti, c = divmod(j, 4)
                        wt(sc, ev_trc[j])
                        if ti >= 2 and c == 0:
                            wt(sc, ev_spl_ab[ti - 2])
                        ev_absc[j] = s_sc.inc(sc.activation(
                            out=ab_st[ti % 2][:, c * 512:(c + 1) * 512],
                            in_=ps_trS[j % 4][:, :], func=AF.Abs))

                    for i in range(IT):
                        @blk.gpsimd
                        def _(g, i=i):
                            if i >= 2:
                                wt(g, tile_mm_last[i - 2])
                            for q in range(4):
                                src = wgT.rearrange("(ht p) (it i) -> p ht it i", p=P, i=P)[
                                    :, 8 * q:8 * (q + 1), i, :]
                                dst = w_sb[i % 2][:, :].rearrange("p (ht i) -> p ht i", i=P)[
                                    :, 8 * q:8 * (q + 1), :]
                                ev_wld1[i] = s_wld.inc(g.dma_start(out=dst, in_=src), 16)
                            n_pc = 5 if i < 8 else 4
                            for _k in range(n_pc):
                                if pci[0] < len(pc_jobs):
                                    htj, it0, n_it = pc_jobs[pci[0]]
                                    src = wdT.rearrange("(it p) (ht h) -> p it ht h", p=P, h=P)[
                                        :, it0:it0 + n_it, htj, :]
                                    dst = wd_bf[htj, it0:it0 + n_it, :, :].rearrange("it p h -> p it h")
                                    s_pc.inc(g.dma_start(out=dst, in_=src), 16)
                                    pci[0] += 1

                        @blk.tensor
                        def _(t, i=i):
                            for c in range(4):
                                j = 4 * i + c
                                if j == 0:
                                    wt(t, ev_p0_done)
                                if c == 0:
                                    wt(t, ev_wld1[i])
                                if j >= 4:
                                    wt(t, ev_silu[j - 4])
                                for ht in range(HT):
                                    instr = t.matmul(ps_g[j % 4][:, :],
                                                     w_sb[i % 2][:, ht * P:(ht + 1) * P],
                                                     xlnT_sl(ht, c * 512, (c + 1) * 512),
                                                     start=(ht == 0), stop=(ht == HT - 1))
                                ev_mmc[j] = s_mm.inc(instr)
                                if i >= 1:
                                    emit_tr1(t, 4 * (i - 1) + c)
                            tile_mm_last[i] = ev_mmc[4 * i + 3]

                        @blk.scalar
                        def _(sc, i=i):
                            for c in range(4):
                                j = 4 * i + c
                                wt(sc, ev_mmc[j])
                                if i >= 2 and c == 0:
                                    wt(sc, ev_spl_sT[i - 2])
                                if j >= 8:
                                    wt(sc, ev_trc[j - 8])
                                ev_silu[j] = s_sc.inc(sc.activation(
                                    out=sT_st[i % 2][:, c * 512:(c + 1) * 512],
                                    in_=ps_g[j % 4][:, :], func=AF.Silu))
                                if i >= 1:
                                    emit_abs1(sc, 4 * (i - 1) + c)

                        def emit_ab_spill(sync, i):
                            wt(sync, ev_absc[4 * i + 3])
                            for half in range(2):
                                src = ab_st[i % 2][:, :].rearrange("p (nt n) -> p nt n", n=P)[
                                    :, 8 * half:8 * (half + 1), :]
                                dst = sabs_sp.rearrange("nt p (it i) -> p nt it i", i=P)[
                                    :, 8 * half:8 * (half + 1), i, :]
                                ev_spl_ab[i] = s_spl2.inc(sync.dma_start(out=dst, in_=src), 16)

                        @blk.sync
                        def _(sync, i=i, emit_ab_spill=emit_ab_spill):
                            wt(sync, ev_silu[4 * i + 3])
                            ev_spl_sT[i] = s_spl.inc(sync.dma_start(
                                out=sT_sp[i, :, :], in_=sT_st[i % 2][:, :]), 16)
                            if i >= 1:
                                emit_ab_spill(sync, i - 1)

                    @blk.tensor
                    def _(t):
                        for c in range(4):
                            emit_tr1(t, 4 * (IT - 1) + c)

                    @blk.scalar
                    def _(sc):
                        for c in range(4):
                            emit_abs1(sc, 4 * (IT - 1) + c)

                    @blk.sync
                    def _(sync):
                        emit_ab_spill(sync, IT - 1)

            p1_sc_final = (s_sc.h, s_sc.v)
            p1_spl_final = (s_spl.h, s_spl.v)
            p1_spl2_final = (s_spl2.h, s_spl2.v)

            if phases < 2:
                _finish_partial(nc, S, out)
                return nc

            with ExitStack() as c2:
                vbuf = c2.enter_context(nc.sbuf_tensor("vbuf", [P, I], BF16))
                sbuf2 = c2.enter_context(nc.sbuf_tensor("sbuf2", [P, I], BF16))
                dummy = c2.enter_context(nc.sbuf_tensor("bis_dummy", [P, NQ], BF16))
                u_st = [c2.enter_context(nc.sbuf_tensor(f"u_st{j}", [P, NTOK], BF16))
                        for j in range(2)]
                vst = c2.enter_context(nc.sbuf_tensor("vst", [P, 10], F32))
                sst = c2.enter_context(nc.sbuf_tensor("sst", [P, 16], F32))
                vbf = c2.enter_context(nc.sbuf_tensor("vbf", [P, 1], BF16))
                vgt8 = c2.enter_context(nc.sbuf_tensor("vgt8", [P, 1], mybir.dt.uint8))
                vle8 = c2.enter_context(nc.sbuf_tensor("vle8", [P, 1], mybir.dt.uint8))
                sbf1 = c2.enter_context(nc.sbuf_tensor("sbf1", [P, 1], BF16))
                ps_u = [c2.enter_context(nc.psum_tensor(f"ps_u{j}", [P, 2048], F32))
                        for j in range(2)]

                with nc.Block() as blk:
                    ev_wld2 = {}
                    ev_mmu = {}
                    ev_uspl = {}
                    ev_vl = [None] * 8
                    ev_sl = [None] * 8
                    ev_vt = [None] * 8
                    ev_st_ = [None] * 8

                    def emit_uspill(g, ti):
                        g.wait_ge(*ue_ev(ti))
                        ev_uspl[ti] = s_uspl.inc(g.dma_start(
                            out=uT_sp[ti, :, :], in_=u_st[ti % 2][:, :]), 16)

                    for ti in range(IT):
                        gi = IT + ti

                        @blk.gpsimd
                        def _(g, ti=ti, gi=gi):
                            wt(g, tile_mm_last[gi - 2])
                            for q in range(4):
                                src = wuT.rearrange("(ht p) (it i) -> p ht it i", p=P, i=P)[
                                    :, 8 * q:8 * (q + 1), ti, :]
                                dst = w_sb[gi % 2][:, :].rearrange("p (ht i) -> p ht i", i=P)[
                                    :, 8 * q:8 * (q + 1), :]
                                ev_wld2[ti] = s_wld.inc(g.dma_start(out=dst, in_=src), 16)
                            if ti >= 2:
                                emit_uspill(g, ti - 2)

                        @blk.tensor
                        def _(t, ti=ti, gi=gi):
                            for c in range(4):
                                j = 4 * ti + c
                                if j == 0:
                                    wt(t, p1_sc_final)
                                if c == 0:
                                    wt(t, ev_wld2[ti])
                                    if ti >= 2:
                                        t.wait_ge(*ue_ev(ti - 2))
                                for ht in range(HT):
                                    instr = t.matmul(ps_u[ti % 2][:, c * 512:(c + 1) * 512],
                                                     w_sb[gi % 2][:, ht * P:(ht + 1) * P],
                                                     xlnT_sl(ht, c * 512, (c + 1) * 512),
                                                     start=(ht == 0), stop=(ht == HT - 1))
                                ev_mmu[j] = s_mm.inc(instr)
                            tile_mm_last[gi] = ev_mmu[4 * ti + 3]

                    @blk.gpsimd
                    def _(g):
                        emit_uspill(g, IT - 2)
                        emit_uspill(g, IT - 1)

                    sc_ptr = [0]
                    ve_ptr = [0]

                    def due_tile(slot):
                        return slot // 2

                    def v_emit_evacs(v, upto_ti):
                        while ve_ptr[0] < len(ve_tiles) and ve_tiles[ve_ptr[0]] <= upto_ti:
                            ti = ve_tiles[ve_ptr[0]]
                            wt(v, ev_mmu[4 * ti + 3])
                            if ti >= 2:
                                wt(v, ev_uspl[ti - 2])
                            s_ueve.inc(v.tensor_copy(u_st[ti % 2][:, :], ps_u[ti % 2][:, :]))
                            ve_ptr[0] += 1

                    def s_emit_evacs(sc, upto_ti):
                        while sc_ptr[0] < len(sc_tiles) and sc_tiles[sc_ptr[0]] <= upto_ti:
                            ti = sc_tiles[sc_ptr[0]]
                            wt(sc, ev_mmu[4 * ti + 3])
                            if ti >= 2:
                                wt(sc, ev_uspl[ti - 2])
                            s_uesc.inc(sc.activation(
                                out=u_st[ti % 2][:, :], in_=ps_u[ti % 2][:, :],
                                func=AF.Identity))
                            sc_ptr[0] += 1

                    for k in range(8):
                        @blk.sync
                        def _(sync, k=k):
                            if k == 0:
                                sync.wait_ge(*p1_spl2_final)
                                sync.wait_ge(*p1_spl_final)
                            if k >= 1:
                                wt(sync, ev_vt[k - 1])
                            ev_vl[k] = s_hld.inc(sync.dma_start(out=vbuf[:, :],
                                                                in_=sabs_sp[k, :, :]), 16)
                            if k >= 1:
                                wt(sync, ev_st_[k - 1])
                            ev_sl[k] = s_hld.inc(sync.dma_start(out=sbuf2[:, :],
                                                                in_=sabs_sp[8 + k, :, :]), 16)

                        @blk.vector
                        def _(v, k=k):
                            wt(v, ev_vl[k])
                            chain(v, s_ve2, v.memset(vst[:, 0:1], 0.0))
                            chain(v, s_ve2, v.memset(vst[:, 1:2], HI0))
                            for it_ in range(BIS_ITERS):
                                chain(v, s_ve2, v.tensor_scalar(out=vst[:, 2:3], in0=vst[:, 0:1],
                                                                scalar1=vst[:, 1:2], scalar2=0.5,
                                                                op0=ALU.add, op1=ALU.mult))
                                chain(v, s_ve2, v.tensor_copy(vbf[:, :], vst[:, 2:3]))
                                chain(v, s_ve2, v.tensor_copy(vst[:, 3:4], vbf[:, :]))
                                for q in range(4):
                                    acc_col = vst[:, 4:5] if q == 0 else vst[:, 5:6]
                                    chain(v, s_ve2, v.tensor_scalar(
                                        out=dummy[:, :], in0=vbuf[:, q * NQ:(q + 1) * NQ],
                                        scalar1=vst[:, 3:4], scalar2=0.0,
                                        op0=ALU.is_gt, op1=ALU.add, accum_out=acc_col))
                                    if q >= 1:
                                        chain(v, s_ve2, v.tensor_tensor(
                                            out=vst[:, 4:5], in0=vst[:, 4:5],
                                            in1=vst[:, 5:6], op=ALU.add))
                                chain(v, s_ve2, v.tensor_scalar(out=vgt8[:, :], in0=vst[:, 4:5],
                                                                scalar1=KKEEP, scalar2=None,
                                                                op0=ALU.is_gt))
                                chain(v, s_ve2, v.tensor_scalar(out=vle8[:, :], in0=vst[:, 4:5],
                                                                scalar1=KKEEP, scalar2=None,
                                                                op0=ALU.is_le))
                                chain(v, s_ve2, v.select(vst[:, 0:1], vgt8[:, :],
                                                         vst[:, 3:4], vst[:, 0:1]))
                                chain(v, s_ve2, v.select(vst[:, 1:2], vle8[:, :],
                                                         vst[:, 3:4], vst[:, 1:2]))
                                v_emit_evacs(v, due_tile(k * BIS_ITERS + it_))
                            ev_vt[k] = chain(v, s_ve2, v.tensor_copy(t_all[:, k:k + 1], vst[:, 1:2]))
                            s_bve.inc(v.tensor_copy(vst[:, 9:10], vst[:, 1:2]))

                        @blk.scalar
                        def _(sc, k=k):
                            wt(sc, ev_sl[k])
                            chain(sc, s_sc2, sc.memzero(sst[:, 0:1]))
                            chain(sc, s_sc2, sc.activation(out=sst[:, 1:2], in_=sst[:, 0:1],
                                                           func=AF.Identity, scale=0.0,
                                                           bias=cst[:, 1:2]))
                            for it_ in range(BIS_ITERS):
                                chain(sc, s_sc2, sc.activation(out=sst[:, 2:3], in_=sst[:, 1:2],
                                                               func=AF.Identity, scale=0.5))
                                chain(sc, s_sc2, sc.activation(out=sst[:, 3:4], in_=sst[:, 0:1],
                                                               func=AF.Identity, scale=0.5,
                                                               bias=sst[:, 2:3]))
                                chain(sc, s_sc2, sc.activation(out=sbf1[:, :], in_=sst[:, 3:4],
                                                               func=AF.Identity))
                                chain(sc, s_sc2, sc.activation(out=sst[:, 4:5], in_=sbf1[:, :],
                                                               func=AF.Identity))
                                chain(sc, s_sc2, sc.activation(out=sst[:, 5:6], in_=sst[:, 4:5],
                                                               func=AF.Identity, scale=-SNAP))
                                for q in range(4):
                                    chain(sc, s_sc2, sc.activation(
                                        out=dummy[:, :], in_=sbuf2[:, q * NQ:(q + 1) * NQ],
                                        func=AF.Sign, bias=sst[:, 5:6],
                                        accum_out=sst[:, 12 + q:13 + q]))
                                chain(sc, s_sc2, sc.activation(out=sst[:, 6:7], in_=sst[:, 12:13],
                                                               func=AF.Identity, bias=sst[:, 13:14]))
                                chain(sc, s_sc2, sc.activation(out=sst[:, 2:3], in_=sst[:, 14:15],
                                                               func=AF.Identity, bias=sst[:, 15:16]))
                                chain(sc, s_sc2, sc.activation(out=sst[:, 3:4], in_=sst[:, 2:3],
                                                               func=AF.Identity, bias=sst[:, 6:7]))
                                chain(sc, s_sc2, sc.activation(out=sst[:, 7:8], in_=sst[:, 3:4],
                                                               func=AF.Sign))
                                chain(sc, s_sc2, sc.activation(out=sst[:, 8:9], in_=sst[:, 7:8],
                                                               func=AF.Relu))
                                chain(sc, s_sc2, sc.activation(out=sst[:, 9:10], in_=sst[:, 8:9],
                                                               func=AF.Identity, scale=-1.0,
                                                               bias=cst[:, 0:1]))
                                chain(sc, s_sc2, sc.activation(out=sst[:, 10:11], in_=sst[:, 0:1],
                                                               func=AF.Identity, scale=-1.0,
                                                               bias=sst[:, 4:5]))
                                chain(sc, s_sc2, sc.activation(out=sst[:, 11:12], in_=sst[:, 1:2],
                                                               func=AF.Identity, scale=-1.0,
                                                               bias=sst[:, 4:5]))
                                chain(sc, s_sc2, sc.activation(out=sst[:, 0:1], in_=sst[:, 8:9],
                                                               func=AF.Identity, scale=sst[:, 10:11],
                                                               bias=sst[:, 0:1]))
                                chain(sc, s_sc2, sc.activation(out=sst[:, 1:2], in_=sst[:, 9:10],
                                                               func=AF.Identity, scale=sst[:, 11:12],
                                                               bias=sst[:, 1:2]))
                                s_emit_evacs(sc, due_tile(k * BIS_ITERS + it_))
                            ev_st_[k] = chain(sc, s_sc2, sc.activation(
                                out=t_all[:, 8 + k:9 + k], in_=sst[:, 0:1],
                                func=AF.Identity, scale=SNAP))
                            s_bsc.inc(sc.activation(out=sbf1[:, :], in_=sst[:, 1:2],
                                                    func=AF.Identity))

                    @blk.vector
                    def _(v):
                        v_emit_evacs(v, IT - 1)

                    @blk.scalar
                    def _(sc):
                        s_emit_evacs(sc, IT - 1)

                    @blk.sync
                    def _(sync):
                        sync.wait_ge(s_bve.h, 8)
                        sync.wait_ge(s_bsc.h, 8)
                        sync.dma_start(out=tflat_sp.rearrange("(k p) -> p k", p=P),
                                       in_=t_all[:, :],
                                       allow_slow_non_contiguous=True).then_inc(s_spl.h, 16)
                        s_spl.v += 16

            p23_mm_final = (s_mm.h, s_mm.v)
            p23_sc2_final = (s_sc2.h, s_sc2.v)
            p23_ve2_final = (s_ve2.h, s_ve2.v)
            p23_spl_final = (s_spl.h, s_spl.v)
            p23_uspl_final = (s_uspl.h, s_uspl.v)
            ue_sc_total = (s_uesc.h, len(sc_tiles))
            ue_ve_total = (s_ueve.h, len(ve_tiles))

            if phases < 3:
                _finish_partial(nc, S, out)
                return nc

        with ExitStack() as c4:
            gcol = c4.enter_context(nc.sbuf_tensor("gcol", [P, IT * 512], BF16))
            s_stage = [c4.enter_context(nc.sbuf_tensor(f"s_stage{j}", [P, 8 * 512], BF16))
                       for j in range(2)]
            abs_st4 = c4.enter_context(nc.sbuf_tensor("abs_st4", [P, 8 * 512], BF16))
            t_rep = c4.enter_context(nc.sbuf_tensor("t_rep", [P, NTOK], F32))
            tfl = t_rep[0:1, :]
            wdr = c4.enter_context(nc.sbuf_tensor("wdr", [P, 4 * 8 * P], BF16))
            yT_sb = [c4.enter_context(nc.sbuf_tensor(f"yT_sb{j}", [P, 512], BF16)) for j in range(2)]
            x2y_sb = c4.enter_context(nc.sbuf_tensor("x2y_sb", [P, 4 * D], BF16))
            x_st4 = c4.enter_context(nc.sbuf_tensor("x_st4", [P, D], F32))
            st4 = c4.enter_context(nc.sbuf_tensor("st4", [P, 8], F32))
            x2ln_st = [c4.enter_context(nc.sbuf_tensor(f"x2ln_st{j}", [P, D], BF16))
                       for j in range(2)]
            xlt_st = [c4.enter_context(nc.sbuf_tensor(f"xlt_st{j}", [P, 8 * P], BF16)) for j in range(2)]
            ps_yT = [c4.enter_context(nc.psum_tensor(f"ps_yT{j}", [P, 512], F32)) for j in range(2)]
            ps_trY = [c4.enter_context(nc.psum_tensor(f"ps_trY{j}", [P, 512], BF16)) for j in range(2)]
            ps_trL = [c4.enter_context(nc.psum_tensor(f"ps_trL{j}", [P, 1024], BF16)) for j in range(2)]

            it_groups = [(g * 8, min(8, IT - g * 8)) for g in range(NGR)]

            with nc.Block() as blk:
                ev_pc_done = (s_pc.h, s_pc.v)
                ev_tfl = [None]
                ev_trep = {}
                ev_uld = {}
                ev_sld = {}
                ev_absg = {}
                ev_apply = {}
                ev_ring = {}
                ev_mmg = {}
                ev_yev = {}
                ev_trY = {}
                ev_chev = {}
                ev_rmw = {}
                ev_rb = {}
                ev_norm4 = {}
                ev_trL = {}
                ev_xlte = {}
                ev_spl_xlt = {}
                s_trl = S["trl"]

                @blk.gpsimd
                def _(g):
                    g.wait_ge(*ev_pc_done)
                    g.nop().then_inc(S["pcg"].h, 1)
                S["pcg"].v += 1
                ev_pc_all = (S["pcg"].h, S["pcg"].v)

                @blk.sync
                def _(sync):
                    sync.wait_ge(*p23_spl_final)
                    ev_tfl[0] = s_hld.inc(sync.dma_start(
                        out=tfl, in_=tflat_sp.rearrange("(a b) -> a b", a=1)), 16)

                for c in range(4):
                    @blk.tensor
                    def _(t, c=c):
                        if c == 0:
                            wt(t, ev_tfl[0])
                            t.wait_ge(*ue_sc_total)
                            t.wait_ge(*ue_ve_total)
                            wt(t, p23_ve2_final)
                        if c >= 2:
                            wt(t, ev_trep[c - 2])
                        instr = t.matmul(ps_yT[c % 2][:, :], ones1[:, :],
                                         tfl[:, c * 512:(c + 1) * 512], start=True, stop=True)
                        ev_trep[(c, "mm")] = s_tr.inc(instr)

                    @blk.vector
                    def _(v, c=c):
                        wt(v, ev_trep[(c, "mm")])
                        ev_trep[c] = s_ve.inc(v.tensor_copy(
                            t_rep[:, c * 512:(c + 1) * 512], ps_yT[c % 2][:, :]))

                def emit_rmw(b, nt):
                    @blk.gpsimd
                    def _(g, b=b, nt=nt):
                        if b == 0 and nt == 0:
                            g.wait_ge(S["xcp"].h, S["xcp"].v)
                        wt(g, ev_chev[b * HT + HT - 1])
                        for c0 in range(0, D, 1024):
                            ev_rmw[(b, nt)] = s_acc.inc(g.dma_start(
                                out=out[(b * 4 + nt) * P:(b * 4 + nt + 1) * P,
                                        c0:c0 + 1024],
                                in_=x2y_sb[:, nt * D + c0:nt * D + c0 + 1024],
                                accum_op=ALU.add), 16)

                def emit_rb(b, nt):
                    m = b * 4 + nt

                    @blk.sync
                    def _(sync, b=b, nt=nt, m=m):
                        if m >= 1:
                            wt(sync, ev_norm4[divmod(m - 1, 4)])
                        wt(sync, ev_rmw[(b, nt)])
                        ev_rb[(b, nt)] = s_hld.inc(sync.dma_start(
                            out=x_st4[:, :], in_=out[m * P:(m + 1) * P, :]), 16)

                def emit_lnchain(b, nt):
                    m = b * 4 + nt
                    extra = (s_trl.h, 4 * (m - 2) + 4) if m >= 2 else None
                    ev_norm4[(b, nt)], _ = emit_ln(
                        blk, x_st4[:, :], x2ln_st[m % 2][:, :], st4,
                        x2ln_st[m % 2][:, :], ev_rb[(b, nt)], ev_extra_sc=extra)

                def emit_trl(b, nt, g2):
                    m = b * 4 + nt
                    k = m * 4 + g2

                    @blk.tensor
                    def _(t, m=m, g2=g2, k=k, b=b, nt=nt):
                        if g2 == 0:
                            wt(t, ev_norm4[(b, nt)])
                        if k >= 2:
                            wt(t, ev_xlte[divmod(k - 2, 4)])
                        for h in range(8):
                            ht2 = 8 * g2 + h
                            instr = t.transpose(ps_trL[g2 % 2][:, h * P:(h + 1) * P],
                                                x2ln_st[m % 2][:, ht2 * P:(ht2 + 1) * P],
                                                ident_bf[:, :])
                        ev_trL[(m, g2)] = s_trl.inc(instr)

                    @blk.scalar
                    def _(sc, m=m, g2=g2, k=k):
                        wt(sc, ev_trL[(m, g2)])
                        if k >= 2:
                            wt(sc, ev_spl_xlt[divmod(k - 2, 4)])
                        ev_xlte[(m, g2)] = s_sc.inc(sc.activation(
                            out=xlt_st[k % 2][:, :], in_=ps_trL[g2 % 2][:, :],
                            func=AF.Identity))

                    @blk.sync
                    def _(sync, m=m, g2=g2, k=k):
                        wt(sync, ev_xlte[(m, g2)])
                        src = xlt_st[k % 2][:, :].rearrange("p (h n) -> p h n", n=P)
                        dst = x2lnT_sp[8 * g2:8 * (g2 + 1), :,
                                       m * P:(m + 1) * P].rearrange("ht p n -> p ht n")
                        ev_spl_xlt[(m, g2)] = s_spl2.inc(
                            sync.dma_start(out=dst, in_=src), 16)

                def deferred_sched(pb):
                    sched = {}

                    def at(ht, fn):
                        sched.setdefault(ht, []).append(fn)

                    at(1, lambda: emit_rb(pb, 0))
                    at(2, lambda: emit_lnchain(pb, 0))
                    at(3, lambda: emit_rb(pb, 1))
                    at(4, lambda: emit_lnchain(pb, 1))
                    at(8, lambda: emit_rb(pb, 2))
                    for g2 in range(4):
                        at(8 + g2, lambda g2=g2: emit_trl(pb, 0, g2))
                    at(12, lambda: emit_lnchain(pb, 2))
                    for g2 in range(4):
                        at(12 + g2, lambda g2=g2: emit_trl(pb, 1, g2))
                    at(15, lambda: emit_rb(pb, 3))
                    for g2 in range(4):
                        at(16 + g2, lambda g2=g2: emit_trl(pb, 2, g2))
                    at(17, lambda: emit_lnchain(pb, 3))
                    for g2 in range(4):
                        at(20 + g2, lambda g2=g2: emit_trl(pb, 3, g2))
                    return sched

                def emit_apply(b, g2):
                    it0, n_it = it_groups[g2]
                    ga = b * NGR + g2

                    @blk.sync
                    def _(sync, b=b, g2=g2, it0=it0, n_it=n_it, ga=ga):
                        if ga == 0:
                            sync.wait_ge(*ue_sc_total)
                            sync.wait_ge(*ue_ve_total)
                            sync.wait_ge(*p23_sc2_final)
                            sync.wait_ge(*p23_ve2_final)
                            sync.wait_ge(*p23_mm_final)
                            sync.wait_ge(*p23_uspl_final)
                        if b > 0:
                            wt(sync, ev_mmg[(b - 1, HT - 1, g2)])
                        src = uT_sp[it0:it0 + n_it, :, b * NBTOK:(b + 1) * NBTOK].rearrange(
                            "it p n -> p it n")
                        dst = gcol[:, it0 * 512:(it0 + n_it) * 512].rearrange(
                            "p (it n) -> p it n", n=512)
                        ev_uld[ga] = s_hld.inc(sync.dma_start(out=dst, in_=src), 16)
                        if ga >= 2:
                            wt(sync, ev_apply[ga - 2])
                        src2 = sT_sp[it0:it0 + n_it, :, b * NBTOK:(b + 1) * NBTOK].rearrange(
                            "it p n -> p it n")
                        dst2 = s_stage[ga % 2][:, 0:n_it * 512].rearrange(
                            "p (it n) -> p it n", n=512)
                        ev_sld[ga] = s_hld.inc(sync.dma_start(out=dst2, in_=src2), 16)

                    @blk.vector
                    def _(v, b=b, g2=g2, it0=it0, n_it=n_it, ga=ga):
                        wt(v, ev_uld[ga])
                        wt(v, ev_sld[ga])
                        gsl = gcol[:, it0 * 512:(it0 + n_it) * 512]
                        chain(v, s_ve2, v.tensor_tensor(
                            out=gsl, in0=gsl, in1=s_stage[ga % 2][:, 0:n_it * 512],
                            op=ALU.mult))
                        chain(v, s_ve2, v.scalar_tensor_tensor(
                            out=abs_st4[:, 0:n_it * 512],
                            in0=s_stage[ga % 2][:, 0:n_it * 512], scalar=-1.0,
                            in1=s_stage[ga % 2][:, 0:n_it * 512],
                            op0=ALU.mult, op1=ALU.max))
                        for jj in range(n_it):
                            chain(v, s_ve2, v.tensor_tensor(
                                out=s_stage[ga % 2][:, jj * 512:(jj + 1) * 512],
                                in0=abs_st4[:, jj * 512:(jj + 1) * 512],
                                in1=t_rep[:, b * NBTOK:(b + 1) * NBTOK], op=ALU.is_gt))
                        ev_apply[ga] = chain(v, s_ve2, v.tensor_tensor(
                            out=gsl, in0=gsl, in1=s_stage[ga % 2][:, 0:n_it * 512],
                            op=ALU.mult))

                def emit_try_chev(b, ht):
                    hi = b * HT + ht

                    @blk.tensor
                    def _(t, b=b, ht=ht, hi=hi):
                        wt(t, ev_yev[hi])
                        if hi >= 2:
                            wt(t, ev_chev[hi - 2])
                        for c in range(4):
                            instr = t.transpose(ps_trY[ht % 2][:, c * P:(c + 1) * P],
                                                yT_sb[ht % 2][:, c * P:(c + 1) * P],
                                                ident_bf[:, :])
                        ev_trY[hi] = s_tr.inc(instr)

                    @blk.vector
                    def _(v, b=b, ht=ht, hi=hi):
                        wt(v, ev_trY[hi])
                        if ht == 0 and b > 0:
                            wt(v, ev_rmw[(b - 1, 3)])
                        for c in range(4):
                            instr = v.tensor_copy(
                                x2y_sb[:, c * D + ht * P: c * D + (ht + 1) * P],
                                ps_trY[ht % 2][:, c * P:(c + 1) * P])
                        ev_chev[hi] = s_ve.inc(instr)

                N_EARLY = 4

                for b in range(NB):
                    sched = deferred_sched(b - 1) if b > 0 else {}
                    for g2 in range(N_EARLY if b > 0 else 0, NGR):
                        emit_apply(b, g2)

                    for ht in range(HT):
                        hi = b * HT + ht

                        for g2, (it0, n_it) in enumerate(it_groups):
                            @blk.gpsimd
                            def _(g, b=b, ht=ht, hi=hi, g2=g2, it0=it0, n_it=n_it):
                                gi = hi * NGR + g2
                                if gi == 0:
                                    g.wait_ge(*ev_pc_all)
                                    g.wait_ge(*p23_mm_final)
                                if gi >= 4:
                                    pb, rem = divmod(gi - 4, HT * NGR)
                                    pht, pgg = divmod(rem, NGR)
                                    wt(g, ev_mmg[(pb, pht, pgg)])
                                src = wd_bf[ht, it0:it0 + n_it, :, :].rearrange("it p h -> p it h")
                                dst = wdr[:, :].rearrange("p (r i h) -> p r i h", r=4, h=P)[
                                    :, gi % 4, 0:n_it, :]
                                ev_ring[gi] = s_wld.inc(g.dma_start(out=dst, in_=src), 16)

                            @blk.tensor
                            def _(t, b=b, ht=ht, hi=hi, g2=g2, it0=it0, n_it=n_it):
                                gi = hi * NGR + g2
                                if g2 == 0 and hi >= 2:
                                    wt(t, ev_yev[hi - 2])
                                wt(t, ev_ring[gi])
                                if ht == 0:
                                    wt(t, ev_apply[b * NGR + g2])
                                for jj in range(n_it):
                                    it = it0 + jj
                                    instr = t.matmul(
                                        ps_yT[ht % 2][:, :],
                                        wdr[:, :].rearrange("p (r i h) -> p r i h", r=4, h=P)[
                                            :, gi % 4, jj, :],
                                        gcol[:, it * 512:(it + 1) * 512],
                                        start=(it == 0), stop=(it == IT - 1))
                                ev_mmg[(b, ht, g2)] = s_mm.inc(instr)

                        if ht == 0 and b > 0:
                            for nt in range(4):
                                emit_rmw(b - 1, nt)
                        if ht == HT - 1 and b + 1 < NB:
                            for g2 in range(N_EARLY):
                                emit_apply(b + 1, g2)

                        @blk.scalar
                        def _(sc, b=b, ht=ht, hi=hi):
                            wt(sc, ev_mmg[(b, ht, NGR - 1)])
                            if hi >= 2:
                                wt(sc, ev_trY[hi - 2])
                            ev_yev[hi] = s_sc.inc(sc.activation(out=yT_sb[ht % 2][:, :],
                                                                in_=ps_yT[ht % 2][:, :],
                                                                func=AF.Identity))

                        if ht >= 1:
                            emit_try_chev(b, ht - 1)

                        for fn in sched.get(ht, []):
                            fn()

                    emit_try_chev(b, HT - 1)

                lb = NB - 1
                for nt in range(4):
                    emit_rmw(lb, nt)
                emit_rb(lb, 0)
                emit_lnchain(lb, 0)
                emit_rb(lb, 1)
                emit_lnchain(lb, 1)
                for g2 in range(4):
                    emit_trl(lb, 0, g2)
                emit_rb(lb, 2)
                emit_lnchain(lb, 2)
                for g2 in range(4):
                    emit_trl(lb, 1, g2)
                emit_rb(lb, 3)
                emit_lnchain(lb, 3)
                for g2 in range(4):
                    emit_trl(lb, 2, g2)
                for g2 in range(4):
                    emit_trl(lb, 3, g2)

        p4_finals = [(s_mm.h, s_mm.v), (s_tr.h, s_tr.v), (s_sc.h, s_sc.v),
                     (s_ve.h, s_ve.v), (s_ve2.h, s_ve2.v), (s_hld.h, s_hld.v),
                     (s_spl.h, s_spl.v), (s_spl2.h, s_spl2.v),
                     (s_trl.h, s_trl.v)]

        if phases < 5:
            _finish_partial(nc, S, out)
            return nc

        with ExitStack() as c5:
            xlnT2 = c5.enter_context(nc.sbuf_tensor("xlnT2", [P, HT * NTOK], BF16))
            bt_sb = c5.enter_context(nc.sbuf_tensor("bt_sb", [P, D], BF16))
            bs_sb = c5.enter_context(nc.sbuf_tensor("bs_sb", [P, HT], F32))
            hw_sb = c5.enter_context(nc.sbuf_tensor("hw_sb", [P, NTOK], BF16))

            ev5 = {}
            with ExitStack() as c5a:
                wr_sb = c5a.enter_context(nc.sbuf_tensor("wr_sb", [P, HT * 8], BF16))
                at_sb = c5a.enter_context(nc.sbuf_tensor("at_sb", [P, HT * P], BF16))
                rep_sb = c5a.enter_context(nc.sbuf_tensor("rep_sb", [8, P], F32))
                lgT_sb = c5a.enter_context(nc.sbuf_tensor("lgT_sb", [8, NTOK], F32))
                lg_sb = c5a.enter_context(nc.sbuf_tensor("lg_sb", [P, NT * 8], F32))
                m8_sb = c5a.enter_context(nc.sbuf_tensor("m8_sb", [P, NT * 8], F32))
                dcol = c5a.enter_context(nc.sbuf_tensor("dcol", [P, NT], F32))
                w1c = c5a.enter_context(nc.sbuf_tensor("w1c", [P, NT], F32))
                w1s = c5a.enter_context(nc.sbuf_tensor("w1s", [P, NT], F32))
                w2s = c5a.enter_context(nc.sbuf_tensor("w2s", [P, NT], F32))
                wdc = c5a.enter_context(nc.sbuf_tensor("wdc", [P, NT], F32))
                eq1 = c5a.enter_context(nc.sbuf_tensor("eq1", [P, 8], F32))
                eq2 = c5a.enter_context(nc.sbuf_tensor("eq2", [P, 8], F32))
                t2s = c5a.enter_context(nc.sbuf_tensor("t2s", [P, 8], F32))
                comb = c5a.enter_context(nc.sbuf_tensor("comb", [P, NT * 8], F32))
                ctT_sb = c5a.enter_context(nc.sbuf_tensor("ctT_sb", [8, NTOK], F32))
                hT_sb = c5a.enter_context(nc.sbuf_tensor("hT_sb", [P, NTOK], F32))
                ps8 = c5a.enter_context(nc.psum_tensor("ps8", [8, NTOK], F32))
                ps_big = c5a.enter_context(nc.psum_tensor("ps_big", [P, NTOK], F32))

                with nc.Block() as blk:
                    ev_x2l = [None] * HT

                    @blk.sync
                    def _(sync):
                        sync.wait_ge(s_spl2.h, s_spl2.v)
                        for ht in range(HT):
                            ev_x2l[ht] = s_hld.inc(sync.dma_start(
                                out=xlnT2[:, ht * NTOK:(ht + 1) * NTOK],
                                in_=x2lnT_sp[ht, :, :]), 16)
                        ev5["bs"] = s_hld.inc(sync.dma_start(out=bs_sb[:, :], in_=bsbi[:, :]), 16)
                        ev5["rep"] = s_hld.inc(sync.dma_start(out=rep_sb[:, :], in_=repmi[:, :]), 16)

                    @blk.gpsimd
                    def _(g):
                        for ev in p4_finals:
                            g.wait_ge(*ev)
                        for q in range(4):
                            src = wrT.rearrange("(ht p) e -> p ht e", p=P)[:, 8 * q:8 * (q + 1), :]
                            dst = wr_sb[:, :].rearrange("p (ht e) -> p ht e", e=8)[
                                :, 8 * q:8 * (q + 1), :]
                            ev5["wr"] = s_wld.inc(g.dma_start(out=dst, in_=src), 16)
                        for q in range(4):
                            src = atT.rearrange("(ht p) e -> p ht e", p=P)[:, 8 * q:8 * (q + 1), :]
                            dst = at_sb[:, :].rearrange("p (ht e) -> p ht e", e=P)[
                                :, 8 * q:8 * (q + 1), :]
                            ev5["at"] = s_wld.inc(g.dma_start(out=dst, in_=src), 16)
                        ev5["bt"] = s_wld.inc(g.dma_start(out=bt_sb[:, :], in_=btT[:, :]), 16)

                    @blk.tensor
                    def _(t):
                        wt(t, ev5["wr"])
                        for ht in range(HT):
                            wt(t, ev_x2l[ht])
                            for c in range(4):
                                instr = t.matmul(ps8[:, c * 512:(c + 1) * 512],
                                                 wr_sb[:, ht * 8:(ht + 1) * 8],
                                                 xlnT2[:, ht * NTOK + c * 512: ht * NTOK + (c + 1) * 512],
                                                 start=(ht == 0), stop=(ht == HT - 1))
                        ev5["lgT_mm"] = s_mm.inc(instr)

                    @blk.vector
                    def _(v):
                        wt(v, ev5["lgT_mm"])
                        ev5["lgT_ev"] = s_ve.inc(v.tensor_copy(lgT_sb[:, :], ps8[:, :]))

                    @blk.tensor
                    def _(t):
                        wt(t, ev5["lgT_ev"])
                        for nt in range(NT):
                            instr = t.transpose(ps_big[:, nt * 8:(nt + 1) * 8],
                                                lgT_sb[:, nt * P:(nt + 1) * P],
                                                ident_f[0:8, 0:8])
                        ev5["lg_tr"] = s_tr.inc(instr)

                    @blk.vector
                    def _(v):
                        wt(v, ev5["lg_tr"])
                        ev5["lg_ev"] = s_ve.inc(v.tensor_copy(lg_sb[:, :], ps_big[:, 0:NT * 8]))

                    @blk.tensor
                    def _(t):
                        wt(t, ev5["lg_ev"])
                        wt(t, ev5["at"])
                        for ht in range(HT):
                            for c in range(4):
                                instr = t.matmul(ps_big[:, c * 512:(c + 1) * 512],
                                                 at_sb[:, ht * P:(ht + 1) * P],
                                                 xlnT2[:, ht * NTOK + c * 512: ht * NTOK + (c + 1) * 512],
                                                 start=(ht == 0), stop=(ht == HT - 1))
                        ev5["hT_mm"] = s_mm.inc(instr)

                    @blk.vector
                    def _(v):
                        wt(v, ev5["hT_mm"])
                        ev5["hT_ev"] = s_ve.inc(v.tensor_copy(hT_sb[:, :], ps_big[:, :]))
                        wt(v, ev5["lg_ev"])
                        for nt in range(NT):
                            chain(v, s_ve2, v.max(out=m8_sb[:, nt * 8:(nt + 1) * 8],
                                                  in_=lg_sb[:, nt * 8:(nt + 1) * 8]))
                        l1v = m8_sb[:, :].rearrange("p (nt e) -> p nt e", e=8)[:, :, 0:1]
                        l2v = m8_sb[:, :].rearrange("p (nt e) -> p nt e", e=8)[:, :, 1:2]
                        ev5["dcol"] = chain(v, s_ve2, v.tensor_tensor(
                            out=dcol[:, :].rearrange("p (nt o) -> p nt o", o=1),
                            in0=l1v, in1=l2v, op=ALU.subtract))

                    @blk.scalar
                    def _(sc):
                        wt(sc, ev5["dcol"])
                        ev5["w1"] = s_sc.inc(sc.activation(out=w1c[:, :], in_=dcol[:, :],
                                                           func=AF.Sigmoid))

                    @blk.vector
                    def _(v):
                        wt(v, ev5["w1"])
                        chain(v, s_ve2, v.tensor_scalar(out=w1s[:, :], in0=w1c[:, :],
                                                        scalar1=2.0, scalar2=None, op0=ALU.mult))
                        chain(v, s_ve2, v.tensor_scalar(out=w2s[:, :], in0=w1c[:, :],
                                                        scalar1=-2.0, scalar2=2.0,
                                                        op0=ALU.mult, op1=ALU.add))
                        chain(v, s_ve2, v.tensor_tensor(out=wdc[:, :], in0=w1s[:, :],
                                                        in1=w2s[:, :], op=ALU.subtract))
                        for nt in range(NT):
                            lg_sl = lg_sb[:, nt * 8:(nt + 1) * 8]
                            chain(v, s_ve2, v.tensor_scalar(out=eq1[:, :], in0=lg_sl,
                                                            scalar1=m8_sb[:, nt * 8:nt * 8 + 1],
                                                            scalar2=None, op0=ALU.is_equal))
                            chain(v, s_ve2, v.tensor_scalar(out=eq2[:, :], in0=lg_sl,
                                                            scalar1=m8_sb[:, nt * 8 + 1:nt * 8 + 2],
                                                            scalar2=None, op0=ALU.is_equal))
                            chain(v, s_ve2, v.tensor_scalar(out=t2s[:, :], in0=eq2[:, :],
                                                            scalar1=w2s[:, nt:nt + 1],
                                                            scalar2=None, op0=ALU.mult))
                            chain(v, s_ve2, v.scalar_tensor_tensor(
                                out=comb[:, nt * 8:(nt + 1) * 8], in0=eq1[:, :],
                                scalar=w1s[:, nt:nt + 1], in1=t2s[:, :],
                                op0=ALU.mult, op1=ALU.add))
                        ev5["comb"] = chain(v, s_ve2, v.tensor_copy(comb[:, 0:1], comb[:, 0:1]))

                    @blk.tensor
                    def _(t):
                        wt(t, ev5["comb"])
                        for nt in range(NT):
                            instr = t.transpose(ps8[:, nt * P:(nt + 1) * P],
                                                comb[:, nt * 8:(nt + 1) * 8], ident_f[:, :])
                        ev5["ct_tr"] = s_tr.inc(instr)

                    @blk.vector
                    def _(v):
                        wt(v, ev5["ct_tr"])
                        ev5["ct_ev"] = s_ve.inc(v.tensor_copy(ctT_sb[:, :], ps8[:, :]))

                    @blk.tensor
                    def _(t):
                        wt(t, ev5["ct_ev"])
                        wt(t, ev5["hT_ev"])
                        wt(t, ev5["rep"])
                        for c in range(4):
                            instr = t.matmul(ps_big[:, c * 512:(c + 1) * 512], rep_sb[:, :],
                                             ctT_sb[:, c * 512:(c + 1) * 512],
                                             start=True, stop=True)
                        ev5["crep"] = s_tr.inc(instr)

                    @blk.vector
                    def _(v):
                        wt(v, ev5["crep"])
                        ev5["hw"] = chain(v, s_ve2, v.tensor_tensor(
                            out=hw_sb[:, :], in0=hT_sb[:, :], in1=ps_big[:, :], op=ALU.mult))

            with ExitStack() as c5b:
                wbr = c5b.enter_context(nc.sbuf_tensor("wbr", [P, 8 * 8 * P], BF16))
                y2b = [c5b.enter_context(nc.sbuf_tensor(f"y2b{j}", [P, 1024], F32)) for j in range(3)]
                o_st = [c5b.enter_context(nc.sbuf_tensor(f"o_st{j}", [P, 1024], F32)) for j in range(4)]
                ps_y2 = [c5b.enter_context(nc.psum_tensor(f"ps_y2{j}", [P, 1024], F32)) for j in range(3)]
                ps_trO = c5b.enter_context(nc.psum_tensor("ps_trO", [P, 1024], F32))

                with nc.Block() as blk:
                    ev_ring5 = {}
                    ev_mm5 = {}
                    ev_y2e = {}
                    ev_tro = {}
                    ev_oev = {}
                    ev_oacc = {}
                    ev_out_all = (s_acc.h, s_acc.v)

                    def emit_tro(k):
                        @blk.tensor
                        def _(t, k=k):
                            wt(t, ev_y2e[k])
                            if k >= 1:
                                wt(t, ev_oev[k - 1])
                            for c in range(8):
                                instr = t.transpose(ps_trO[:, c * P:(c + 1) * P],
                                                    y2b[k % 3][:, c * P:(c + 1) * P],
                                                    ident_f[:, :])
                            ev_tro[k] = s_tr.inc(instr)

                        @blk.vector
                        def _(v, k=k):
                            wt(v, ev_tro[k])
                            if k >= 4:
                                wt(v, ev_oacc[k - 4])
                            ev_oev[k] = s_ve.inc(v.tensor_copy(o_st[k % 4][:, :], ps_trO[:, :]))

                        @blk.gpsimd
                        def _(g, k=k):
                            ho, nh = divmod(k, 2)
                            wt(g, ev_oev[k])
                            src = o_st[k % 4][:, :].rearrange("p (nt n) -> p nt n", n=P)
                            dst = out.rearrange("(nt p) (ho h) -> p nt ho h", p=P, h=P)[
                                :, nh * 8:(nh + 1) * 8, ho, :]
                            ev_oacc[k] = s_acc.inc(g.dma_start(out=dst, in_=src,
                                                               accum_op=ALU.add), 16)

                    for ho in range(HT):
                        @blk.gpsimd
                        def _(g, ho=ho):
                            if ho == 0:
                                g.wait_ge(*ev_out_all)
                            for q in range(4):
                                gi = ho * 4 + q
                                if gi >= 8:
                                    pho, pq = divmod(gi - 8, 4)
                                    wt(g, ev_mm5[(pho, 1)])
                                src = wbT.rearrange("(hi p) (ho h) -> p hi ho h", p=P, h=P)[
                                    :, 8 * q:8 * (q + 1), ho, :]
                                dst = wbr[:, :].rearrange("p (r i h) -> p r i h", r=8, h=P)[
                                    :, gi % 8, :, :]
                                ev_ring5[gi] = s_wld.inc(g.dma_start(out=dst, in_=src), 16)

                        for nh in range(2):
                            @blk.tensor
                            def _(t, ho=ho, nh=nh):
                                k = ho * 2 + nh
                                if k >= 3:
                                    wt(t, ev_y2e[k - 3])
                                if nh == 0:
                                    wt(t, ev_ring5[ho * 4 + 3])
                                if k == 0:
                                    wt(t, ev5["hw"])
                                    wt(t, ev5["bt"])
                                for hi in range(HT):
                                    for c in range(2):
                                        instr = t.matmul(
                                            ps_y2[k % 3][:, c * 512:(c + 1) * 512],
                                            wbr[:, :].rearrange("p (r i h) -> p r i h", r=8, h=P)[
                                                :, (ho * 4 + hi // 8) % 8, hi % 8, :],
                                            xlnT2[:, hi * NTOK + nh * 1024 + c * 512:
                                                  hi * NTOK + nh * 1024 + (c + 1) * 512],
                                            start=(hi == 0), stop=False)
                                for c in range(2):
                                    instr = t.matmul(
                                        ps_y2[k % 3][:, c * 512:(c + 1) * 512],
                                        bt_sb[:, ho * P:(ho + 1) * P],
                                        hw_sb[:, nh * 1024 + c * 512: nh * 1024 + (c + 1) * 512],
                                        start=False, stop=True)
                                ev_mm5[(ho, nh)] = s_mm.inc(instr)

                            @blk.scalar
                            def _(sc, ho=ho, nh=nh):
                                k = ho * 2 + nh
                                wt(sc, ev_mm5[(ho, nh)])
                                if k >= 3:
                                    wt(sc, ev_tro[k - 3])
                                ev_y2e[k] = s_sc.inc(sc.activation(
                                    out=y2b[k % 3][:, :], in_=ps_y2[k % 3][:, :], func=AF.Identity,
                                    bias=bs_sb[:, ho:ho + 1]))

                            if ho * 2 + nh >= 1:
                                emit_tro(ho * 2 + nh - 1)

                    emit_tro(HT * 2 - 1)

        with nc.Block() as blk:
            @blk.gpsimd
            def _(g):
                g.wait_ge(s_acc.h, s_acc.v)

            @blk.sync
            def _(sync):
                sync.wait_ge(s_spl.h, s_spl.v)
                sync.wait_ge(s_spl2.h, s_spl2.v)

    return nc


def _finish_partial(nc, S, out):
    with nc.Block() as blk:
        @blk.gpsimd
        def _(g):
            g.wait_ge(S["wld"].h, S["wld"].v)
            g.wait_ge(S["pc"].h, S["pc"].v)
            g.wait_ge(S["acc"].h, S["acc"].v)

        @blk.sync
        def _(sync):
            sync.wait_ge(S["hld"].h, S["hld"].v)
            sync.wait_ge(S["spl"].h, S["spl"].v)
            sync.wait_ge(S["spl2"].h, S["spl2"].v)


def _prep(inputs):
    f32 = np.float32
    x = np.ascontiguousarray(np.asarray(inputs["L0_attn_x"], dtype=f32).reshape(NTOK * 8, D))
    wgT_ = np.ascontiguousarray(np.asarray(inputs["L0_attn_w_gate"], dtype=f32).T)
    wuT_ = np.ascontiguousarray(np.asarray(inputs["L0_attn_w_up"], dtype=f32).T)
    wdT_ = np.ascontiguousarray(np.asarray(inputs["L0_attn_w_down"], dtype=f32).T)
    wbT_ = np.ascontiguousarray(np.asarray(inputs["L0_ffn_W_base"], dtype=f32).T)
    wrT_ = np.ascontiguousarray(np.asarray(inputs["L0_ffn_W_router"], dtype=f32).T)
    A = np.asarray(inputs["L0_ffn_A"], dtype=f32)
    B = np.asarray(inputs["L0_ffn_B"], dtype=f32)
    atT_ = np.ascontiguousarray(np.transpose(A, (2, 0, 1)).reshape(D, P))
    btT_ = np.ascontiguousarray(np.transpose(B, (0, 2, 1)).reshape(P, D))
    b_base = np.asarray(inputs["L0_ffn_b_base"], dtype=f32)
    bsb_ = np.ascontiguousarray(b_base.reshape(HT, P).T)
    idbf_ = np.eye(P, dtype=ml_dtypes.bfloat16)
    idf_ = np.eye(P, dtype=f32)
    repm_ = np.zeros((8, P), dtype=f32)
    for e in range(8):
        repm_[e, e * 16:(e + 1) * 16] = 1.0
    shared = {"wgT": wgT_, "wuT": wuT_, "wdT": wdT_, "wbT": wbT_, "wrT": wrT_,
              "atT": atT_, "btT": btT_, "bsbi": bsb_, "idbf": idbf_, "idf": idf_,
              "repmi": repm_}
    in_maps = []
    for c in range(8):
        m = dict(shared)
        m["x"] = x[c * NTOK:(c + 1) * NTOK]
        in_maps.append(m)
    return in_maps


_LAST_RESULTS = None


def kernel(**inputs):
    global _LAST_RESULTS
    in_maps = _prep(inputs)
    nc = build_program(phases=_PHASES, dbg=_DBG)
    res = run_bass_kernel_spmd(nc, in_maps, core_ids=list(range(8)))
    _LAST_RESULTS = res
    outs = [r["out"] for r in res.results]
    return np.concatenate(outs, axis=0).reshape(4, 4096, D)


# version 08
# speedup vs baseline: 1.1777x; 1.0801x over previous
import numpy as np
import ml_dtypes
from contextlib import ExitStack

import concourse.bass as bass
import concourse.mybir as mybir
from concourse.bass_utils import run_bass_kernel_spmd

F32 = mybir.dt.float32
BF16 = mybir.dt.bfloat16
AF = mybir.ActivationFunctionType
ALU = mybir.AluOpType
AX = mybir.AxisListType

P = 128
NTOK = 2048
NT = NTOK // P
D = 4096
HT = D // P
I = 11008
IT = I // P
NHALF = I // 2
NQ = I // 4
KKEEP = float(I // 2)
EPS = 1e-5
BIS_ITERS = 14
HI0 = 8.0
SNAP = 1.0 + 2.0 ** -9
NB = 4
NBTOK = NTOK // NB
NGR = (IT + 7) // 8
NCH = 4 * IT
TR_TILE = 76

_PHASES = 6
_DBG = False


class Sem:
    def __init__(self, h):
        self.h = h
        self.v = 0

    def inc(self, instr, n=1):
        instr.then_inc(self.h, n)
        self.v += n
        return (self.h, self.v)


def chain(eng, sem, instr):
    ev = sem.inc(instr, 1)
    eng.wait_ge(*ev)
    return ev


def wt(eng, ev):
    if ev is not None:
        eng.wait_ge(*ev)


def build_program(phases=6, dbg=False):
    nc = bass.Bass("TRN2", target_bir_lowering=False)

    x_in = nc.dram_tensor("x", [NTOK, D], F32, kind="ExternalInput").ap()
    wgs = nc.dram_tensor("wgs", [IT, P, HT * P], F32, kind="ExternalInput").ap()
    wus = nc.dram_tensor("wus", [IT, P, HT * P], F32, kind="ExternalInput").ap()
    wds = nc.dram_tensor("wds", [HT, NGR, P, 8 * P], F32, kind="ExternalInput").ap()
    wbs = nc.dram_tensor("wbs", [HT, 4, P, 8 * P], F32, kind="ExternalInput").ap()
    wrT = nc.dram_tensor("wrT", [D, 8], F32, kind="ExternalInput").ap()
    atT = nc.dram_tensor("atT", [D, P], F32, kind="ExternalInput").ap()
    btT = nc.dram_tensor("btT", [P, D], F32, kind="ExternalInput").ap()
    bsbi = nc.dram_tensor("bsbi", [P, HT], F32, kind="ExternalInput").ap()
    idbf = nc.dram_tensor("idbf", [P, P], BF16, kind="ExternalInput").ap()
    idf = nc.dram_tensor("idf", [P, P], F32, kind="ExternalInput").ap()
    repmi = nc.dram_tensor("repmi", [8, P], F32, kind="ExternalInput").ap()
    out = nc.dram_tensor("out", [NTOK, D], F32, kind="ExternalOutput").ap()

    sk = "ExternalOutput" if dbg else "Internal"
    sT_sp = nc.dram_tensor("sT_sp", [IT, P, NTOK], BF16, kind=sk).ap()
    sabs_sp = nc.dram_tensor("sabs_sp", [NT, P, I], BF16, kind=sk).ap()
    uT_sp = nc.dram_tensor("uT_sp", [IT, P, NTOK], BF16, kind=sk).ap()
    x2lnT_sp = nc.dram_tensor("x2lnT_sp", [HT, P, NTOK], BF16, kind=sk).ap()
    wd_bf = nc.dram_tensor("wd_bf", [HT, NGR, P, 8 * P], BF16, kind="Internal").ap()
    tflat_sp = nc.dram_tensor("tflat_sp", [NTOK], F32, kind="Internal").ap()

    with ExitStack() as ctx:
        ident_bf = ctx.enter_context(nc.sbuf_tensor("ident_bf", [P, P], BF16))
        ident_f = ctx.enter_context(nc.sbuf_tensor("ident_f", [P, P], F32))
        t_all = ctx.enter_context(nc.sbuf_tensor("t_all", [P, NT], F32))
        ones1 = ctx.enter_context(nc.sbuf_tensor("ones1", [1, P], F32))
        cst = ctx.enter_context(nc.sbuf_tensor("cst", [P, 2], F32))

        S = {}
        for name in ["hld", "spl", "spl2", "wld", "pc", "mm", "tr", "sc", "ve",
                     "sc2", "ve2", "acc", "uesc", "ueve", "bsc", "bve", "uspl",
                     "pcg", "xcp", "trl"]:
            S[name] = Sem(ctx.enter_context(nc.semaphore("s_" + name)))
        s_hld, s_spl, s_spl2, s_wld, s_pc = S["hld"], S["spl"], S["spl2"], S["wld"], S["pc"]
        s_mm, s_tr, s_sc, s_ve = S["mm"], S["tr"], S["sc"], S["ve"]
        s_sc2, s_ve2, s_acc = S["sc2"], S["ve2"], S["acc"]
        s_uesc, s_ueve = S["uesc"], S["ueve"]
        s_bsc, s_bve = S["bsc"], S["bve"]
        s_uspl = S["uspl"]

        sc_tiles = [q for q in range(IT) if q % 2 == 0]
        ve_tiles = [q for q in range(IT) if q % 2 == 1]

        def ue_ev(ti):
            if ti % 2 == 0:
                return (s_uesc.h, sc_tiles.index(ti) + 1)
            return (s_ueve.h, ve_tiles.index(ti) + 1)

        def emit_ln(blk, x_ap, xsq_ap, st, out_bf_ap, ev_in_ready, ev_extra_sc=None):
            evs = {}

            @blk.scalar
            def _(sc):
                wt(sc, ev_in_ready)
                if ev_extra_sc is not None:
                    wt(sc, ev_extra_sc)
                evs["sq"] = s_sc.inc(sc.activation(out=xsq_ap, in_=x_ap, func=AF.Square,
                                                   accum_out=st[:, 2:3]))

            @blk.vector
            def _(v):
                wt(v, ev_in_ready)
                chain(v, s_ve2, v.reduce_sum(st[:, 0:1], x_ap, axis=AX.X))
                chain(v, s_ve2, v.tensor_scalar(out=st[:, 3:4], in0=st[:, 0:1],
                                                scalar1=1.0 / D, scalar2=None, op0=ALU.mult))
                wt(v, evs["sq"])
                chain(v, s_ve2, v.tensor_scalar(out=st[:, 4:5], in0=st[:, 2:3],
                                                scalar1=1.0 / D, scalar2=None, op0=ALU.mult))
                chain(v, s_ve2, v.tensor_tensor(out=st[:, 1:2], in0=st[:, 3:4],
                                                in1=st[:, 3:4], op=ALU.mult))
                chain(v, s_ve2, v.tensor_tensor(out=st[:, 5:6], in0=st[:, 4:5],
                                                in1=st[:, 1:2], op=ALU.subtract))
                evs["var"] = chain(v, s_ve2, v.tensor_scalar(out=st[:, 5:6], in0=st[:, 5:6],
                                                             scalar1=EPS, scalar2=None, op0=ALU.add))

            @blk.scalar
            def _(sc):
                wt(sc, evs["var"])
                evs["sqrt"] = s_sc.inc(sc.activation(out=st[:, 6:7], in_=st[:, 5:6], func=AF.Sqrt))

            @blk.vector
            def _(v):
                wt(v, evs["sqrt"])
                chain(v, s_ve2, v.reciprocal(st[:, 6:7], st[:, 6:7]))
                evs["rb"] = chain(v, s_ve2, v.scalar_tensor_tensor(
                    out=st[:, 7:8], in0=st[:, 3:4], scalar=-1.0, in1=st[:, 6:7],
                    op0=ALU.mult, op1=ALU.mult))

            @blk.scalar
            def _(sc):
                wt(sc, evs["rb"])
                evs["norm"] = s_sc.inc(sc.activation(out=out_bf_ap, in_=x_ap, func=AF.Identity,
                                                     scale=st[:, 6:7], bias=st[:, 7:8]))

            return evs["norm"], evs["rb"]

        tile_mm_last = {}

        with ExitStack() as ctx03:
            xlnT = ctx03.enter_context(nc.sbuf_tensor("xlnT", [P, HT * NTOK], BF16))
            w_sb = [ctx03.enter_context(nc.sbuf_tensor(f"w_sb{j}", [P, HT * P], BF16))
                    for j in range(2)]

            def xlnT_sl(ht, c0, c1):
                return xlnT[:, ht * NTOK + c0: ht * NTOK + c1]

            with ExitStack() as c0:
                x_st = [c0.enter_context(nc.sbuf_tensor(f"x_st{j}", [P, D], F32)) for j in range(2)]
                st0 = c0.enter_context(nc.sbuf_tensor("st0", [P, 8], F32))
                xln_st = [c0.enter_context(nc.sbuf_tensor(f"xln_st{j}", [P, D], BF16)) for j in range(2)]
                ps_trX = [c0.enter_context(nc.psum_tensor(f"ps_trX{j}", [P, 1024], BF16)) for j in range(2)]

                with nc.Block() as blk:
                    @blk.sync
                    def _(sync):
                        sync.dma_start(out=ident_bf[:, :], in_=idbf[:, :]).then_inc(s_hld.h, 16)
                        sync.dma_start(out=ident_f[:, :], in_=idf[:, :]).then_inc(s_hld.h, 16)
                    s_hld.v += 32

                    @blk.gpsimd
                    def _(g):
                        g.memset(ones1[:, :], 1.0)
                        g.memset(cst[:, 0:1], 1.0)
                        g.memset(cst[:, 1:2], HI0)
                        for nt in range(NT):
                            S["xcp"].inc(g.dma_start(
                                out=out[nt * P:(nt + 1) * P, :],
                                in_=x_in[nt * P:(nt + 1) * P, :]), 16)

                    ev_norm = [None] * NT
                    ev_ldx = [None] * NT
                    ev_tr = {}
                    ev_evac = {}
                    for nt in range(NT):
                        @blk.sync
                        def _(sync, nt=nt):
                            if nt >= 2:
                                wt(sync, ev_norm[nt - 2])
                            ev_ldx[nt] = s_hld.inc(sync.dma_start(
                                out=x_st[nt % 2][:, :], in_=x_in[nt * P:(nt + 1) * P, :]), 16)

                        ev_norm[nt], _rb = emit_ln(
                            blk, x_st[nt % 2][:, :], xln_st[nt % 2][:, :], st0,
                            xln_st[nt % 2][:, :], ev_ldx[nt],
                            ev_extra_sc=(ev_tr[4 * (nt - 2) + 3] if nt >= 2 else None))

                        for g2 in range(4):
                            @blk.tensor
                            def _(t, nt=nt, g2=g2):
                                k = 4 * nt + g2
                                if g2 == 0:
                                    wt(t, ev_norm[nt])
                                if k >= 2:
                                    wt(t, ev_evac[k - 2])
                                for h in range(8):
                                    ht = 8 * g2 + h
                                    instr = t.transpose(ps_trX[g2 % 2][:, h * P:(h + 1) * P],
                                                        xln_st[nt % 2][:, ht * P:(ht + 1) * P],
                                                        ident_bf[:, :])
                                ev_tr[k] = s_tr.inc(instr)

                            @blk.vector
                            def _(v, nt=nt, g2=g2):
                                k = 4 * nt + g2
                                wt(v, ev_tr[k])
                                dst = xlnT[:, :].rearrange("p (ht n) -> p ht n", n=NTOK)[
                                    :, 8 * g2:8 * (g2 + 1), nt * P:(nt + 1) * P]
                                src = ps_trX[g2 % 2][:, :].rearrange("p (h n) -> p h n", n=P)
                                ev_evac[k] = s_ve.inc(v.tensor_copy(dst, src))

            ev_p0_done = (s_ve.h, s_ve.v)

            if phases < 1:
                _finish_partial(nc, S, out)
                return nc

            pc_jobs = []
            for ht in range(HT):
                for it0 in range(0, IT, 8):
                    pc_jobs.append((ht, it0, min(8, IT - it0)))

            with ExitStack() as c1:
                sT_st = [c1.enter_context(nc.sbuf_tensor(f"sT_st{j}", [P, NTOK], BF16))
                         for j in range(2)]
                ab_st = [c1.enter_context(nc.sbuf_tensor(f"ab_st{j}", [P, NTOK], BF16))
                         for j in range(2)]
                ps_g = [c1.enter_context(nc.psum_tensor(f"ps_g{j}", [P, 512], F32))
                        for j in range(4)]
                ps_trS = [c1.enter_context(nc.psum_tensor(f"ps_trS{j}", [P, 512], BF16))
                          for j in range(4)]

                with nc.Block() as blk:
                    ev_wld1 = {}
                    ev_mmc = {}
                    ev_silu = {}
                    ev_trc = {}
                    ev_absc = {}
                    ev_spl_sT = {}
                    ev_spl_ab = {}
                    pci = [0]

                    def emit_tr1(t, j):
                        ti, c = divmod(j, 4)
                        wt(t, ev_silu[j])
                        if j >= 4:
                            wt(t, ev_absc[j - 4])
                        for h in range(4):
                            tok = c * 4 + h
                            instr = t.transpose(ps_trS[j % 4][:, h * P:(h + 1) * P],
                                                sT_st[ti % 2][:, tok * P:(tok + 1) * P],
                                                ident_bf[:, :])
                        ev_trc[j] = s_tr.inc(instr)

                    def emit_abs1(sc, j):
                        ti, c = divmod(j, 4)
                        wt(sc, ev_trc[j])
                        if ti >= 2 and c == 0:
                            wt(sc, ev_spl_ab[ti - 2])
                        ev_absc[j] = s_sc.inc(sc.activation(
                            out=ab_st[ti % 2][:, c * 512:(c + 1) * 512],
                            in_=ps_trS[j % 4][:, :], func=AF.Abs))

                    for i in range(IT):
                        @blk.gpsimd
                        def _(g, i=i):
                            if i >= 2:
                                wt(g, tile_mm_last[i - 2])
                            ev_wld1[i] = s_wld.inc(g.dma_start(
                                out=w_sb[i % 2][:, :], in_=wgs[i, :, :]), 16)
                            n_pc = 5 if i < 8 else 4
                            for _k in range(n_pc):
                                if pci[0] < len(pc_jobs):
                                    htj, it0, n_it = pc_jobs[pci[0]]
                                    s_pc.inc(g.dma_start(
                                        out=wd_bf[htj, it0 // 8, :, 0:n_it * P],
                                        in_=wds[htj, it0 // 8, :, 0:n_it * P]), 16)
                                    pci[0] += 1

                        @blk.tensor
                        def _(t, i=i):
                            for c in range(4):
                                j = 4 * i + c
                                if j == 0:
                                    wt(t, ev_p0_done)
                                if c == 0:
                                    wt(t, ev_wld1[i])
                                if j >= 4:
                                    wt(t, ev_silu[j - 4])
                                for ht in range(HT):
                                    instr = t.matmul(ps_g[j % 4][:, :],
                                                     w_sb[i % 2][:, ht * P:(ht + 1) * P],
                                                     xlnT_sl(ht, c * 512, (c + 1) * 512),
                                                     start=(ht == 0), stop=(ht == HT - 1))
                                ev_mmc[j] = s_mm.inc(instr)
                                if i >= 1:
                                    emit_tr1(t, 4 * (i - 1) + c)
                            tile_mm_last[i] = ev_mmc[4 * i + 3]

                        @blk.scalar
                        def _(sc, i=i):
                            for c in range(4):
                                j = 4 * i + c
                                wt(sc, ev_mmc[j])
                                if i >= 2 and c == 0:
                                    wt(sc, ev_spl_sT[i - 2])
                                if j >= 8:
                                    wt(sc, ev_trc[j - 8])
                                ev_silu[j] = s_sc.inc(sc.activation(
                                    out=sT_st[i % 2][:, c * 512:(c + 1) * 512],
                                    in_=ps_g[j % 4][:, :], func=AF.Silu))
                                if i >= 1:
                                    emit_abs1(sc, 4 * (i - 1) + c)

                        def emit_ab_spill(sync, i):
                            wt(sync, ev_absc[4 * i + 3])
                            for half in range(2):
                                src = ab_st[i % 2][:, :].rearrange("p (nt n) -> p nt n", n=P)[
                                    :, 8 * half:8 * (half + 1), :]
                                dst = sabs_sp.rearrange("nt p (it i) -> p nt it i", i=P)[
                                    :, 8 * half:8 * (half + 1), i, :]
                                ev_spl_ab[i] = s_spl2.inc(sync.dma_start(out=dst, in_=src), 16)

                        @blk.sync
                        def _(sync, i=i, emit_ab_spill=emit_ab_spill):
                            wt(sync, ev_silu[4 * i + 3])
                            ev_spl_sT[i] = s_spl.inc(sync.dma_start(
                                out=sT_sp[i, :, :], in_=sT_st[i % 2][:, :]), 16)
                            if i >= 1:
                                emit_ab_spill(sync, i - 1)

                    @blk.tensor
                    def _(t):
                        for c in range(4):
                            emit_tr1(t, 4 * (IT - 1) + c)

                    @blk.scalar
                    def _(sc):
                        for c in range(4):
                            emit_abs1(sc, 4 * (IT - 1) + c)

                    @blk.sync
                    def _(sync):
                        emit_ab_spill(sync, IT - 1)

            p1_sc_final = (s_sc.h, s_sc.v)
            p1_spl_final = (s_spl.h, s_spl.v)
            p1_spl2_final = (s_spl2.h, s_spl2.v)

            if phases < 2:
                _finish_partial(nc, S, out)
                return nc

            with ExitStack() as c2:
                vbuf = c2.enter_context(nc.sbuf_tensor("vbuf", [P, I], BF16))
                sbuf2 = c2.enter_context(nc.sbuf_tensor("sbuf2", [P, I], BF16))
                dummy = c2.enter_context(nc.sbuf_tensor("bis_dummy", [P, NQ], BF16))
                u_st = [c2.enter_context(nc.sbuf_tensor(f"u_st{j}", [P, NTOK], BF16))
                        for j in range(2)]
                vst = c2.enter_context(nc.sbuf_tensor("vst", [P, 10], F32))
                sst = c2.enter_context(nc.sbuf_tensor("sst", [P, 16], F32))
                vbf = c2.enter_context(nc.sbuf_tensor("vbf", [P, 1], BF16))
                vgt8 = c2.enter_context(nc.sbuf_tensor("vgt8", [P, 1], mybir.dt.uint8))
                vle8 = c2.enter_context(nc.sbuf_tensor("vle8", [P, 1], mybir.dt.uint8))
                sbf1 = c2.enter_context(nc.sbuf_tensor("sbf1", [P, 1], BF16))
                ps_u = [c2.enter_context(nc.psum_tensor(f"ps_u{j}", [P, 2048], F32))
                        for j in range(2)]

                with nc.Block() as blk:
                    ev_wld2 = {}
                    ev_mmu = {}
                    ev_uspl = {}
                    ev_vl = [None] * 8
                    ev_sl = [None] * 8
                    ev_vt = [None] * 8
                    ev_st_ = [None] * 8

                    def emit_uspill(g, ti):
                        g.wait_ge(*ue_ev(ti))
                        ev_uspl[ti] = s_uspl.inc(g.dma_start(
                            out=uT_sp[ti, :, :], in_=u_st[ti % 2][:, :]), 16)

                    for ti in range(IT):
                        gi = IT + ti

                        @blk.gpsimd
                        def _(g, ti=ti, gi=gi):
                            wt(g, tile_mm_last[gi - 2])
                            ev_wld2[ti] = s_wld.inc(g.dma_start(
                                out=w_sb[gi % 2][:, :], in_=wus[ti, :, :]), 16)
                            if ti >= 2:
                                emit_uspill(g, ti - 2)

                        @blk.tensor
                        def _(t, ti=ti, gi=gi):
                            for c in range(4):
                                j = 4 * ti + c
                                if j == 0:
                                    wt(t, p1_sc_final)
                                if c == 0:
                                    wt(t, ev_wld2[ti])
                                    if ti >= 2:
                                        t.wait_ge(*ue_ev(ti - 2))
                                for ht in range(HT):
                                    instr = t.matmul(ps_u[ti % 2][:, c * 512:(c + 1) * 512],
                                                     w_sb[gi % 2][:, ht * P:(ht + 1) * P],
                                                     xlnT_sl(ht, c * 512, (c + 1) * 512),
                                                     start=(ht == 0), stop=(ht == HT - 1))
                                ev_mmu[j] = s_mm.inc(instr)
                            tile_mm_last[gi] = ev_mmu[4 * ti + 3]

                    @blk.gpsimd
                    def _(g):
                        emit_uspill(g, IT - 2)
                        emit_uspill(g, IT - 1)

                    sc_ptr = [0]
                    ve_ptr = [0]

                    def due_tile(slot):
                        return slot // 2

                    def v_emit_evacs(v, upto_ti):
                        while ve_ptr[0] < len(ve_tiles) and ve_tiles[ve_ptr[0]] <= upto_ti:
                            ti = ve_tiles[ve_ptr[0]]
                            wt(v, ev_mmu[4 * ti + 3])
                            if ti >= 2:
                                wt(v, ev_uspl[ti - 2])
                            s_ueve.inc(v.tensor_copy(u_st[ti % 2][:, :], ps_u[ti % 2][:, :]))
                            ve_ptr[0] += 1

                    def s_emit_evacs(sc, upto_ti):
                        while sc_ptr[0] < len(sc_tiles) and sc_tiles[sc_ptr[0]] <= upto_ti:
                            ti = sc_tiles[sc_ptr[0]]
                            wt(sc, ev_mmu[4 * ti + 3])
                            if ti >= 2:
                                wt(sc, ev_uspl[ti - 2])
                            s_uesc.inc(sc.activation(
                                out=u_st[ti % 2][:, :], in_=ps_u[ti % 2][:, :],
                                func=AF.Identity))
                            sc_ptr[0] += 1

                    for k in range(8):
                        @blk.sync
                        def _(sync, k=k):
                            if k == 0:
                                sync.wait_ge(*p1_spl2_final)
                                sync.wait_ge(*p1_spl_final)
                            if k >= 1:
                                wt(sync, ev_vt[k - 1])
                            ev_vl[k] = s_hld.inc(sync.dma_start(out=vbuf[:, :],
                                                                in_=sabs_sp[k, :, :]), 16)
                            if k >= 1:
                                wt(sync, ev_st_[k - 1])
                            ev_sl[k] = s_hld.inc(sync.dma_start(out=sbuf2[:, :],
                                                                in_=sabs_sp[8 + k, :, :]), 16)

                        @blk.vector
                        def _(v, k=k):
                            wt(v, ev_vl[k])
                            chain(v, s_ve2, v.memset(vst[:, 0:1], 0.0))
                            chain(v, s_ve2, v.memset(vst[:, 1:2], HI0))
                            for it_ in range(BIS_ITERS):
                                chain(v, s_ve2, v.tensor_scalar(out=vst[:, 2:3], in0=vst[:, 0:1],
                                                                scalar1=vst[:, 1:2], scalar2=0.5,
                                                                op0=ALU.add, op1=ALU.mult))
                                chain(v, s_ve2, v.tensor_copy(vbf[:, :], vst[:, 2:3]))
                                chain(v, s_ve2, v.tensor_copy(vst[:, 3:4], vbf[:, :]))
                                for q in range(4):
                                    acc_col = vst[:, 4:5] if q == 0 else vst[:, 5:6]
                                    chain(v, s_ve2, v.tensor_scalar(
                                        out=dummy[:, :], in0=vbuf[:, q * NQ:(q + 1) * NQ],
                                        scalar1=vst[:, 3:4], scalar2=0.0,
                                        op0=ALU.is_gt, op1=ALU.add, accum_out=acc_col))
                                    if q >= 1:
                                        chain(v, s_ve2, v.tensor_tensor(
                                            out=vst[:, 4:5], in0=vst[:, 4:5],
                                            in1=vst[:, 5:6], op=ALU.add))
                                chain(v, s_ve2, v.tensor_scalar(out=vgt8[:, :], in0=vst[:, 4:5],
                                                                scalar1=KKEEP, scalar2=None,
                                                                op0=ALU.is_gt))
                                chain(v, s_ve2, v.tensor_scalar(out=vle8[:, :], in0=vst[:, 4:5],
                                                                scalar1=KKEEP, scalar2=None,
                                                                op0=ALU.is_le))
                                chain(v, s_ve2, v.select(vst[:, 0:1], vgt8[:, :],
                                                         vst[:, 3:4], vst[:, 0:1]))
                                chain(v, s_ve2, v.select(vst[:, 1:2], vle8[:, :],
                                                         vst[:, 3:4], vst[:, 1:2]))
                                v_emit_evacs(v, due_tile(k * BIS_ITERS + it_))
                            ev_vt[k] = chain(v, s_ve2, v.tensor_copy(t_all[:, k:k + 1], vst[:, 1:2]))
                            s_bve.inc(v.tensor_copy(vst[:, 9:10], vst[:, 1:2]))

                        @blk.scalar
                        def _(sc, k=k):
                            wt(sc, ev_sl[k])
                            chain(sc, s_sc2, sc.memzero(sst[:, 0:1]))
                            chain(sc, s_sc2, sc.activation(out=sst[:, 1:2], in_=sst[:, 0:1],
                                                           func=AF.Identity, scale=0.0,
                                                           bias=cst[:, 1:2]))
                            for it_ in range(BIS_ITERS):
                                chain(sc, s_sc2, sc.activation(out=sst[:, 2:3], in_=sst[:, 1:2],
                                                               func=AF.Identity, scale=0.5))
                                chain(sc, s_sc2, sc.activation(out=sst[:, 3:4], in_=sst[:, 0:1],
                                                               func=AF.Identity, scale=0.5,
                                                               bias=sst[:, 2:3]))
                                chain(sc, s_sc2, sc.activation(out=sbf1[:, :], in_=sst[:, 3:4],
                                                               func=AF.Identity))
                                chain(sc, s_sc2, sc.activation(out=sst[:, 4:5], in_=sbf1[:, :],
                                                               func=AF.Identity))
                                chain(sc, s_sc2, sc.activation(out=sst[:, 5:6], in_=sst[:, 4:5],
                                                               func=AF.Identity, scale=-SNAP))
                                for q in range(4):
                                    chain(sc, s_sc2, sc.activation(
                                        out=dummy[:, :], in_=sbuf2[:, q * NQ:(q + 1) * NQ],
                                        func=AF.Sign, bias=sst[:, 5:6],
                                        accum_out=sst[:, 12 + q:13 + q]))
                                chain(sc, s_sc2, sc.activation(out=sst[:, 6:7], in_=sst[:, 12:13],
                                                               func=AF.Identity, bias=sst[:, 13:14]))
                                chain(sc, s_sc2, sc.activation(out=sst[:, 2:3], in_=sst[:, 14:15],
                                                               func=AF.Identity, bias=sst[:, 15:16]))
                                chain(sc, s_sc2, sc.activation(out=sst[:, 3:4], in_=sst[:, 2:3],
                                                               func=AF.Identity, bias=sst[:, 6:7]))
                                chain(sc, s_sc2, sc.activation(out=sst[:, 7:8], in_=sst[:, 3:4],
                                                               func=AF.Sign))
                                chain(sc, s_sc2, sc.activation(out=sst[:, 8:9], in_=sst[:, 7:8],
                                                               func=AF.Relu))
                                chain(sc, s_sc2, sc.activation(out=sst[:, 9:10], in_=sst[:, 8:9],
                                                               func=AF.Identity, scale=-1.0,
                                                               bias=cst[:, 0:1]))
                                chain(sc, s_sc2, sc.activation(out=sst[:, 10:11], in_=sst[:, 0:1],
                                                               func=AF.Identity, scale=-1.0,
                                                               bias=sst[:, 4:5]))
                                chain(sc, s_sc2, sc.activation(out=sst[:, 11:12], in_=sst[:, 1:2],
                                                               func=AF.Identity, scale=-1.0,
                                                               bias=sst[:, 4:5]))
                                chain(sc, s_sc2, sc.activation(out=sst[:, 0:1], in_=sst[:, 8:9],
                                                               func=AF.Identity, scale=sst[:, 10:11],
                                                               bias=sst[:, 0:1]))
                                chain(sc, s_sc2, sc.activation(out=sst[:, 1:2], in_=sst[:, 9:10],
                                                               func=AF.Identity, scale=sst[:, 11:12],
                                                               bias=sst[:, 1:2]))
                                s_emit_evacs(sc, due_tile(k * BIS_ITERS + it_))
                            ev_st_[k] = chain(sc, s_sc2, sc.activation(
                                out=t_all[:, 8 + k:9 + k], in_=sst[:, 0:1],
                                func=AF.Identity, scale=SNAP))
                            s_bsc.inc(sc.activation(out=sbf1[:, :], in_=sst[:, 1:2],
                                                    func=AF.Identity))

                    @blk.vector
                    def _(v):
                        v_emit_evacs(v, IT - 1)

                    @blk.scalar
                    def _(sc):
                        s_emit_evacs(sc, IT - 1)

                    @blk.sync
                    def _(sync):
                        sync.wait_ge(s_bve.h, 8)
                        sync.wait_ge(s_bsc.h, 8)
                        sync.dma_start(out=tflat_sp.rearrange("(k p) -> p k", p=P),
                                       in_=t_all[:, :],
                                       allow_slow_non_contiguous=True).then_inc(s_spl.h, 16)
                        s_spl.v += 16

            p23_mm_final = (s_mm.h, s_mm.v)
            p23_sc2_final = (s_sc2.h, s_sc2.v)
            p23_ve2_final = (s_ve2.h, s_ve2.v)
            p23_spl_final = (s_spl.h, s_spl.v)
            p23_uspl_final = (s_uspl.h, s_uspl.v)
            ue_sc_total = (s_uesc.h, len(sc_tiles))
            ue_ve_total = (s_ueve.h, len(ve_tiles))

            if phases < 3:
                _finish_partial(nc, S, out)
                return nc

        with ExitStack() as c4:
            gcol = c4.enter_context(nc.sbuf_tensor("gcol", [P, IT * 512], BF16))
            s_stage = [c4.enter_context(nc.sbuf_tensor(f"s_stage{j}", [P, 8 * 512], BF16))
                       for j in range(2)]
            abs_st4 = c4.enter_context(nc.sbuf_tensor("abs_st4", [P, 8 * 512], BF16))
            t_rep = c4.enter_context(nc.sbuf_tensor("t_rep", [P, NTOK], F32))
            tfl = t_rep[0:1, :]
            wdr = c4.enter_context(nc.sbuf_tensor("wdr", [P, 6 * 8 * P], BF16))
            yT_sb = [c4.enter_context(nc.sbuf_tensor(f"yT_sb{j}", [P, 512], BF16)) for j in range(2)]
            x2y_sb = c4.enter_context(nc.sbuf_tensor("x2y_sb", [P, 4 * D], BF16))
            x_st4 = c4.enter_context(nc.sbuf_tensor("x_st4", [P, D], F32))
            st4 = c4.enter_context(nc.sbuf_tensor("st4", [P, 8], F32))
            x2ln_st = [c4.enter_context(nc.sbuf_tensor(f"x2ln_st{j}", [P, D], BF16))
                       for j in range(2)]
            xlt_st = [c4.enter_context(nc.sbuf_tensor(f"xlt_st{j}", [P, 8 * P], BF16)) for j in range(2)]
            ps_yT = [c4.enter_context(nc.psum_tensor(f"ps_yT{j}", [P, 512], F32)) for j in range(2)]
            ps_trY = [c4.enter_context(nc.psum_tensor(f"ps_trY{j}", [P, 512], BF16)) for j in range(2)]
            ps_trL = [c4.enter_context(nc.psum_tensor(f"ps_trL{j}", [P, 1024], BF16)) for j in range(2)]

            it_groups = [(g * 8, min(8, IT - g * 8)) for g in range(NGR)]

            with nc.Block() as blk:
                ev_pc_done = (s_pc.h, s_pc.v)
                ev_tfl = [None]
                ev_trep = {}
                ev_uld = {}
                ev_sld = {}
                ev_absg = {}
                ev_apply = {}
                ev_ring = {}
                ev_mmg = {}
                ev_yev = {}
                ev_trY = {}
                ev_chev = {}
                ev_rmw = {}
                ev_rb = {}
                ev_norm4 = {}
                ev_trL = {}
                ev_xlte = {}
                ev_spl_xlt = {}
                s_trl = S["trl"]

                @blk.gpsimd
                def _(g):
                    g.wait_ge(*ev_pc_done)
                    g.nop().then_inc(S["pcg"].h, 1)
                S["pcg"].v += 1
                ev_pc_all = (S["pcg"].h, S["pcg"].v)

                @blk.sync
                def _(sync):
                    sync.wait_ge(*p23_spl_final)
                    ev_tfl[0] = s_hld.inc(sync.dma_start(
                        out=tfl, in_=tflat_sp.rearrange("(a b) -> a b", a=1)), 16)

                for c in range(4):
                    @blk.tensor
                    def _(t, c=c):
                        if c == 0:
                            wt(t, ev_tfl[0])
                            t.wait_ge(*ue_sc_total)
                            t.wait_ge(*ue_ve_total)
                            wt(t, p23_ve2_final)
                        if c >= 2:
                            wt(t, ev_trep[c - 2])
                        instr = t.matmul(ps_yT[c % 2][:, :], ones1[:, :],
                                         tfl[:, c * 512:(c + 1) * 512], start=True, stop=True)
                        ev_trep[(c, "mm")] = s_tr.inc(instr)

                    @blk.vector
                    def _(v, c=c):
                        wt(v, ev_trep[(c, "mm")])
                        ev_trep[c] = s_ve.inc(v.tensor_copy(
                            t_rep[:, c * 512:(c + 1) * 512], ps_yT[c % 2][:, :]))

                def emit_rmw(b, nt):
                    @blk.gpsimd
                    def _(g, b=b, nt=nt):
                        if b == 0 and nt == 0:
                            g.wait_ge(S["xcp"].h, S["xcp"].v)
                        wt(g, ev_chev[b * HT + HT - 1])
                        for c0 in range(0, D, 1024):
                            ev_rmw[(b, nt)] = s_acc.inc(g.dma_start(
                                out=out[(b * 4 + nt) * P:(b * 4 + nt + 1) * P,
                                        c0:c0 + 1024],
                                in_=x2y_sb[:, nt * D + c0:nt * D + c0 + 1024],
                                accum_op=ALU.add), 16)

                def emit_rb(b, nt):
                    m = b * 4 + nt

                    @blk.sync
                    def _(sync, b=b, nt=nt, m=m):
                        if m >= 1:
                            wt(sync, ev_norm4[divmod(m - 1, 4)])
                        wt(sync, ev_rmw[(b, nt)])
                        ev_rb[(b, nt)] = s_hld.inc(sync.dma_start(
                            out=x_st4[:, :], in_=out[m * P:(m + 1) * P, :]), 16)

                def emit_lnchain(b, nt):
                    m = b * 4 + nt
                    extra = (s_trl.h, 4 * (m - 2) + 4) if m >= 2 else None
                    ev_norm4[(b, nt)], _ = emit_ln(
                        blk, x_st4[:, :], x2ln_st[m % 2][:, :], st4,
                        x2ln_st[m % 2][:, :], ev_rb[(b, nt)], ev_extra_sc=extra)

                def emit_trl(b, nt, g2):
                    m = b * 4 + nt
                    k = m * 4 + g2

                    @blk.tensor
                    def _(t, m=m, g2=g2, k=k, b=b, nt=nt):
                        if g2 == 0:
                            wt(t, ev_norm4[(b, nt)])
                        if k >= 2:
                            wt(t, ev_xlte[divmod(k - 2, 4)])
                        for h in range(8):
                            ht2 = 8 * g2 + h
                            instr = t.transpose(ps_trL[g2 % 2][:, h * P:(h + 1) * P],
                                                x2ln_st[m % 2][:, ht2 * P:(ht2 + 1) * P],
                                                ident_bf[:, :])
                        ev_trL[(m, g2)] = s_trl.inc(instr)

                    @blk.scalar
                    def _(sc, m=m, g2=g2, k=k):
                        wt(sc, ev_trL[(m, g2)])
                        if k >= 2:
                            wt(sc, ev_spl_xlt[divmod(k - 2, 4)])
                        ev_xlte[(m, g2)] = s_sc.inc(sc.activation(
                            out=xlt_st[k % 2][:, :], in_=ps_trL[g2 % 2][:, :],
                            func=AF.Identity))

                    @blk.sync
                    def _(sync, m=m, g2=g2, k=k):
                        wt(sync, ev_xlte[(m, g2)])
                        src = xlt_st[k % 2][:, :].rearrange("p (h n) -> p h n", n=P)
                        dst = x2lnT_sp[8 * g2:8 * (g2 + 1), :,
                                       m * P:(m + 1) * P].rearrange("ht p n -> p ht n")
                        ev_spl_xlt[(m, g2)] = s_spl2.inc(
                            sync.dma_start(out=dst, in_=src), 16)

                def deferred_sched(pb):
                    sched = {}

                    def at(ht, fn):
                        sched.setdefault(ht, []).append(fn)

                    at(1, lambda: emit_rb(pb, 0))
                    at(2, lambda: emit_lnchain(pb, 0))
                    at(3, lambda: emit_rb(pb, 1))
                    at(4, lambda: emit_lnchain(pb, 1))
                    at(8, lambda: emit_rb(pb, 2))
                    for g2 in range(4):
                        at(8 + g2, lambda g2=g2: emit_trl(pb, 0, g2))
                    at(12, lambda: emit_lnchain(pb, 2))
                    for g2 in range(4):
                        at(12 + g2, lambda g2=g2: emit_trl(pb, 1, g2))
                    at(15, lambda: emit_rb(pb, 3))
                    for g2 in range(4):
                        at(16 + g2, lambda g2=g2: emit_trl(pb, 2, g2))
                    at(17, lambda: emit_lnchain(pb, 3))
                    for g2 in range(4):
                        at(20 + g2, lambda g2=g2: emit_trl(pb, 3, g2))
                    return sched

                def emit_apply(b, g2):
                    it0, n_it = it_groups[g2]
                    ga = b * NGR + g2

                    @blk.sync
                    def _(sync, b=b, g2=g2, it0=it0, n_it=n_it, ga=ga):
                        if ga == 0:
                            sync.wait_ge(*ue_sc_total)
                            sync.wait_ge(*ue_ve_total)
                            sync.wait_ge(*p23_sc2_final)
                            sync.wait_ge(*p23_ve2_final)
                            sync.wait_ge(*p23_mm_final)
                            sync.wait_ge(*p23_uspl_final)
                        if b > 0:
                            wt(sync, ev_mmg[(b - 1, HT - 1, g2)])
                        src = uT_sp[it0:it0 + n_it, :, b * NBTOK:(b + 1) * NBTOK].rearrange(
                            "it p n -> p it n")
                        dst = gcol[:, it0 * 512:(it0 + n_it) * 512].rearrange(
                            "p (it n) -> p it n", n=512)
                        ev_uld[ga] = s_hld.inc(sync.dma_start(out=dst, in_=src), 16)
                        if ga >= 2:
                            wt(sync, ev_apply[ga - 2])
                        src2 = sT_sp[it0:it0 + n_it, :, b * NBTOK:(b + 1) * NBTOK].rearrange(
                            "it p n -> p it n")
                        dst2 = s_stage[ga % 2][:, 0:n_it * 512].rearrange(
                            "p (it n) -> p it n", n=512)
                        ev_sld[ga] = s_hld.inc(sync.dma_start(out=dst2, in_=src2), 16)

                    @blk.vector
                    def _(v, b=b, g2=g2, it0=it0, n_it=n_it, ga=ga):
                        wt(v, ev_uld[ga])
                        wt(v, ev_sld[ga])
                        gsl = gcol[:, it0 * 512:(it0 + n_it) * 512]
                        chain(v, s_ve2, v.tensor_tensor(
                            out=gsl, in0=gsl, in1=s_stage[ga % 2][:, 0:n_it * 512],
                            op=ALU.mult))
                        chain(v, s_ve2, v.scalar_tensor_tensor(
                            out=abs_st4[:, 0:n_it * 512],
                            in0=s_stage[ga % 2][:, 0:n_it * 512], scalar=-1.0,
                            in1=s_stage[ga % 2][:, 0:n_it * 512],
                            op0=ALU.mult, op1=ALU.max))
                        for jj in range(n_it):
                            chain(v, s_ve2, v.tensor_tensor(
                                out=s_stage[ga % 2][:, jj * 512:(jj + 1) * 512],
                                in0=abs_st4[:, jj * 512:(jj + 1) * 512],
                                in1=t_rep[:, b * NBTOK:(b + 1) * NBTOK], op=ALU.is_gt))
                        ev_apply[ga] = chain(v, s_ve2, v.tensor_tensor(
                            out=gsl, in0=gsl, in1=s_stage[ga % 2][:, 0:n_it * 512],
                            op=ALU.mult))

                def emit_try_chev(b, ht):
                    hi = b * HT + ht

                    @blk.tensor
                    def _(t, b=b, ht=ht, hi=hi):
                        wt(t, ev_yev[hi])
                        if hi >= 2:
                            wt(t, ev_chev[hi - 2])
                        for c in range(4):
                            instr = t.transpose(ps_trY[ht % 2][:, c * P:(c + 1) * P],
                                                yT_sb[ht % 2][:, c * P:(c + 1) * P],
                                                ident_bf[:, :])
                        ev_trY[hi] = s_tr.inc(instr)

                    @blk.vector
                    def _(v, b=b, ht=ht, hi=hi):
                        wt(v, ev_trY[hi])
                        if ht == 0 and b > 0:
                            wt(v, ev_rmw[(b - 1, 3)])
                        for c in range(4):
                            instr = v.tensor_copy(
                                x2y_sb[:, c * D + ht * P: c * D + (ht + 1) * P],
                                ps_trY[ht % 2][:, c * P:(c + 1) * P])
                        ev_chev[hi] = s_ve.inc(instr)

                N_EARLY = 4

                for b in range(NB):
                    sched = deferred_sched(b - 1) if b > 0 else {}
                    for g2 in range(N_EARLY if b > 0 else 0, NGR):
                        emit_apply(b, g2)

                    for ht in range(HT):
                        hi = b * HT + ht

                        for g2, (it0, n_it) in enumerate(it_groups):
                            @blk.gpsimd
                            def _(g, b=b, ht=ht, hi=hi, g2=g2, it0=it0, n_it=n_it):
                                gi = hi * NGR + g2
                                if gi == 0:
                                    g.wait_ge(*ev_pc_all)
                                    g.wait_ge(*p23_mm_final)
                                if gi >= 6:
                                    pb, rem = divmod(gi - 6, HT * NGR)
                                    pht, pgg = divmod(rem, NGR)
                                    wt(g, ev_mmg[(pb, pht, pgg)])
                                src = wd_bf[ht, g2, :, 0:n_it * P]
                                dst = wdr[:, gi % 6 * (8 * P):gi % 6 * (8 * P) + n_it * P]
                                ev_ring[gi] = s_wld.inc(g.dma_start(out=dst, in_=src), 16)

                            @blk.tensor
                            def _(t, b=b, ht=ht, hi=hi, g2=g2, it0=it0, n_it=n_it):
                                gi = hi * NGR + g2
                                if g2 == 0 and hi >= 2:
                                    wt(t, ev_yev[hi - 2])
                                wt(t, ev_ring[gi])
                                if ht == 0:
                                    wt(t, ev_apply[b * NGR + g2])
                                for jj in range(n_it):
                                    it = it0 + jj
                                    instr = t.matmul(
                                        ps_yT[ht % 2][:, :],
                                        wdr[:, gi % 6 * (8 * P) + jj * P:
                                            gi % 6 * (8 * P) + (jj + 1) * P],
                                        gcol[:, it * 512:(it + 1) * 512],
                                        start=(it == 0), stop=(it == IT - 1))
                                ev_mmg[(b, ht, g2)] = s_mm.inc(instr)

                        if ht == 0 and b > 0:
                            for nt in range(4):
                                emit_rmw(b - 1, nt)
                        if ht == HT - 1 and b + 1 < NB:
                            for g2 in range(N_EARLY):
                                emit_apply(b + 1, g2)

                        @blk.scalar
                        def _(sc, b=b, ht=ht, hi=hi):
                            wt(sc, ev_mmg[(b, ht, NGR - 1)])
                            if hi >= 2:
                                wt(sc, ev_trY[hi - 2])
                            ev_yev[hi] = s_sc.inc(sc.activation(out=yT_sb[ht % 2][:, :],
                                                                in_=ps_yT[ht % 2][:, :],
                                                                func=AF.Identity))

                        if ht >= 1:
                            emit_try_chev(b, ht - 1)

                        for fn in sched.get(ht, []):
                            fn()

                    emit_try_chev(b, HT - 1)

                lb = NB - 1
                for nt in range(4):
                    emit_rmw(lb, nt)
                emit_rb(lb, 0)
                emit_lnchain(lb, 0)
                emit_rb(lb, 1)
                emit_lnchain(lb, 1)
                for g2 in range(4):
                    emit_trl(lb, 0, g2)
                emit_rb(lb, 2)
                emit_lnchain(lb, 2)
                for g2 in range(4):
                    emit_trl(lb, 1, g2)
                emit_rb(lb, 3)
                emit_lnchain(lb, 3)
                for g2 in range(4):
                    emit_trl(lb, 2, g2)
                for g2 in range(4):
                    emit_trl(lb, 3, g2)

        p4_finals = [(s_mm.h, s_mm.v), (s_tr.h, s_tr.v), (s_sc.h, s_sc.v),
                     (s_ve.h, s_ve.v), (s_ve2.h, s_ve2.v), (s_hld.h, s_hld.v),
                     (s_spl.h, s_spl.v), (s_spl2.h, s_spl2.v),
                     (s_trl.h, s_trl.v)]

        if phases < 5:
            _finish_partial(nc, S, out)
            return nc

        with ExitStack() as c5:
            xlnT2 = c5.enter_context(nc.sbuf_tensor("xlnT2", [P, HT * NTOK], BF16))
            bt_sb = c5.enter_context(nc.sbuf_tensor("bt_sb", [P, D], BF16))
            bs_sb = c5.enter_context(nc.sbuf_tensor("bs_sb", [P, HT], F32))
            hw_sb = c5.enter_context(nc.sbuf_tensor("hw_sb", [P, NTOK], BF16))

            ev5 = {}
            with ExitStack() as c5a:
                wr_sb = c5a.enter_context(nc.sbuf_tensor("wr_sb", [P, HT * 8], BF16))
                at_sb = c5a.enter_context(nc.sbuf_tensor("at_sb", [P, HT * P], BF16))
                rep_sb = c5a.enter_context(nc.sbuf_tensor("rep_sb", [8, P], F32))
                lgT_sb = c5a.enter_context(nc.sbuf_tensor("lgT_sb", [8, NTOK], F32))
                lg_sb = c5a.enter_context(nc.sbuf_tensor("lg_sb", [P, NT * 8], F32))
                m8_sb = c5a.enter_context(nc.sbuf_tensor("m8_sb", [P, NT * 8], F32))
                dcol = c5a.enter_context(nc.sbuf_tensor("dcol", [P, NT], F32))
                w1c = c5a.enter_context(nc.sbuf_tensor("w1c", [P, NT], F32))
                w1s = c5a.enter_context(nc.sbuf_tensor("w1s", [P, NT], F32))
                w2s = c5a.enter_context(nc.sbuf_tensor("w2s", [P, NT], F32))
                wdc = c5a.enter_context(nc.sbuf_tensor("wdc", [P, NT], F32))
                eq1 = c5a.enter_context(nc.sbuf_tensor("eq1", [P, 8], F32))
                eq2 = c5a.enter_context(nc.sbuf_tensor("eq2", [P, 8], F32))
                t2s = c5a.enter_context(nc.sbuf_tensor("t2s", [P, 8], F32))
                comb = c5a.enter_context(nc.sbuf_tensor("comb", [P, NT * 8], F32))
                ctT_sb = c5a.enter_context(nc.sbuf_tensor("ctT_sb", [8, NTOK], F32))
                hT_sb = c5a.enter_context(nc.sbuf_tensor("hT_sb", [P, NTOK], F32))
                ps8 = c5a.enter_context(nc.psum_tensor("ps8", [8, NTOK], F32))
                ps_big = c5a.enter_context(nc.psum_tensor("ps_big", [P, NTOK], F32))

                with nc.Block() as blk:
                    ev_x2l = [None] * HT

                    @blk.sync
                    def _(sync):
                        sync.wait_ge(s_spl2.h, s_spl2.v)
                        for ht in range(HT):
                            ev_x2l[ht] = s_hld.inc(sync.dma_start(
                                out=xlnT2[:, ht * NTOK:(ht + 1) * NTOK],
                                in_=x2lnT_sp[ht, :, :]), 16)
                        ev5["bs"] = s_hld.inc(sync.dma_start(out=bs_sb[:, :], in_=bsbi[:, :]), 16)
                        ev5["rep"] = s_hld.inc(sync.dma_start(out=rep_sb[:, :], in_=repmi[:, :]), 16)

                    @blk.gpsimd
                    def _(g):
                        for ev in p4_finals:
                            g.wait_ge(*ev)
                        for q in range(4):
                            src = wrT.rearrange("(ht p) e -> p ht e", p=P)[:, 8 * q:8 * (q + 1), :]
                            dst = wr_sb[:, :].rearrange("p (ht e) -> p ht e", e=8)[
                                :, 8 * q:8 * (q + 1), :]
                            ev5["wr"] = s_wld.inc(g.dma_start(out=dst, in_=src), 16)
                        for q in range(4):
                            src = atT.rearrange("(ht p) e -> p ht e", p=P)[:, 8 * q:8 * (q + 1), :]
                            dst = at_sb[:, :].rearrange("p (ht e) -> p ht e", e=P)[
                                :, 8 * q:8 * (q + 1), :]
                            ev5["at"] = s_wld.inc(g.dma_start(out=dst, in_=src), 16)
                        ev5["bt"] = s_wld.inc(g.dma_start(out=bt_sb[:, :], in_=btT[:, :]), 16)

                    @blk.tensor
                    def _(t):
                        wt(t, ev5["wr"])
                        for ht in range(HT):
                            wt(t, ev_x2l[ht])
                            for c in range(4):
                                instr = t.matmul(ps8[:, c * 512:(c + 1) * 512],
                                                 wr_sb[:, ht * 8:(ht + 1) * 8],
                                                 xlnT2[:, ht * NTOK + c * 512: ht * NTOK + (c + 1) * 512],
                                                 start=(ht == 0), stop=(ht == HT - 1))
                        ev5["lgT_mm"] = s_mm.inc(instr)

                    @blk.vector
                    def _(v):
                        wt(v, ev5["lgT_mm"])
                        ev5["lgT_ev"] = s_ve.inc(v.tensor_copy(lgT_sb[:, :], ps8[:, :]))

                    @blk.tensor
                    def _(t):
                        wt(t, ev5["lgT_ev"])
                        for nt in range(NT):
                            instr = t.transpose(ps_big[:, nt * 8:(nt + 1) * 8],
                                                lgT_sb[:, nt * P:(nt + 1) * P],
                                                ident_f[0:8, 0:8])
                        ev5["lg_tr"] = s_tr.inc(instr)

                    @blk.vector
                    def _(v):
                        wt(v, ev5["lg_tr"])
                        ev5["lg_ev"] = s_ve.inc(v.tensor_copy(lg_sb[:, :], ps_big[:, 0:NT * 8]))

                    @blk.tensor
                    def _(t):
                        wt(t, ev5["lg_ev"])
                        wt(t, ev5["at"])
                        for ht in range(HT):
                            for c in range(4):
                                instr = t.matmul(ps_big[:, c * 512:(c + 1) * 512],
                                                 at_sb[:, ht * P:(ht + 1) * P],
                                                 xlnT2[:, ht * NTOK + c * 512: ht * NTOK + (c + 1) * 512],
                                                 start=(ht == 0), stop=(ht == HT - 1))
                        ev5["hT_mm"] = s_mm.inc(instr)

                    @blk.vector
                    def _(v):
                        wt(v, ev5["hT_mm"])
                        ev5["hT_ev"] = s_ve.inc(v.tensor_copy(hT_sb[:, :], ps_big[:, :]))
                        wt(v, ev5["lg_ev"])
                        for nt in range(NT):
                            chain(v, s_ve2, v.max(out=m8_sb[:, nt * 8:(nt + 1) * 8],
                                                  in_=lg_sb[:, nt * 8:(nt + 1) * 8]))
                        l1v = m8_sb[:, :].rearrange("p (nt e) -> p nt e", e=8)[:, :, 0:1]
                        l2v = m8_sb[:, :].rearrange("p (nt e) -> p nt e", e=8)[:, :, 1:2]
                        ev5["dcol"] = chain(v, s_ve2, v.tensor_tensor(
                            out=dcol[:, :].rearrange("p (nt o) -> p nt o", o=1),
                            in0=l1v, in1=l2v, op=ALU.subtract))

                    @blk.scalar
                    def _(sc):
                        wt(sc, ev5["dcol"])
                        ev5["w1"] = s_sc.inc(sc.activation(out=w1c[:, :], in_=dcol[:, :],
                                                           func=AF.Sigmoid))

                    @blk.vector
                    def _(v):
                        wt(v, ev5["w1"])
                        chain(v, s_ve2, v.tensor_scalar(out=w1s[:, :], in0=w1c[:, :],
                                                        scalar1=2.0, scalar2=None, op0=ALU.mult))
                        chain(v, s_ve2, v.tensor_scalar(out=w2s[:, :], in0=w1c[:, :],
                                                        scalar1=-2.0, scalar2=2.0,
                                                        op0=ALU.mult, op1=ALU.add))
                        chain(v, s_ve2, v.tensor_tensor(out=wdc[:, :], in0=w1s[:, :],
                                                        in1=w2s[:, :], op=ALU.subtract))
                        for nt in range(NT):
                            lg_sl = lg_sb[:, nt * 8:(nt + 1) * 8]
                            chain(v, s_ve2, v.tensor_scalar(out=eq1[:, :], in0=lg_sl,
                                                            scalar1=m8_sb[:, nt * 8:nt * 8 + 1],
                                                            scalar2=None, op0=ALU.is_equal))
                            chain(v, s_ve2, v.tensor_scalar(out=eq2[:, :], in0=lg_sl,
                                                            scalar1=m8_sb[:, nt * 8 + 1:nt * 8 + 2],
                                                            scalar2=None, op0=ALU.is_equal))
                            chain(v, s_ve2, v.tensor_scalar(out=t2s[:, :], in0=eq2[:, :],
                                                            scalar1=w2s[:, nt:nt + 1],
                                                            scalar2=None, op0=ALU.mult))
                            chain(v, s_ve2, v.scalar_tensor_tensor(
                                out=comb[:, nt * 8:(nt + 1) * 8], in0=eq1[:, :],
                                scalar=w1s[:, nt:nt + 1], in1=t2s[:, :],
                                op0=ALU.mult, op1=ALU.add))
                        ev5["comb"] = chain(v, s_ve2, v.tensor_copy(comb[:, 0:1], comb[:, 0:1]))

                    @blk.tensor
                    def _(t):
                        wt(t, ev5["comb"])
                        for nt in range(NT):
                            instr = t.transpose(ps8[:, nt * P:(nt + 1) * P],
                                                comb[:, nt * 8:(nt + 1) * 8], ident_f[:, :])
                        ev5["ct_tr"] = s_tr.inc(instr)

                    @blk.vector
                    def _(v):
                        wt(v, ev5["ct_tr"])
                        ev5["ct_ev"] = s_ve.inc(v.tensor_copy(ctT_sb[:, :], ps8[:, :]))

                    @blk.tensor
                    def _(t):
                        wt(t, ev5["ct_ev"])
                        wt(t, ev5["hT_ev"])
                        wt(t, ev5["rep"])
                        for c in range(4):
                            instr = t.matmul(ps_big[:, c * 512:(c + 1) * 512], rep_sb[:, :],
                                             ctT_sb[:, c * 512:(c + 1) * 512],
                                             start=True, stop=True)
                        ev5["crep"] = s_tr.inc(instr)

                    @blk.vector
                    def _(v):
                        wt(v, ev5["crep"])
                        ev5["hw"] = chain(v, s_ve2, v.tensor_tensor(
                            out=hw_sb[:, :], in0=hT_sb[:, :], in1=ps_big[:, :], op=ALU.mult))

            with ExitStack() as c5b:
                wbr = c5b.enter_context(nc.sbuf_tensor("wbr", [P, 12 * 8 * P], BF16))
                y2b = [c5b.enter_context(nc.sbuf_tensor(f"y2b{j}", [P, 1024], F32)) for j in range(3)]
                o_st = [c5b.enter_context(nc.sbuf_tensor(f"o_st{j}", [P, 1024], F32)) for j in range(4)]
                ps_y2 = [c5b.enter_context(nc.psum_tensor(f"ps_y2{j}", [P, 1024], F32)) for j in range(3)]
                ps_trO = c5b.enter_context(nc.psum_tensor("ps_trO", [P, 1024], F32))

                with nc.Block() as blk:
                    ev_ring5 = {}
                    ev_mm5 = {}
                    ev_y2e = {}
                    ev_tro = {}
                    ev_oev = {}
                    ev_oacc = {}
                    ev_out_all = (s_acc.h, s_acc.v)

                    def emit_tro(k):
                        @blk.tensor
                        def _(t, k=k):
                            wt(t, ev_y2e[k])
                            if k >= 1:
                                wt(t, ev_oev[k - 1])
                            for c in range(8):
                                instr = t.transpose(ps_trO[:, c * P:(c + 1) * P],
                                                    y2b[k % 3][:, c * P:(c + 1) * P],
                                                    ident_f[:, :])
                            ev_tro[k] = s_tr.inc(instr)

                        @blk.vector
                        def _(v, k=k):
                            wt(v, ev_tro[k])
                            if k >= 4:
                                wt(v, ev_oacc[k - 4])
                            ev_oev[k] = s_ve.inc(v.tensor_copy(o_st[k % 4][:, :], ps_trO[:, :]))

                        @blk.gpsimd
                        def _(g, k=k):
                            ho, nh = divmod(k, 2)
                            wt(g, ev_oev[k])
                            src = o_st[k % 4][:, :].rearrange("p (nt n) -> p nt n", n=P)
                            dst = out.rearrange("(nt p) (ho h) -> p nt ho h", p=P, h=P)[
                                :, nh * 8:(nh + 1) * 8, ho, :]
                            ev_oacc[k] = s_acc.inc(g.dma_start(out=dst, in_=src,
                                                               accum_op=ALU.add), 16)

                    for ho in range(HT):
                        @blk.gpsimd
                        def _(g, ho=ho):
                            if ho == 0:
                                g.wait_ge(*ev_out_all)
                            for q in range(4):
                                gi = ho * 4 + q
                                if gi >= 12:
                                    pho, pq = divmod(gi - 12, 4)
                                    wt(g, ev_mm5[(pho, 1)])
                                src = wbs[ho, q, :, :]
                                dst = wbr[:, :].rearrange("p (r i h) -> p r i h", r=12, h=P)[
                                    :, gi % 12, :, :]
                                ev_ring5[gi] = s_wld.inc(g.dma_start(out=dst, in_=src), 16)

                        for nh in range(2):
                            @blk.tensor
                            def _(t, ho=ho, nh=nh):
                                k = ho * 2 + nh
                                if k >= 3:
                                    wt(t, ev_y2e[k - 3])
                                if nh == 0:
                                    wt(t, ev_ring5[ho * 4 + 3])
                                if k == 0:
                                    wt(t, ev5["hw"])
                                    wt(t, ev5["bt"])
                                for hi in range(HT):
                                    for c in range(2):
                                        instr = t.matmul(
                                            ps_y2[k % 3][:, c * 512:(c + 1) * 512],
                                            wbr[:, :].rearrange("p (r i h) -> p r i h", r=12, h=P)[
                                                :, (ho * 4 + hi // 8) % 12, hi % 8, :],
                                            xlnT2[:, hi * NTOK + nh * 1024 + c * 512:
                                                  hi * NTOK + nh * 1024 + (c + 1) * 512],
                                            start=(hi == 0), stop=False)
                                for c in range(2):
                                    instr = t.matmul(
                                        ps_y2[k % 3][:, c * 512:(c + 1) * 512],
                                        bt_sb[:, ho * P:(ho + 1) * P],
                                        hw_sb[:, nh * 1024 + c * 512: nh * 1024 + (c + 1) * 512],
                                        start=False, stop=True)
                                ev_mm5[(ho, nh)] = s_mm.inc(instr)

                            @blk.scalar
                            def _(sc, ho=ho, nh=nh):
                                k = ho * 2 + nh
                                wt(sc, ev_mm5[(ho, nh)])
                                if k >= 3:
                                    wt(sc, ev_tro[k - 3])
                                ev_y2e[k] = s_sc.inc(sc.activation(
                                    out=y2b[k % 3][:, :], in_=ps_y2[k % 3][:, :], func=AF.Identity,
                                    bias=bs_sb[:, ho:ho + 1]))

                            if ho * 2 + nh >= 1:
                                emit_tro(ho * 2 + nh - 1)

                    emit_tro(HT * 2 - 1)

        with nc.Block() as blk:
            @blk.gpsimd
            def _(g):
                g.wait_ge(s_acc.h, s_acc.v)

            @blk.sync
            def _(sync):
                sync.wait_ge(s_spl.h, s_spl.v)
                sync.wait_ge(s_spl2.h, s_spl2.v)

    return nc


def _finish_partial(nc, S, out):
    with nc.Block() as blk:
        @blk.gpsimd
        def _(g):
            g.wait_ge(S["wld"].h, S["wld"].v)
            g.wait_ge(S["pc"].h, S["pc"].v)
            g.wait_ge(S["acc"].h, S["acc"].v)

        @blk.sync
        def _(sync):
            sync.wait_ge(S["hld"].h, S["hld"].v)
            sync.wait_ge(S["spl"].h, S["spl"].v)
            sync.wait_ge(S["spl2"].h, S["spl2"].v)


def _prep(inputs):
    f32 = np.float32
    x = np.ascontiguousarray(np.asarray(inputs["L0_attn_x"], dtype=f32).reshape(NTOK * 8, D))
    wg = np.asarray(inputs["L0_attn_w_gate"], dtype=f32)
    wu = np.asarray(inputs["L0_attn_w_up"], dtype=f32)
    wd = np.asarray(inputs["L0_attn_w_down"], dtype=f32)
    wb = np.asarray(inputs["L0_ffn_W_base"], dtype=f32)
    wgs_ = np.ascontiguousarray(
        wg.reshape(IT, P, HT, P).transpose(0, 3, 2, 1).reshape(IT, P, HT * P))
    wus_ = np.ascontiguousarray(
        wu.reshape(IT, P, HT, P).transpose(0, 3, 2, 1).reshape(IT, P, HT * P))
    wds_ = np.zeros((HT, NGR, P, 8 * P), dtype=f32)
    wdr4 = wd.reshape(HT, P, IT, P).transpose(0, 2, 3, 1)
    for g in range(NGR):
        n_it = min(8, IT - 8 * g)
        wds_[:, g, :, 0:n_it * P] = wdr4[:, 8 * g:8 * g + n_it].transpose(
            0, 2, 1, 3).reshape(HT, P, n_it * P)
    wbs_ = np.ascontiguousarray(
        wb.reshape(HT, P, 4, 8, P).transpose(0, 2, 4, 3, 1).reshape(HT, 4, P, 8 * P))
    wrT_ = np.ascontiguousarray(np.asarray(inputs["L0_ffn_W_router"], dtype=f32).T)
    A = np.asarray(inputs["L0_ffn_A"], dtype=f32)
    B = np.asarray(inputs["L0_ffn_B"], dtype=f32)
    atT_ = np.ascontiguousarray(np.transpose(A, (2, 0, 1)).reshape(D, P))
    btT_ = np.ascontiguousarray(np.transpose(B, (0, 2, 1)).reshape(P, D))
    b_base = np.asarray(inputs["L0_ffn_b_base"], dtype=f32)
    bsb_ = np.ascontiguousarray(b_base.reshape(HT, P).T)
    idbf_ = np.eye(P, dtype=ml_dtypes.bfloat16)
    idf_ = np.eye(P, dtype=f32)
    repm_ = np.zeros((8, P), dtype=f32)
    for e in range(8):
        repm_[e, e * 16:(e + 1) * 16] = 1.0
    shared = {"wgs": wgs_, "wus": wus_, "wds": wds_, "wbs": wbs_, "wrT": wrT_,
              "atT": atT_, "btT": btT_, "bsbi": bsb_, "idbf": idbf_, "idf": idf_,
              "repmi": repm_}
    in_maps = []
    for c in range(8):
        m = dict(shared)
        m["x"] = x[c * NTOK:(c + 1) * NTOK]
        in_maps.append(m)
    return in_maps


_LAST_RESULTS = None


def kernel(**inputs):
    global _LAST_RESULTS
    in_maps = _prep(inputs)
    nc = build_program(phases=_PHASES, dbg=_DBG)
    res = run_bass_kernel_spmd(nc, in_maps, core_ids=list(range(8)))
    _LAST_RESULTS = res
    outs = [r["out"] for r in res.results]
    return np.concatenate(outs, axis=0).reshape(4, 4096, D)
